# version 03
# baseline (speedup 1.0000x reference)
import math
import numpy as np

import concourse.bass as bass
import concourse.mybir as mybir
from concourse.bass_utils import run_bass_kernel_spmd

F32 = mybir.dt.float32
BF16 = mybir.dt.bfloat16
AF = mybir.ActivationFunctionType
ALU = mybir.AluOpType
AX = mybir.AxisListType

T = 2048
D = 1024
NCH = 8
HEADS = 16
HD = 64
DI = 2730
DEPTH = 6
LAT = 512
MAGIC = 1.5 * 2.0 ** 23
PI = math.pi
NEG = -1.0e30
PI_LO = 3.1415925
QK_SCALE = 8.0
MLA_SCALE = 0.125

STAGE = 4
TAP_SELECT = None
N_INNER = DEPTH

ENGS = ("tensor", "vector", "scalar", "gpsimd", "sync")


def cdiv(a, b):
    return (a + b - 1) // b


class P:

    def __init__(self, nc):
        self.nc = nc
        self.items = {e: [] for e in ENGS}
        self.semh = {}
        self.cnt = {}
        self.hw = {e: {} for e in ENGS}

    def sem(self, name):
        if name not in self.semh:
            self.semh[name] = self.nc._stack.enter_context(self.nc.semaphore(name))
            self.cnt[name] = 0
        return self.semh[name]

    def wait(self, eng, ev):
        if ev is None:
            return
        if isinstance(ev, list):
            for e2 in ev:
                self.wait(eng, e2)
            return
        name, val = ev
        if val <= 0:
            return
        assert val <= self.cnt.get(name, 0), \
            f"wait on future {name}={val}>{self.cnt.get(name, 0)}"
        if val <= self.hw[eng].get(name, 0):
            return
        self.hw[eng][name] = val
        s = self.sem(name)
        self.items[eng].append(lambda e, s=s, v=val: e.wait_ge(s, v))

    def op(self, eng, fn, inc=None, inc_by=1, waits=()):
        for ev in waits:
            self.wait(eng, ev)
        ev = None
        if inc is not None:
            s = self.sem(inc)
            self.cnt[inc] += inc_by
            ev = (inc, self.cnt[inc])

            def wrapped(e, fn=fn, s=s, by=inc_by):
                fn(e).then_inc(s, by)
            self.items[eng].append(wrapped)
        else:
            self.items[eng].append(fn)
        return ev

    def ev(self, name):
        return (name, self.cnt.get(name, 0))

    def evs_prefix(self, prefix):
        return [(n, c) for n, c in self.cnt.items() if n.startswith(prefix) and c]

    def replay(self, block):
        for eng in ENGS:
            seq = self.items[eng]

            def runner(e, seq=seq):
                for f in seq:
                    f(e)
            getattr(block, eng)(runner)


ECN = {"tensor": "sT", "vector": "sV", "scalar": "sA", "gpsimd": "sP"}


def build_program(stage=STAGE, n_inner=N_INNER, taps=None):
    import contextlib
    nc = bass.Bass("TRN2", target_bir_lowering=False)
    nc._stack = contextlib.ExitStack()
    st = nc._stack

    dx = nc.dram_tensor("x", [T, D], F32, kind="ExternalInput").ap()
    dy = nc.dram_tensor("y", [T, D], F32, kind="ExternalOutput").ap()
    dWc_down = nc.dram_tensor("Wc_down", [D, LAT], F32, kind="ExternalInput").ap()
    dWq_up = nc.dram_tensor("Wq_up", [LAT, D], F32, kind="ExternalInput").ap()
    dWkv_up = nc.dram_tensor("Wkv_up", [LAT, 128], F32, kind="ExternalInput").ap()
    dWc_proj = nc.dram_tensor("Wc_proj", [D, D], F32, kind="ExternalInput").ap()
    dWqkv = nc.dram_tensor("Wqkv", [DEPTH, D, 3 * D], F32, kind="ExternalInput").ap()
    dqs = nc.dram_tensor("q_scale", [DEPTH, HD], F32, kind="ExternalInput").ap()
    dks = nc.dram_tensor("k_scale", [DEPTH, HD], F32, kind="ExternalInput").ap()
    dWo = nc.dram_tensor("Wo", [DEPTH, D, D], F32, kind="ExternalInput").ap()
    dlng = nc.dram_tensor("ln_g", [DEPTH, D], F32, kind="ExternalInput").ap()
    dlnb = nc.dram_tensor("ln_b", [DEPTH, D], F32, kind="ExternalInput").ap()
    dW1 = nc.dram_tensor("W1", [DEPTH, D, 2 * DI], F32, kind="ExternalInput").ap()
    db1 = nc.dram_tensor("b1", [DEPTH, 2 * DI], F32, kind="ExternalInput").ap()
    dW2 = nc.dram_tensor("W2", [DEPTH, DI, D], F32, kind="ExternalInput").ap()
    db2 = nc.dram_tensor("b2", [DEPTH, D], F32, kind="ExternalInput").ap()
    dx1 = nc.dram_tensor("x1s", [128, NCH * T], F32, kind="Internal").ap()

    xT = st.enter_context(nc.sbuf_tensor("xT", [128, NCH * T], F32))
    XS = st.enter_context(nc.sbuf_tensor("XS", [128, NCH * T], BF16))
    AO = st.enter_context(nc.sbuf_tensor("AO", [128, NCH * T], BF16))
    TAB = st.enter_context(nc.sbuf_tensor("TAB", [128, 4 * T], BF16))
    CH = st.enter_context(nc.sbuf_tensor("CH", [128, 3 * 1024], F32))
    RN2 = st.enter_context(nc.sbuf_tensor("RN2", [2, T], F32))
    AR = st.enter_context(nc.sbuf_tensor("AR", [128, 10112], BF16))
    WSL = st.enter_context(nc.sbuf_tensor("WSL", [128, 6144], BF16))
    WIL = st.enter_context(nc.sbuf_tensor("WIL", [128, 2048], BF16))
    IDN = st.enter_context(nc.sbuf_tensor("IDN", [128, 128], F32))
    TRIL = st.enter_context(nc.sbuf_tensor("TRIL", [128, 128], F32))
    TRIU = st.enter_context(nc.sbuf_tensor("TRIU", [128, 128], F32))
    ONES = st.enter_context(nc.sbuf_tensor("ONES", [128, 128], BF16))
    SELQK = st.enter_context(nc.sbuf_tensor("SELQK", [2, 128], F32))
    HSEL = st.enter_context(nc.sbuf_tensor("HSEL", [128, 2], BF16))
    BIAS = st.enter_context(nc.sbuf_tensor("BIAS", [128, 512], F32))
    MISC = st.enter_context(nc.sbuf_tensor("MISC", [128, 32], F32))
    BROW = st.enter_context(nc.sbuf_tensor("BROW", [44, 128], F32))

    PS = st.enter_context(nc.psum_tensor("PS", [128, 4096], F32))

    p = P(nc)

    def bias_col(L, name, idx=0):
        off = {"qs": 0, "ks": 1, "lng": 2, "lnb": 10, "b2": 18, "b1a": 26,
               "b1g": 48}[name]
        c = L * 80 + off + idx
        return BIAS[:, c:c + 1]

    NEGM = MISC[:, 0:1]
    NEGM2 = MISC[:, 1:2]
    LCOL = MISC[:, 2:3]
    LINV = MISC[:, 3:4]
    IOTAC = MISC[:, 4:5]
    INVF = MISC[:, 5:6]
    LNBS = MISC[:, 6:7]
    SGN = MISC[:, 7:8]
    C0 = MISC[:, 8:9]
    C1 = MISC[:, 9:10]
    EPS5 = MISC[:, 10:11]
    EPS30 = MISC[:, 11:12]

    KW, KX = 4, 6
    dma_ring = {"W": [0, [0] * KW], "X": [0, [0] * KX]}

    def _dma(queue_key, eng, fn, waits):
        idx, counts = dma_ring[queue_key]
        K = len(counts)
        s = idx % K
        name = f"s{queue_key}D{s}"
        p.wait(eng, (name, 16 * counts[s]) if counts[s] else None)
        for ev in waits:
            p.wait(eng, ev)
        counts[s] += 1
        dma_ring[queue_key][0] = idx + 1
        return p.op(eng, fn, inc=name, inc_by=16)

    def dma_w(dst, src, waits=()):
        return _dma("W", "gpsimd", lambda e, d=dst, s=src: e.dma_start(d, s), waits)

    def dma_s(dst, src, waits=(), transpose=False, slow=False):
        if transpose:
            return _dma("X", "sync",
                        lambda e, d=dst, s=src: e.dma_start_transpose(d, s), waits)
        if slow:
            return _dma("X", "sync", lambda e, d=dst, s=src:
                        e.dma_start(d, s, allow_slow_non_contiguous=True), waits)
        return _dma("X", "sync", lambda e, d=dst, s=src: e.dma_start(d, s), waits)

    def eop(eng, fn, waits=(), inc=True):
        return p.op(eng, fn, inc=ECN[eng] if inc else None, waits=waits)

    def mm(out, lhsT, rhs, start, stop, waits=(), inc=False):
        return eop("tensor",
                   lambda e, o=out, l=lhsT, r=rhs, s0=start, s1=stop:
                   e.matmul(o, l, r, start=s0, stop=s1),
                   waits=waits, inc=inc)

    def vfence():
        return eop("vector", lambda e: e.tensor_copy(MISC[:, 13:14],
                                                     MISC[:, 12:13]))

    def pfence():
        return eop("gpsimd", lambda e: e.memset(MISC[:, 14:15], 0.0))

    def tsc(eng, out, in0, s1, s2, op0, op1=None, waits=(), inc=True):
        if s2 is None:
            return eop(eng, lambda e, o=out, i=in0, a=s1, oo=op0:
                       e.tensor_scalar(o, i, a, None, oo), waits=waits, inc=inc)
        return eop(eng, lambda e, o=out, i=in0, a=s1, b=s2, oo=op0, o1=op1:
                   e.tensor_scalar(o, i, a, b, oo, o1), waits=waits, inc=inc)

    def slab_src(dram, row0, c0, w, nk, kstep=128, lay_off=0):
        Cst = dram.shape[-1]
        off = lay_off + row0 * Cst + c0
        return bass.AP(dram.tensor, dram.offset + off,
                       [[Cst, 128], [kstep * Cst, nk], [1, w]])

    g, v, a = "gpsimd", "vector", "scalar"

    ev = eop(g, lambda e: e.memset(IDN[:, :], 0.0))
    eop(g, lambda e: e.affine_select(out=IDN[:, :], in_=IDN[:, :],
                                     compare_op=ALU.not_equal, fill=1.0, base=0,
                                     pattern=[[-1, 128]], channel_multiplier=1),
        waits=[ev])
    ev = eop(g, lambda e: e.memset(TRIL[:, :], 0.0))
    eop(g, lambda e: e.affine_select(out=TRIL[:, :], in_=TRIL[:, :],
                                     compare_op=ALU.is_ge, fill=NEG, base=0,
                                     pattern=[[-1, 128]], channel_multiplier=1),
        waits=[ev])
    ev = eop(g, lambda e: e.memset(TRIU[:, :], 0.0))
    eop(g, lambda e: e.affine_select(out=TRIU[:, :], in_=TRIU[:, :],
                                     compare_op=ALU.is_ge, fill=NEG, base=0,
                                     pattern=[[1, 128]], channel_multiplier=-1),
        waits=[ev])
    eop(g, lambda e: e.memset(ONES[:, :], 1.0), inc=False)
    eop(g, lambda e: e.memset(EPS5, 1e-5), inc=False)
    eop(g, lambda e: e.memset(EPS30, 1e-30), inc=False)
    eop(g, lambda e: e.memset(MISC[:, 12:13], 0.0), inc=False)
    ev = eop(g, lambda e: e.memset(SELQK[:, :], 1.0))
    ev = eop(g, lambda e: e.affine_select(out=SELQK[:, :], in_=SELQK[:, :],
                                          compare_op=ALU.is_ge, fill=0.0, base=63,
                                          pattern=[[-1, 128]],
                                          channel_multiplier=64), waits=[ev])
    eop(g, lambda e: e.affine_select(out=SELQK[:, :], in_=SELQK[:, :],
                                     compare_op=ALU.is_ge, fill=0.0, base=0,
                                     pattern=[[1, 128]], channel_multiplier=-64),
        waits=[ev])
    ev = eop(g, lambda e: e.memset(HSEL[:, :], 1.0))
    ev = eop(g, lambda e: e.affine_select(out=HSEL[:, :], in_=HSEL[:, :],
                                          compare_op=ALU.is_ge, fill=0.0, base=63,
                                          pattern=[[64, 2]],
                                          channel_multiplier=-1), waits=[ev])
    eop(g, lambda e: e.affine_select(out=HSEL[:, :], in_=HSEL[:, :],
                                     compare_op=ALU.is_ge, fill=0.0, base=64,
                                     pattern=[[-128, 2]], channel_multiplier=1),
        waits=[ev])
    eop(g, lambda e: e.iota(CH[:, 0:T], [[1, T]], channel_multiplier=0,
                            allow_small_or_imprecise_dtypes=True), inc=False)
    ev_g0 = eop(g, lambda e: e.iota(IOTAC, [[0, 1]], channel_multiplier=1,
                                    allow_small_or_imprecise_dtypes=True))

    ev = tsc(v, C0, IOTAC, 1.0 / 64.0, -0.499, ALU.mult, ALU.add, waits=[ev_g0])
    ev = tsc(v, C0, C0, MAGIC, MAGIC, ALU.add, ALU.subtract, waits=[ev])
    ev = tsc(v, C1, C0, -64.0, None, ALU.mult, waits=[ev])
    ev = eop(v, lambda e: e.tensor_add(C1, C1, IOTAC), waits=[ev])
    ev = tsc(v, C0, C1, 0.5, -0.25, ALU.mult, ALU.add, waits=[ev])
    ev = tsc(v, C0, C0, MAGIC, MAGIC, ALU.add, ALU.subtract, waits=[ev])
    ev = tsc(v, INVF, C0, -math.log(10000.0) / 32.0, None, ALU.mult, waits=[ev])
    ev_invf = eop(a, lambda e: e.activation(INVF, INVF, AF.Exp), waits=[ev])
    ev = tsc(v, C1, C0, 2.0 / 89.6, 25.6 / 89.6, ALU.mult, ALU.add, waits=[ev])
    ev_lnbs = eop(a, lambda e: e.activation(LNBS, C1, AF.Ln), waits=[ev])
    ev = tsc(v, C0, IOTAC, 0.5, -0.25, ALU.mult, ALU.add, waits=[ev])
    ev = tsc(v, C0, C0, MAGIC, MAGIC, ALU.add, ALU.subtract, waits=[ev])
    ev = tsc(v, C0, C0, -2.0, None, ALU.mult, waits=[ev])
    ev = eop(v, lambda e: e.tensor_add(C0, C0, IOTAC), waits=[ev])
    ev_sgn = tsc(v, SGN, C0, 2.0, -1.0, ALU.mult, ALU.add, waits=[ev])

    ev_trow = ev_g0
    p.wait("tensor", p.ev("sP"))

    CH1 = CH[:, 2048:3072]
    for h in range(2):
        A_ = PS[:, 2048:3072]
        B_ = PS[:, 3072:4096]
        A2_ = PS[:, 0:1024]
        trow_h = CH[:, h * 1024:(h + 1) * 1024]
        ev = tsc(v, A_, trow_h, INVF, None, ALU.mult, waits=[ev_trow, ev_invf])
        ev = tsc(v, B_, A_, 1.0 / (2 * PI), 0.25, ALU.mult, ALU.add, waits=[ev])
        ev = tsc(v, CH1, B_, MAGIC, MAGIC, ALU.add, ALU.subtract, waits=[ev])
        ev = eop(v, lambda e, A=A_, B=B_:
                 e.scalar_tensor_tensor(B, CH1, -2 * PI, A, ALU.mult, ALU.add),
                 waits=[ev])
        ev = tsc(v, B_, B_, PI / 2, None, ALU.add, waits=[ev])
        ev = tsc(v, B_, B_, -PI_LO, PI_LO, ALU.max, ALU.min, waits=[ev])
        ev_c = eop(a, lambda e, B=B_: e.activation(B, B, AF.Sin), waits=[ev])
        ev_cc = eop(v, lambda e, B=B_: e.tensor_copy(CH1, B), waits=[ev_c])
        ev = tsc(v, B_, trow_h, 1.0 / 128.0, -8.0, ALU.mult, ALU.add,
                 waits=[ev_cc, ev_c])
        ev = tsc(v, B_, B_, LNBS, None, ALU.mult, waits=[ev, ev_lnbs])
        ev_s = eop(a, lambda e, B=B_: e.activation(B, B, AF.Exp), waits=[ev])
        ev1 = eop(v, lambda e, B=B_, o=TAB[:, 0 * T + h * 1024: 0 * T + (h + 1) * 1024]:
                  e.tensor_mul(o, CH1, B), waits=[ev_s, ev_cc])
        ev = tsc(v, A2_, trow_h, 1.0 / 128.0, -8.0, ALU.mult, ALU.add, waits=[ev1])
        ev = tsc(v, A2_, A2_, LNBS, None, ALU.mult, waits=[ev])
        ev_iv = eop(a, lambda e, A2=A2_: e.activation(A2, A2, AF.Exp, scale=-1.0),
                    waits=[ev])
        ev2 = eop(v, lambda e, A2=A2_, o=TAB[:, 2 * T + h * 1024: 2 * T + (h + 1) * 1024]:
                  e.tensor_mul(o, CH1, A2), waits=[ev_iv])
        ev = tsc(v, CH1, A_, 1.0 / (2 * PI), MAGIC, ALU.mult, ALU.add, waits=[ev2])
        ev = tsc(v, CH1, CH1, MAGIC, None, ALU.subtract, waits=[ev])
        ev = eop(v, lambda e, A=A_: e.scalar_tensor_tensor(A, CH1, -2 * PI, A,
                                                           ALU.mult, ALU.add),
                 waits=[ev])
        ev = tsc(v, A_, A_, -PI_LO, PI_LO, ALU.max, ALU.min, waits=[ev])
        ev_sn = eop(a, lambda e, A=A_: e.activation(A, A, AF.Sin), waits=[ev])
        ev_sc = tsc(v, CH1, A_, SGN, None, ALU.mult, waits=[ev_sn, ev_sgn])
        ev3 = eop(v, lambda e, B=B_, o=TAB[:, 1 * T + h * 1024: 1 * T + (h + 1) * 1024]:
                  e.tensor_mul(o, CH1, B), waits=[ev_sc])
        ev4 = eop(v, lambda e, A2=A2_, o=TAB[:, 3 * T + h * 1024: 3 * T + (h + 1) * 1024]:
                  e.tensor_mul(o, CH1, A2), waits=[ev3])
        ev_trow = ev4
    ev_tables = p.ev("sV")

    ev_bias_ms = eop(g, lambda e: e.memset(BIAS[:, :], 0.0))
    ev_q = None
    for L in range(DEPTH):
        for nm, dr in (("qs", dqs), ("ks", dks)):
            col = L * 80 + {"qs": 0, "ks": 1}[nm]
            for hq in range(2):
                for tq in range(2):
                    dst = bass.AP(BIAS, col + (64 * hq + tq) * 512, [[1024, 32]])
                    srcp = bass.AP(dr.tensor, dr.offset + L * HD + 32 * tq,
                                   [[1, 32]])
                    ev_q = dma_s(dst, srcp,
                                 waits=[ev_q] if ev_q else [ev_bias_ms])
    ev_qs_gather = ev_q

    def vec_load(src_ap, n_rows, dst_ap, prev_ev, n_cols=128):
        wl = [prev_ev] if prev_ev else []
        ev_d = dma_s(BROW[0:n_rows, 0:n_cols], src_ap, waits=wl)
        ev_m = eop("tensor", lambda e, n=n_rows, m=n_cols:
                   e.transpose(PS[0:m, 1024:1024 + n], BROW[0:n, 0:m],
                               IDN[0:n, 0:n]),
                   waits=[ev_d], inc=True)
        return eop(v, lambda e, d=dst_ap, n=n_rows, m=n_cols:
                   e.tensor_copy(d, PS[0:m, 1024:1024 + n]), waits=[ev_m])

    ev_b = None
    for L in range(DEPTH):
        ev_b = vec_load(bass.AP(dlng.tensor, dlng.offset + L * D, [[128, 8], [1, 128]]),
                        8, BIAS[:, L * 80 + 2: L * 80 + 10], ev_b)
        ev_b = vec_load(bass.AP(dlnb.tensor, dlnb.offset + L * D, [[128, 8], [1, 128]]),
                        8, BIAS[:, L * 80 + 10: L * 80 + 18], ev_b)
        ev_b = vec_load(bass.AP(db2.tensor, db2.offset + L * D, [[128, 8], [1, 128]]),
                        8, BIAS[:, L * 80 + 18: L * 80 + 26], ev_b)
        ev_b = vec_load(bass.AP(db1.tensor, db1.offset + L * 2 * DI, [[128, 22], [1, 128]]),
                        22, BIAS[:, L * 80 + 26: L * 80 + 48], ev_b)
        ev_b = vec_load(bass.AP(db1.tensor, db1.offset + L * 2 * DI + DI,
                                [[128, 21], [1, 128]]),
                        21, BIAS[:, L * 80 + 48: L * 80 + 69], ev_b)
        ev_b = vec_load(bass.AP(db1.tensor, db1.offset + L * 2 * DI + DI + 21 * 128,
                                [[42, 1], [1, 42]]),
                        1, BIAS[0:42, L * 80 + 69: L * 80 + 70], ev_b, n_cols=42)
    ev_bias = ev_b

    ev_in_prev = [None, None]
    ev_mm = None
    for tt in range(16):
        pg = tt % 2
        stg = CH[:, pg * 1024:(pg + 1) * 1024]
        wl = []
        if ev_in_prev[pg] is not None:
            wl.append(ev_in_prev[pg])
        if tt < 2:
            wl += [ev_tables, ev_bias]
        ev_d = dma_s(stg, dx[tt * 128:(tt + 1) * 128, :], waits=wl)
        pb = PS[:, 2048 + pg * 1024: 2048 + (pg + 1) * 1024]
        for c in range(NCH):
            w = [ev_d] if c == 0 else []
            ev_mm = eop("tensor",
                        lambda e, c=c, stg=stg, pb=pb:
                        e.transpose(pb[:, c * 128:(c + 1) * 128],
                                    stg[:, c * 128:(c + 1) * 128], IDN[:, :]),
                        waits=w, inc=(c == NCH - 1))
        dst = bass.AP(xT, tt * 128, [[NCH * T, 128], [T, NCH], [1, 128]])
        ev_in_prev[pg] = eop(v, lambda e, d=dst, pb=pb: e.tensor_copy(d, pb),
                             waits=[ev_mm])
    ev_xin = p.ev("sV")


    def emit_xs(shift, waits=()):
        last = None
        wl = list(waits)
        for c in range(NCH):
            src = xT[:, c * T:(c + 1) * T]
            dst = XS[:, c * T:(c + 1) * T]
            if shift and c >= 4:
                last = eop(g, lambda e, d=dst, s=src:
                           e.tensor_copy(d[:, 1:T], s[:, 0:T - 1]), waits=wl)
                last = eop(g, lambda e, d=dst: e.memset(d[:, 0:1], 0.0))
            else:
                last = eop(g, lambda e, d=dst, s=src: e.tensor_copy(d, s), waits=wl)
            wl = []
        return last

    def emit_ln(L, fp32_io=False, save_evs=None):
        last = None
        p.wait("tensor", p.ev("sV"))
        p.wait("tensor", p.ev("sA"))
        p.wait("vector", p.ev("sV"))
        ev_sqmm = [None, None]
        ev_s1_read = None
        ev_s2_read = None
        for h in range(2):
            S1h = PS[:, 0:1024]
            S2h = PS[:, 1024:2048]
            MUB = PS[:, 2048:3072]
            WKB = PS[:, 3072:4096]
            ev_mm2 = None
            for sl in range(2):
                ssl = slice(sl * 512, (sl + 1) * 512)
                for c in range(NCH):
                    xs_c = XS[:, c * T + h * 1024: c * T + (h + 1) * 1024]
                    mm(S1h[:, ssl], ONES[:, :], xs_c[:, ssl], c == 0, c == NCH - 1,
                       inc=(c == NCH - 1),
                       waits=[ev_s1_read] if (c == 0 and sl == 0 and
                                              ev_s1_read) else ())
            for sl in range(2):
                ssl = slice(sl * 512, (sl + 1) * 512)
                for c in range(NCH):
                    x2 = AR[:, (c % 2) * 512:(c % 2) * 512 + 512]
                    xs_c = XS[:, c * T + h * 1024: c * T + (h + 1) * 1024]
                    wsq = []
                    if ev_sqmm[c % 2] is not None:
                        wsq.append(ev_sqmm[c % 2])
                    ev_sq = eop(a, lambda e, i=xs_c[:, ssl], o=x2:
                                e.activation(o, i, AF.Square), waits=wsq)
                    wm2 = [ev_sq]
                    if c == 0 and sl == 0 and ev_s2_read:
                        wm2.append(ev_s2_read)
                    ev_mm2 = mm(S2h[:, ssl], ONES[:, :], x2,
                                c == 0, c == NCH - 1, waits=wm2, inc=True)
                    ev_sqmm[c % 2] = ev_mm2
            ev_s1_read = tsc(v, MUB, S1h, 1.0 / D, None, ALU.mult, waits=[ev_mm2])
            p.wait("vector", ev_s1_read)
            ev_cp0 = eop(v, lambda e: e.tensor_copy(CH[:, 0:1024], MUB))
            if TAP_SELECT is not None and L is not None:
                eop(v, lambda e, h=h: e.tensor_copy(
                    AR[:, 6144 + h * 1024: 6144 + (h + 1) * 1024], CH[:, 0:1024]))
                if h == 0:
                    eop(v, lambda e: e.tensor_copy(AR[:, 8192:9216], S1h))
                    eop(v, lambda e: e.tensor_copy(AR[:, 9216:10112],
                                                   XS[:, 0:896]))
            ev_s2_read = tsc(v, CH[:, 1024:2048], S2h, 1.0 / D, None, ALU.mult)
            ev_mu2 = eop(v, lambda e: e.tensor_mul(MUB, CH[:, 0:1024],
                                                   CH[:, 0:1024]),
                         waits=[ev_cp0, ev_s2_read])
            p.wait("vector", ev_mu2)
            ev5 = eop(v, lambda e: e.scalar_tensor_tensor(
                CH[:, 1024:2048], MUB, -1.0, CH[:, 1024:2048], ALU.mult, ALU.add))
            ev6 = eop(a, lambda e: e.activation(WKB, CH[:, 1024:2048], AF.Ln,
                                                bias=EPS5), waits=[ev5])
            p.wait("scalar", ev6)
            ev7 = eop(a, lambda e: e.activation(WKB, WKB, AF.Exp, scale=-0.5))
            ev8 = eop(v, lambda e: e.tensor_copy(CH[:, 2048:3072], WKB),
                      waits=[ev7])
            ev9 = eop(v, lambda e: e.scalar_tensor_tensor(
                CH[:, 1024:2048], CH[:, 0:1024], -1.0, CH[:, 2048:3072],
                ALU.mult, ALU.mult), waits=[ev8])
            for c in range(NCH):
                xs_c = XS[:, c * T + h * 1024: c * T + (h + 1) * 1024]
                if fp32_io:
                    xs_c = xT[:, c * T + h * 1024: c * T + (h + 1) * 1024]
                t1 = MUB if c % 2 == 0 else WKB
                ev10 = eop(v, lambda e, t1=t1, i=xs_c:
                           e.tensor_mul(t1, i, CH[:, 2048:3072]), waits=[ev9])
                p.wait("vector", ev10)
                ev11 = eop(v, lambda e, t1=t1: e.tensor_add(t1, t1, CH[:, 1024:2048]))
                p.wait("vector", ev11)
                if L is None:
                    wcp = [save_evs[c]] if save_evs is not None else ()
                    last = eop(v, lambda e, t1=t1, o=xs_c: e.tensor_copy(o, t1),
                               waits=wcp)
                else:
                    last = eop(v, lambda e, t1=t1, o=xs_c, gc=bias_col(L, "lng", c),
                               bc=bias_col(L, "lnb", c):
                               e.tensor_scalar(o, t1, gc, bc, ALU.mult, ALU.add))
        return last

    wsl_used = [None] * 6

    def load_slab(slot_i, src, ncols=1024, dma_cols=None, extra_waits=()):
        wl = list(extra_waits)
        if wsl_used[slot_i] is not None:
            wl.append(wsl_used[slot_i])
        slot = WSL[:, slot_i * 1024: slot_i * 1024 + ncols]
        dst = slot if dma_cols is None else slot[:, 0:dma_cols]
        return slot, dma_w(dst, src, waits=wl)

    def emit_mla():
        ev_xs = emit_xs(shift=False, waits=[ev_xin])
        p.wait("scalar", ev_xs)
        p.wait("tensor", ev_xs)
        p.wait("vector", ev_xs)
        ev_ln = emit_ln(None)
        p.wait("tensor", ev_ln)

        latent = AO
        ev_prev = [None, None]
        ev_lat_mm = None
        for n in range(4):
            slot, ev_w = load_slab(n % 4, slab_src(dWc_down, 0, n * 128, 128, 8))
            for q4 in range(4):
                pg = q4 % 2
                pb = PS[:, pg * 512:(pg + 1) * 512]
                w = [ev_w]
                if ev_prev[pg] is not None:
                    w.append(ev_prev[pg])
                for k in range(NCH):
                    ev_lat_mm = mm(pb, slot[:, k * 128:(k + 1) * 128],
                                   XS[:, k * T + q4 * 512: k * T + (q4 + 1) * 512],
                                   k == 0, k == NCH - 1,
                                   waits=w if k == 0 else (), inc=(k == NCH - 1))
                ev_prev[pg] = eop(a, lambda e,
                                  o=latent[:, n * T + q4 * 512: n * T + (q4 + 1) * 512],
                                  i=pb: e.activation(o, i, AF.Copy), waits=[ev_lat_mm])
            wsl_used[n % 4] = ev_lat_mm
        ev_latent = p.ev("sA")

        kvbuf = AR[:, 0:2048]
        vstage_m = AR[:, 5120:7168]
        slot, ev_w = load_slab(4, slab_src(dWkv_up, 0, 0, 128, 4), ncols=512)
        wlB = [wsl_used[5]] if wsl_used[5] is not None else []
        slotB = WSL[:, 5 * 1024: 5 * 1024 + 512]
        dstB1 = bass.AP(WSL, 5 * 1024 + 0, [[6144, 128], [128, 4], [1, 64]])
        dstB2 = bass.AP(WSL, 5 * 1024 + 64, [[6144, 128], [128, 4], [1, 64]])
        ev_wB1 = dma_w(dstB1, slab_src(dWkv_up, 0, 64, 64, 4), waits=wlB)
        ev_wB = dma_w(dstB2, slab_src(dWkv_up, 0, 0, 64, 4))
        ev_kv_mm = None
        for q4 in range(4):
            pg = q4 % 2
            pb = PS[:, pg * 512:(pg + 1) * 512]
            w = [ev_w, ev_latent]
            if ev_prev[pg] is not None:
                w.append(ev_prev[pg])
            for k in range(4):
                ev_kv_mm = mm(pb, slot[:, k * 128:(k + 1) * 128],
                              latent[:, k * T + q4 * 512: k * T + (q4 + 1) * 512],
                              k == 0, k == 3, waits=w if k == 0 else (), inc=(k == 3))
            eop(a, lambda e, o=kvbuf[0:64, q4 * 512:(q4 + 1) * 512],
                i=pb[0:64, :]: e.activation(o, i, AF.Copy), waits=[ev_kv_mm])
            ev_prev[pg] = eop(a, lambda e,
                              o=vstage_m[64:128, q4 * 512:(q4 + 1) * 512],
                              i=pb[64:128, :]: e.activation(o, i, AF.Copy))
        wsl_used[4] = ev_kv_mm
        for q4 in range(4):
            pg = q4 % 2
            pb = PS[:, pg * 512:(pg + 1) * 512]
            w = [ev_wB, ev_wB1]
            if ev_prev[pg] is not None:
                w.append(ev_prev[pg])
            for k in range(4):
                ev_kv_mm = mm(pb, slotB[:, k * 128:(k + 1) * 128],
                              latent[:, k * T + q4 * 512: k * T + (q4 + 1) * 512],
                              k == 0, k == 3, waits=w if k == 0 else (), inc=(k == 3))
            ev_prev[pg] = eop(a, lambda e,
                              o=kvbuf[64:128, q4 * 512:(q4 + 1) * 512],
                              i=pb[64:128, :]: e.activation(o, i, AF.Copy),
                              waits=[ev_kv_mm])
        wsl_used[5] = ev_kv_mm
        ev_kv = p.ev("sA")
        ev_kdup = ev_kv

        Vtok = AR[:, 4096:5120]
        ev_vt = None
        for tt in range(16):
            ev_vt = dma_s(Vtok[:, tt * 64:(tt + 1) * 64],
                          vstage_m[64:128, tt * 128:(tt + 1) * 128],
                          waits=[ev_kv] if tt == 0 else (), transpose=True)
        ev_vtok = p.evs_prefix("sXD")

        qhat = AR[:, 2048:4096]
        Pbuf = [AR[:, 5120:7168], AR[:, 7168:9216]]
        PT = AR[:, 9216:9984]
        SB = PS[:, 0:2048]
        ev_p_guard0 = ev_vtok

        ev_q_used = None
        ev_exp_prev = None
        ev_p_used = [None, None]
        ev_av_evac = [None, None]
        pt_slot_ev = [None] * 6
        pt_i = 0
        ev_qp_prev = [None, None]
        ev_sim_last = None
        ev_avmm = None

        for c in range(NCH):
            slot, ev_w = load_slab(c % 2, slab_src(dWq_up, 0, c * 128, 128, 4),
                                   ncols=512)
            ev_evq = None
            for q4 in range(4):
                pg = q4 % 2
                pb = PS[:, 2048 + pg * 512: 2048 + (pg + 1) * 512]
                w = [ev_w]
                if ev_qp_prev[pg] is not None:
                    w.append(ev_qp_prev[pg])
                ev_mmq = None
                for k in range(4):
                    ev_mmq = mm(pb, slot[:, k * 128:(k + 1) * 128],
                                latent[:, k * T + q4 * 512: k * T + (q4 + 1) * 512],
                                k == 0, k == 3, waits=w if k == 0 else (), inc=(k == 3))
                w2 = [ev_mmq]
                if ev_q_used is not None and q4 == 0:
                    w2.append(ev_q_used)
                ev_evq = eop(a, lambda e, o=qhat[:, q4 * 512:(q4 + 1) * 512], i=pb:
                             e.activation(o, i, AF.Copy), waits=w2)
                ev_qp_prev[pg] = ev_evq
            wsl_used[c % 2] = ev_mmq
            ev_qhat = ev_evq

            for hh in range(2):
                qrow = qhat[hh * 64:(hh + 1) * 64, :]
                for qt in range(16):
                    srange = 128 * (qt + 1)
                    nchunks = cdiv(srange, 512)
                    w = [ev_qhat, ev_kv, ev_kdup]
                    if ev_exp_prev is not None:
                        w.append(ev_exp_prev)
                    ev_sim = None
                    for ci in range(nchunks):
                        n0 = ci * 512
                        n1 = min(srange, n0 + 512)
                        ev_sim = mm(SB[:, n0:n1], qrow[:, qt * 128:(qt + 1) * 128],
                                    kvbuf[hh * 64:hh * 64 + 64, n0:n1], True, True,
                                    waits=w if ci == 0 else (),
                                    inc=(ci == nchunks - 1))
                    ev_msk = eop(v, lambda e, o=SB[:, (srange - 128):srange]:
                                 e.tensor_add(o, o, TRIL[:, :]), waits=[ev_sim])
                    ev_rmx = eop(v, lambda e, i=SB[:, 0:srange]:
                                 e.tensor_reduce(NEGM, i, AX.X, ALU.max,
                                                 negate=True), waits=[ev_msk])
                    p.wait("vector", ev_rmx)
                    ev_bb = tsc(v, NEGM2, NEGM, MLA_SCALE, None, ALU.mult)
                    pg = qt % 2
                    Pq = Pbuf[pg]
                    w = [ev_bb]
                    if pg == 0:
                        w.append(ev_p_guard0)
                    if ev_p_used[pg] is not None:
                        w.append(ev_p_used[pg])
                    ev_exp = eop(a, lambda e, i=SB[:, 0:srange], o=Pq[:, 0:srange]:
                                 e.activation(o, i, AF.Exp, bias=NEGM2,
                                              scale=MLA_SCALE, accum_out=LCOL),
                                 waits=w)
                    ev_exp_prev = ev_exp
                    ev_rcp = eop(v, lambda e: e.reciprocal(LINV, LCOL),
                                 waits=[ev_exp])
                    p.wait("vector", ev_rcp)
                    ev_pn = tsc(v, Pq[:, 0:srange], Pq[:, 0:srange], LINV, None,
                                ALU.mult)
                    avb = PS[:, 3072 + pg * 512: 3072 + pg * 512 + 512]
                    av = avb[0:64, 0:128]
                    for ci in range(qt + 1):
                        sidx = pt_i % 6
                        pt_i += 1
                        w = [ev_pn]
                        if pt_slot_ev[sidx] is not None:
                            w.append(pt_slot_ev[sidx])
                        ev_tp = dma_s(PT[:, sidx * 128:(sidx + 1) * 128],
                                      Pq[:, ci * 128:(ci + 1) * 128],
                                      waits=w, transpose=True)
                        w2 = [ev_tp, ev_vtok]
                        if ci == 0 and ev_av_evac[pg] is not None:
                            w2.append(ev_av_evac[pg])
                        ev_avmm = mm(av, Vtok[:, ci * 64:(ci + 1) * 64],
                                     PT[:, sidx * 128:(sidx + 1) * 128],
                                     ci == 0, ci == qt, waits=w2, inc=True)
                        pt_slot_ev[sidx] = ev_avmm
                    ev_p_used[pg] = p.evs_prefix("sXD")
                    dst = XS[hh * 64:(hh + 1) * 64,
                             c * T + qt * 128: c * T + (qt + 1) * 128]
                    ev_av_evac[pg] = eop(a, lambda e, d=dst, s=av:
                                         e.activation(d, s, AF.Copy), waits=[ev_avmm])
            ev_q_used = p.ev("sT")
        ev_att = p.ev("sA")

        ev_prev = [None, None]
        ev_pj_mm = None
        for n in range(NCH):
            slot, ev_w = load_slab(n % 4, slab_src(dWc_proj, 0, n * 128, 128, 8))
            for q4 in range(4):
                pg = q4 % 2
                pb = PS[:, pg * 512:(pg + 1) * 512]
                w = [ev_w, ev_att]
                if ev_prev[pg] is not None:
                    w.append(ev_prev[pg])
                for k in range(NCH):
                    ev_pj_mm = mm(pb, slot[:, k * 128:(k + 1) * 128],
                                  XS[:, k * T + q4 * 512: k * T + (q4 + 1) * 512],
                                  k == 0, k == NCH - 1,
                                  waits=w if k == 0 else (), inc=(k == NCH - 1))
                ev_prev[pg] = eop(v, lambda e,
                                  o=xT[:, n * T + q4 * 512: n * T + (q4 + 1) * 512],
                                  i=pb:
                                  e.scalar_tensor_tensor(o, i, 1.0, o, ALU.mult,
                                                         ALU.add),
                                  waits=[ev_pj_mm])
            wsl_used[n % 4] = ev_pj_mm
        return p.ev("sV"), p.ev("sT")

    def emit_attn(L, ev_ready_v, ev_ready_t):
        p.wait("gpsimd", ev_ready_v)
        p.wait("gpsimd", ev_ready_t)
        ev_xs = emit_xs(shift=True)
        p.wait("tensor", ev_xs)
        p.wait("scalar", ev_xs)
        p.wait("vector", ev_xs)
        p.wait("vector", ev_qs_gather)

        qhat = AR[:, 0:2048]
        khat = AR[:, 2048:4096]
        vstage = AR[:, 4096:5120]
        Vtok = AR[:, 5120:7168]
        PbufR = [AR[:, 7168:7680], AR[:, 7680:8192]]
        PT = AR[:, 8192:8960]
        X2B = AR[:, 8960:9984]

        ev_pp_prev = [ev_ready_v, ev_ready_v]
        ev_ss_prev = [None, None]
        ev_rn_prev = p.ev("sV")
        ev_bcb_prev = p.ev("sV")
        ev_exp_prev = [p.ev("sA"), p.ev("sA")]
        ev_p_used = [None, None]
        ev_av_evac = [p.ev("sA"), p.ev("sA")]
        pt_slot_ev = [None] * 6
        pt_i = 0
        ev_vstage_used = [None, None]
        ev_vtok_used = None
        ev_wil_used = None
        ev_avmm = None
        ao_guard = [ev_ready_t]

        lay = L * D * 3 * D
        for c in range(NCH):
            sq, ev_wq = load_slab((3 * c) % 6,
                                  slab_src(dWqkv, 0, c * 128, 128, 8, lay_off=lay))
            sk, ev_wk = load_slab((3 * c + 1) % 6,
                                  slab_src(dWqkv, 0, D + c * 128, 128, 8, lay_off=lay))
            sv_, ev_wv = load_slab((3 * c + 2) % 6,
                                   slab_src(dWqkv, 0, 2 * D + c * 128, 128, 8,
                                            lay_off=lay))

            def il_src(slab):
                return bass.AP(slab.tensor, slab.offset,
                               [[6144, 128], [128, 8], [64, 2], [1, 32], [32, 2]])

            wq_il = WIL[:, 0:1024]
            wk_il = WIL[:, 1024:2048]
            wil_w = [ev_wq]
            if ev_wil_used is not None:
                wil_w.append(ev_wil_used)
            ev_ilq = eop(g, lambda e, d=wq_il, s=il_src(sq): e.tensor_copy(d, s),
                         waits=wil_w)
            ev_ilk = eop(g, lambda e, d=wk_il, s=il_src(sk): e.tensor_copy(d, s),
                         waits=[ev_wk])
            wsl_used[(3 * c) % 6] = ev_ilq
            wsl_used[(3 * c + 1) % 6] = ev_ilk

            def qk_chain(slab_il, ev_il, scale_col, tabc0, tabs0, out):
                nonlocal ev_rn_prev, ev_bcb_prev
                ev_out = None
                for h in range(2):
                    ev_ch_evac = [None, None]
                    for qq in range(2):
                        q4 = 2 * h + qq
                        pg = q4 % 2
                        pb = PS[:, pg * 512:(pg + 1) * 512]
                        w = [ev_il]
                        if ev_pp_prev[pg] is not None:
                            w.append(ev_pp_prev[pg])
                        ev_mmq = None
                        for k in range(NCH):
                            ev_mmq = mm(pb, slab_il[:, k * 128:(k + 1) * 128],
                                        XS[:, k * T + q4 * 512:
                                           k * T + (q4 + 1) * 512],
                                        k == 0, k == NCH - 1,
                                        waits=w if k == 0 else (),
                                        inc=(k == NCH - 1))
                        x2s = X2B[:, qq * 512:(qq + 1) * 512]
                        wsq = [ev_mmq]
                        if ev_ss_prev[qq] is not None:
                            wsq.append(ev_ss_prev[qq])
                        ev_sqr = eop(a, lambda e, i=pb, o=x2s:
                                     e.activation(o, i, AF.Square), waits=wsq)
                        wss = [ev_sqr]
                        if ev_rn_prev is not None:
                            wss.append(ev_rn_prev)
                        ev_ss = mm(PS[0:2, 1024:1536], HSEL[:, :], x2s, True, True,
                                   waits=wss, inc=True)
                        ev_ss_prev[qq] = ev_ss
                        ev_rn_prev = eop(v, lambda e,
                                         o=RN2[:, h * 1024 + qq * 512:
                                               h * 1024 + (qq + 1) * 512],
                                         i=PS[0:2, 1024:1536]: e.tensor_copy(o, i),
                                         waits=[ev_ss])
                        ev_ch_evac[qq] = eop(v, lambda e,
                                             o=CH[:, qq * 512:(qq + 1) * 512], i=pb:
                                             e.tensor_scalar(o, i, scale_col, None,
                                                             ALU.mult))
                        ev_pp_prev[pg] = ev_ch_evac[qq]
                    ev_rs = eop(a, lambda e, sl=slice(h * 1024, (h + 1) * 1024):
                                e.activation(RN2[:, sl], RN2[:, sl], AF.Ln,
                                             bias=EPS30[0:2, :]),
                                waits=[ev_rn_prev])
                    p.wait("scalar", ev_rs)
                    ev_rs = eop(a, lambda e, sl=slice(h * 1024, (h + 1) * 1024):
                                e.activation(RN2[:, sl], RN2[:, sl], AF.Exp,
                                             scale=-0.5))
                    q1 = CH[:, 0:1024]
                    sgw = CH[:, 1024:2048]
                    u_ = CH[:, 2048:3072]
                    shmask = [m_ ^ 1 for m_ in range(32)]
                    ev_sh = eop(v, lambda e, s=shmask: e.stream_shuffle(sgw, q1, s),
                                waits=[ev_ch_evac[0], ev_ch_evac[1]])
                    eop(v, lambda e, tc=TAB[:, tabc0 + h * 1024:
                                            tabc0 + (h + 1) * 1024]:
                        e.tensor_mul(u_, q1, tc))
                    p.wait("vector", ev_sh)
                    ev_sg = eop(v, lambda e, ts_=TAB[:, tabs0 + h * 1024:
                                                     tabs0 + (h + 1) * 1024]:
                                e.tensor_mul(sgw, sgw, ts_))
                    p.wait("vector", ev_sg)
                    ev_e1 = eop(v, lambda e: e.tensor_add(u_, u_, sgw))
                    p.wait("vector", ev_e1)
                    for qq in range(2):
                        bcb = PS[:, 1536:2048]
                        wbc = [ev_rs, ev_e1]
                        if ev_bcb_prev is not None:
                            wbc.append(ev_bcb_prev)
                        ev_bc = mm(bcb, SELQK[:, :],
                                   RN2[:, h * 1024 + qq * 512:
                                       h * 1024 + (qq + 1) * 512],
                                   True, True, waits=wbc, inc=True)
                        ev_out = eop(v, lambda e,
                                     o=out[:, h * 1024 + qq * 512:
                                           h * 1024 + (qq + 1) * 512],
                                     i=u_[:, qq * 512:(qq + 1) * 512], b_=bcb:
                                     e.tensor_mul(o, i, b_), waits=[ev_bc])
                        ev_bcb_prev = ev_out
                return ev_out

            ev_qh = qk_chain(wq_il, ev_ilq, bias_col(L, "qs"), 0 * T, 1 * T, qhat)
            ev_kh = qk_chain(wk_il, ev_ilk, bias_col(L, "ks"), 2 * T, 3 * T, khat)
            ev_wil_used = p.ev("sT")

            ev_vt = None
            for q4 in range(4):
                pg = q4 % 2
                pb = PS[:, pg * 512:(pg + 1) * 512]
                w = [ev_wv]
                if ev_pp_prev[pg] is not None:
                    w.append(ev_pp_prev[pg])
                ev_mmv = None
                for k in range(NCH):
                    ev_mmv = mm(pb, sv_[:, k * 128:(k + 1) * 128],
                                XS[:, k * T + q4 * 512: k * T + (q4 + 1) * 512],
                                k == 0, k == NCH - 1,
                                waits=w if k == 0 else (), inc=(k == NCH - 1))
                vs = vstage[:, (q4 % 2) * 512:(q4 % 2 + 1) * 512]
                wev = [ev_mmv]
                if ev_vstage_used[q4 % 2] is not None:
                    wev.append(ev_vstage_used[q4 % 2])
                ev_ev = eop(a, lambda e, o=vs, i=pb: e.activation(o, i, AF.Copy),
                            waits=wev)
                ev_pp_prev[pg] = ev_ev
                evs_q4 = []
                for i in range(4):
                    tt = 4 * q4 + i
                    wtp = [ev_ev]
                    if tt == 0 and ev_vtok_used is not None:
                        wtp.append(ev_vtok_used)
                    ev_vt = dma_s(Vtok[:, tt * 128:(tt + 1) * 128],
                                  vs[:, i * 128:(i + 1) * 128],
                                  waits=wtp, transpose=True)
                    evs_q4.append(ev_vt)
                ev_vstage_used[q4 % 2] = evs_q4
            ev_vtok = p.evs_prefix("sXD")
            wsl_used[(3 * c + 2) % 6] = p.ev("sT")

            for hh in range(2):
                qrow = qhat[hh * 64:(hh + 1) * 64, :]
                krow = khat[hh * 64:(hh + 1) * 64, :]
                for qt in range(16):
                    s0 = max(0, qt * 128 - 256)
                    send = (qt + 1) * 128
                    sw = send - s0
                    pg = qt % 2
                    Spb = PS[:, 2048 + pg * 512: 2048 + pg * 512 + sw]
                    w = [ev_qh, ev_kh]
                    if ev_exp_prev[pg] is not None:
                        w.append(ev_exp_prev[pg])
                    ev_sim = mm(Spb, qrow[:, qt * 128:(qt + 1) * 128],
                                krow[:, s0:send], True, True, waits=w, inc=True)
                    ev_m = ev_sim
                    if qt >= 2:
                        ev_m = eop(v, lambda e, o=Spb[:, 0:128]:
                                   e.tensor_add(o, o, TRIU[:, :]), waits=[ev_sim])
                    ev_m = eop(v, lambda e, o=Spb[:, sw - 128:sw]:
                               e.tensor_add(o, o, TRIL[:, :]), waits=[ev_m])
                    ev_rmx = eop(v, lambda e, i=Spb:
                                 e.tensor_reduce(NEGM, i, AX.X, ALU.max,
                                                 negate=True), waits=[ev_m])
                    p.wait("vector", ev_rmx)
                    ev_bb = tsc(v, NEGM2, NEGM, QK_SCALE, None, ALU.mult)
                    Pq = PbufR[pg]
                    w = [ev_bb]
                    if ev_p_used[pg] is not None:
                        w.append(ev_p_used[pg])
                    ev_exp = eop(a, lambda e, i=Spb, o=Pq[:, 0:sw]:
                                 e.activation(o, i, AF.Exp, bias=NEGM2,
                                              scale=QK_SCALE, accum_out=LCOL),
                                 waits=w)
                    ev_exp_prev[pg] = ev_exp
                    ev_rcp = eop(v, lambda e: e.reciprocal(LINV, LCOL),
                                 waits=[ev_exp])
                    p.wait("vector", ev_rcp)
                    ev_pn = tsc(v, Pq[:, 0:sw], Pq[:, 0:sw], LINV, None, ALU.mult)
                    avb = PS[:, 3072 + pg * 512: 3072 + pg * 512 + 512]
                    av = avb[0:64, 0:128]
                    ncu = sw // 128
                    for ci in range(ncu):
                        sidx = pt_i % 6
                        pt_i += 1
                        w = [ev_pn]
                        if pt_slot_ev[sidx] is not None:
                            w.append(pt_slot_ev[sidx])
                        ev_tp = dma_s(PT[:, sidx * 128:(sidx + 1) * 128],
                                      Pq[:, ci * 128:(ci + 1) * 128],
                                      waits=w, transpose=True)
                        stile = (s0 // 128) + ci
                        w2 = [ev_tp, ev_vtok]
                        if ci == 0 and ev_av_evac[pg] is not None:
                            w2.append(ev_av_evac[pg])
                        ev_avmm = mm(av,
                                     Vtok[:, stile * 128 + hh * 64:
                                          stile * 128 + hh * 64 + 64],
                                     PT[:, sidx * 128:(sidx + 1) * 128],
                                     ci == 0, ci == ncu - 1, waits=w2,
                                     inc=True)
                        pt_slot_ev[sidx] = ev_avmm
                    ev_p_used[pg] = p.evs_prefix("sXD")
                    dst = AO[hh * 64:(hh + 1) * 64,
                             c * T + qt * 128: c * T + (qt + 1) * 128]
                    w3 = [ev_avmm] + ao_guard
                    ao_guard = []
                    ev_av_evac[pg] = eop(a, lambda e, d=dst, s=av:
                                         e.activation(d, s, AF.Copy), waits=w3)
            ev_vtok_used = p.ev("sT")
        ev_att = p.ev("sA")

        ev_prev = [None, None]
        ev_pj = None
        for n in range(NCH):
            slot, ev_w = load_slab(n % 6, slab_src(dWo, 0, n * 128, 128, 8,
                                                   lay_off=L * D * D))
            for q4 in range(4):
                pg = q4 % 2
                pb = PS[:, pg * 512:(pg + 1) * 512]
                w = [ev_w, ev_att]
                if ev_prev[pg] is not None:
                    w.append(ev_prev[pg])
                for k in range(NCH):
                    ev_pj = mm(pb, slot[:, k * 128:(k + 1) * 128],
                               AO[:, k * T + q4 * 512: k * T + (q4 + 1) * 512],
                               k == 0, k == NCH - 1,
                               waits=w if k == 0 else (), inc=(k == NCH - 1))
                ev_prev[pg] = eop(v, lambda e,
                                  o=xT[:, n * T + q4 * 512: n * T + (q4 + 1) * 512],
                                  i=pb: e.scalar_tensor_tensor(o, i, 1.0, o,
                                                               ALU.mult, ALU.add),
                                  waits=[ev_pj])
            wsl_used[n % 6] = ev_pj
        return p.ev("sV"), p.ev("sT")

    def emit_ffn(L, ev_ready_v, ev_ready_t):
        p.wait("gpsimd", ev_ready_v)
        p.wait("gpsimd", ev_ready_t)
        ev_xs = emit_xs(shift=True)
        p.wait("scalar", ev_xs)
        p.wait("vector", ev_xs)
        p.wait("tensor", ev_xs)
        emit_ln(L)
        p.wait("tensor", p.ev("sA"))
        p.wait("tensor", p.ev("sV"))

        def g_tile(j):
            if j < 16:
                return AO[:, j * 1024:(j + 1) * 1024]
            return AR[:, (j - 16) * 1024:(j - 15) * 1024]

        GGs = [CH[:, 0:1024], CH[:, 1024:2048]]
        AB = CH[:, 2048:3072]
        ev_res_last = None
        for h in range(2):
            hsl0 = h * 1024
            ev_glu = [None] * 22
            ev_act_pair = [None] * 22
            ev_gelu = [None] * 22
            lay1 = L * D * 2 * DI
            for j in range(22):
                wj = 128 if j < 21 else 42
                sa_, ev_wa = load_slab((2 * j) % 6,
                                       slab_src(dW1, 0, j * 128, wj, 8, lay_off=lay1),
                                       ncols=8 * wj)
                sg_, ev_wg = load_slab((2 * j + 1) % 6,
                                       slab_src(dW1, 0, DI + j * 128, wj, 8,
                                                lay_off=lay1),
                                       ncols=8 * wj)
                apb = PS[0:wj, (j % 2) * 1024:(j % 2) * 1024 + 1024]
                gpb = PS[0:wj, 2048 + (j % 2) * 1024: 2048 + (j % 2) * 1024 + 1024]
                w = [ev_wa]
                if j >= 2 and ev_act_pair[j - 2] is not None:
                    w.append(ev_act_pair[j - 2])
                if j == 0:
                    w.append(ev_res_last if ev_res_last is not None else ev_ready_v)
                    w.append(p.ev("sA"))
                ev_mma = None
                for k in range(NCH):
                    for sl in range(2):
                        ev_mma = mm(apb[:, sl * 512:(sl + 1) * 512],
                                    sa_[:, k * wj:(k + 1) * wj],
                                    XS[:, k * T + hsl0 + sl * 512:
                                       k * T + hsl0 + (sl + 1) * 512],
                                    k == 0, k == NCH - 1,
                                    waits=w if (k == 0 and sl == 0) else (),
                                    inc=(k == NCH - 1 and sl == 1))
                wgm = [ev_wg]
                if j >= 2 and ev_gelu[j - 2] is not None:
                    wgm.append(ev_gelu[j - 2])
                if j == 0:
                    wgm.append(p.ev("sA"))
                ev_mmg = None
                for k in range(NCH):
                    for sl in range(2):
                        ev_mmg = mm(gpb[:, sl * 512:(sl + 1) * 512],
                                    sg_[:, k * wj:(k + 1) * wj],
                                    XS[:, k * T + hsl0 + sl * 512:
                                       k * T + hsl0 + (sl + 1) * 512],
                                    k == 0, k == NCH - 1,
                                    waits=wgm if (k == 0 and sl == 0) else (),
                                    inc=(k == NCH - 1 and sl == 1))
                wsl_used[(2 * j) % 6] = ev_mma
                wsl_used[(2 * j + 1) % 6] = ev_mmg
                GG = GGs[j % 2]
                wg1 = [ev_mmg]
                if j >= 2 and ev_glu[j - 2] is not None:
                    wg1.append(ev_glu[j - 2])
                ev_g1 = eop(a, lambda e, i=gpb, o=GG, bc=bias_col(L, "b1g", j), wj=wj:
                            e.activation(o[0:wj, :], i, AF.Gelu, bias=bc[0:wj, :]),
                            waits=wg1)
                ev_gelu[j] = ev_g1
                wg2 = [ev_mma]
                if j >= 1 and ev_glu[j - 1] is not None:
                    wg2.append(ev_glu[j - 1])
                ev_g2 = eop(v, lambda e, i=apb, bc=bias_col(L, "b1a", j), wj=wj:
                            e.tensor_scalar(AB[0:wj, :], i, bc[0:wj, :], None,
                                            ALU.add), waits=wg2)
                ev_act_pair[j] = ev_g2
                ev_glu[j] = eop(v, lambda e, o=g_tile(j), GG=GG, wj=wj:
                                e.tensor_mul(o[0:wj, :], AB[0:wj, :], GG[0:wj, :]),
                                waits=[ev_g1, ev_g2])
            ev_g_all = p.ev("sV")

            ev_prev = [None, None]
            lay2 = L * DI * D
            for n in range(NCH):
                base_slot = 0 if n % 2 == 0 else 3
                slabs = []
                for part in range(3):
                    nk = 8 if part < 2 else 5
                    row0 = part * 8 * 128
                    s_, ev_wx = load_slab(base_slot + part,
                                          slab_src(dW2, row0, n * 128, 128, nk,
                                                   lay_off=lay2),
                                          ncols=((nk + 1) * 128 if part == 2
                                                 else nk * 128),
                                          dma_cols=(nk * 128 if part == 2 else None))
                    slabs.append((s_, ev_wx))
                tail_src = bass.AP(dW2.tensor,
                                   dW2.offset + lay2 + 2688 * D + n * 128,
                                   [[D, 42], [1, 128]])
                s2_, _ = slabs[2]
                ev_wt2 = dma_w(s2_[0:42, 5 * 128:6 * 128], tail_src,
                               waits=[slabs[2][1]])
                ev_mm2 = None
                for q4 in range(2):
                    pg = q4 % 2
                    pb = PS[:, pg * 512:(pg + 1) * 512]
                    w = [ev_g_all, slabs[0][1], slabs[1][1], slabs[2][1], ev_wt2,
                         p.ev("sA")]
                    if ev_prev[pg] is not None:
                        w.append(ev_prev[pg])
                    for j in range(22):
                        wj = 128 if j < 21 else 42
                        part, kk = divmod(j, 8)
                        s_ = slabs[part][0]
                        ev_mm2 = mm(pb, s_[0:wj, kk * 128:(kk + 1) * 128],
                                    g_tile(j)[0:wj, q4 * 512:(q4 + 1) * 512],
                                    j == 0, j == 21,
                                    waits=w if j == 0 else (), inc=(j == 21))
                    ev_prev[pg] = eop(v, lambda e,
                                      o=xT[:, n * T + hsl0 + q4 * 512:
                                           n * T + hsl0 + (q4 + 1) * 512],
                                      i=pb, bc=bias_col(L, "b2", n):
                                      e.scalar_tensor_tensor(o, i, bc, o,
                                                             ALU.add, ALU.add),
                                      waits=[ev_mm2])
                for sidx in range(3):
                    wsl_used[base_slot + sidx] = ev_mm2
            ev_res_last = p.ev("sV")
        return p.ev("sV"), p.ev("sT")

    ev_v, ev_t = (ev_xin, p.ev("sT"))
    if stage >= 1:
        ev_v, ev_t = emit_mla()
    if stage >= 2:
        ev_save = []
        for c in range(NCH):
            ev_save.append(dma_s(dx1[:, c * T:(c + 1) * T],
                                 xT[:, c * T:(c + 1) * T], waits=[ev_v, ev_t]))
        ev_xs0 = emit_xs(shift=False, waits=[ev_v, ev_t])
        p.wait("scalar", ev_xs0)
        p.wait("tensor", ev_xs0)
        p.wait("vector", ev_xs0)
        ev_v = emit_ln(None, fp32_io=True, save_evs=ev_save)
        ev_t = p.ev("sT")
        for L in range(n_inner):
            ev_v, ev_t = emit_attn(L, ev_v, ev_t)
            if stage >= 3:
                ev_v, ev_t = emit_ffn(L, ev_v, ev_t)
            if stage < 4:
                break

    if TAP_SELECT is not None:
        region = {"XS": XS[:, :], "AO": AO[:, :], "AR": AR[:, 0:10112],
                  "TAB": TAB[:, :], "CH": CH[:, :], "xT": xT[:, :],
                  "BIAS": BIAS[:, :], "RN2": RN2[:, :],
                  "MISC": MISC[:, :], "ONES": ONES[:, :],
                  "WIL": WIL[:, :]}[TAP_SELECT]
        ncols = min(region.shape[-1], 16384)
        region = region[:, 0:ncols]
        yflat = bass.AP(dy.tensor, dy.offset, [[16384, 128], [1, ncols]])
        p.wait("gpsimd", ev_v)
        p.wait("gpsimd", ev_t)
        p.wait("gpsimd", p.ev("sA"))
        for e2 in p.evs_prefix("sXD"):
            p.wait("gpsimd", e2)
        dma_w(yflat, region)
        for e2 in p.evs_prefix("sWD"):
            p.wait("gpsimd", e2)
        with nc.Block() as block:
            p.replay(block)
        nc._stack.close()
        return nc

    if stage >= 2:
        ev_add_prev = [None, None]
        ev_ld_prev = [None, None]
        for ch in range(16):
            c, h = divmod(ch, 2)
            pg = ch % 2
            stg = CH[:, pg * 1024:(pg + 1) * 1024]
            reg = slice(c * T + h * 1024, c * T + (h + 1) * 1024)
            wl = []
            if ch < 2:
                wl = [ev_v, ev_t, p.ev("sA")]
            if ev_add_prev[pg] is not None:
                wl.append(ev_add_prev[pg])
            ev_ld = dma_s(stg, dx1[:, reg], waits=wl)
            ev_ld_prev[pg] = ev_ld
            wa = [ev_ld]
            if ch < 2:
                wa.append(ev_v)
            ev_a = eop(v, lambda e, o=xT[:, reg], s=stg:
                       e.tensor_add(o, o, s), waits=wa)
            ev_add_prev[pg] = ev_a
        ev_v = ev_add_prev[pg]
        ev_t = p.ev("sT")
        p.wait("vector", ev_v)

    ev_out_prev = [None, None]
    for tt in range(16):
        pg = tt % 2
        stg = CH[:, pg * 1024:(pg + 1) * 1024]
        pb = PS[:, 2048 + pg * 1024: 2048 + (pg + 1) * 1024]
        ev_mmo = None
        for c in range(NCH):
            w = []
            if tt == 0 and c == 0:
                w = [ev_v, ev_t]
            if c == 0 and ev_out_prev[pg] is not None:
                w.append(ev_out_prev[pg])
            ev_mmo = eop("tensor",
                         lambda e, c=c, pb=pb, tt=tt:
                         e.transpose(pb[:, c * 128:(c + 1) * 128],
                                     xT[:, c * T + tt * 128: c * T + (tt + 1) * 128],
                                     IDN[:, :]),
                         waits=w, inc=(c == NCH - 1))
        ev_ev = eop(v, lambda e, s=stg, pb=pb: e.tensor_copy(s, pb), waits=[ev_mmo])
        ev_out_prev[pg] = dma_s(dy[tt * 128:(tt + 1) * 128, :], stg, waits=[ev_ev])
    for e2 in p.evs_prefix("sXD"):
        p.wait("sync", e2)
    for e2 in p.evs_prefix("sWD"):
        p.wait("sync", e2)

    with nc.Block() as block:
        p.replay(block)
    nc._stack.close()
    import sys
    globals()["_LAST_P"] = p
    return nc


_NC_CACHE = {}


def _get_nc(stage=STAGE, n_inner=N_INNER):
    key = (stage, n_inner)
    if key not in _NC_CACHE:
        _NC_CACHE[key] = build_program(stage, n_inner)
    return _NC_CACHE[key]


def kernel(**inputs):
    x = np.ascontiguousarray(inputs["L0_attn_x"], dtype=np.float32)
    nc = _get_nc()
    wmap = {
        "Wc_down": np.ascontiguousarray(inputs["L0_attn_Wc_down"], dtype=np.float32),
        "Wq_up": np.ascontiguousarray(inputs["L0_attn_Wq_up"], dtype=np.float32),
        "Wkv_up": np.ascontiguousarray(inputs["L0_attn_Wkv_up"], dtype=np.float32),
        "Wc_proj": np.ascontiguousarray(inputs["L0_attn_Wc_proj"], dtype=np.float32),
        "Wqkv": np.ascontiguousarray(inputs["L0_ffn_Wqkv"], dtype=np.float32),
        "q_scale": np.ascontiguousarray(inputs["L0_ffn_q_scale"], dtype=np.float32),
        "k_scale": np.ascontiguousarray(inputs["L0_ffn_k_scale"], dtype=np.float32),
        "Wo": np.ascontiguousarray(inputs["L0_ffn_Wo"], dtype=np.float32),
        "ln_g": np.ascontiguousarray(inputs["L0_ffn_ln_g"], dtype=np.float32),
        "ln_b": np.ascontiguousarray(inputs["L0_ffn_ln_b"], dtype=np.float32),
        "W1": np.ascontiguousarray(inputs["L0_ffn_W1"], dtype=np.float32),
        "b1": np.ascontiguousarray(inputs["L0_ffn_b1"], dtype=np.float32),
        "W2": np.ascontiguousarray(inputs["L0_ffn_W2"], dtype=np.float32),
        "b2": np.ascontiguousarray(inputs["L0_ffn_b2"], dtype=np.float32),
    }
    in_maps = [dict(wmap, x=np.ascontiguousarray(x[b])) for b in range(8)]
    res = run_bass_kernel_spmd(nc, in_maps, core_ids=list(range(8)))
    y = np.stack([res.results[b]["y"] for b in range(8)], axis=0)
    return y


# version 04
# speedup vs baseline: 1.1007x; 1.1007x over previous
import math
import numpy as np

import concourse.bass as bass
import concourse.mybir as mybir
from concourse.bass_utils import run_bass_kernel_spmd

F32 = mybir.dt.float32
BF16 = mybir.dt.bfloat16
AF = mybir.ActivationFunctionType
ALU = mybir.AluOpType
AX = mybir.AxisListType

T = 2048
D = 1024
NCH = 8
HEADS = 16
HD = 64
DI = 2730
DEPTH = 6
LAT = 512
MAGIC = 1.5 * 2.0 ** 23
PI = math.pi
NEG = -1.0e30
PI_LO = 3.1415925
QK_SCALE = 8.0
MLA_SCALE = 0.125

STAGE = 4
TAP_SELECT = None
N_INNER = DEPTH

ENGS = ("tensor", "vector", "scalar", "gpsimd", "sync")


def cdiv(a, b):
    return (a + b - 1) // b


class P:

    def __init__(self, nc):
        self.nc = nc
        self.items = {e: [] for e in ENGS}
        self.semh = {}
        self.cnt = {}
        self.hw = {e: {} for e in ENGS}

    def sem(self, name):
        if name not in self.semh:
            self.semh[name] = self.nc._stack.enter_context(self.nc.semaphore(name))
            self.cnt[name] = 0
        return self.semh[name]

    def wait(self, eng, ev):
        if ev is None:
            return
        if isinstance(ev, list):
            for e2 in ev:
                self.wait(eng, e2)
            return
        name, val = ev
        if val <= 0:
            return
        assert val <= self.cnt.get(name, 0), \
            f"wait on future {name}={val}>{self.cnt.get(name, 0)}"
        if val <= self.hw[eng].get(name, 0):
            return
        self.hw[eng][name] = val
        s = self.sem(name)
        self.items[eng].append(lambda e, s=s, v=val: e.wait_ge(s, v))

    def op(self, eng, fn, inc=None, inc_by=1, waits=()):
        for ev in waits:
            self.wait(eng, ev)
        ev = None
        if inc is not None:
            s = self.sem(inc)
            self.cnt[inc] += inc_by
            ev = (inc, self.cnt[inc])

            def wrapped(e, fn=fn, s=s, by=inc_by):
                fn(e).then_inc(s, by)
            self.items[eng].append(wrapped)
        else:
            self.items[eng].append(fn)
        return ev

    def ev(self, name):
        return (name, self.cnt.get(name, 0))

    def evs_prefix(self, prefix):
        return [(n, c) for n, c in self.cnt.items() if n.startswith(prefix) and c]

    def replay(self, block):
        for eng in ENGS:
            seq = self.items[eng]

            def runner(e, seq=seq):
                for f in seq:
                    f(e)
            getattr(block, eng)(runner)


ECN = {"tensor": "sT", "vector": "sV", "scalar": "sA", "gpsimd": "sP"}


def build_program(stage=STAGE, n_inner=N_INNER, taps=None):
    import contextlib
    nc = bass.Bass("TRN2", target_bir_lowering=False)
    nc._stack = contextlib.ExitStack()
    st = nc._stack

    dx = nc.dram_tensor("x", [T, D], F32, kind="ExternalInput").ap()
    dy = nc.dram_tensor("y", [T, D], F32, kind="ExternalOutput").ap()
    dWc_down = nc.dram_tensor("Wc_down", [D, LAT], F32, kind="ExternalInput").ap()
    dWq_up = nc.dram_tensor("Wq_up", [LAT, D], F32, kind="ExternalInput").ap()
    dWkv_up = nc.dram_tensor("Wkv_up", [LAT, 128], F32, kind="ExternalInput").ap()
    dWc_proj = nc.dram_tensor("Wc_proj", [D, D], F32, kind="ExternalInput").ap()
    dWqkv = nc.dram_tensor("Wqkv", [DEPTH, D, 3 * D], F32, kind="ExternalInput").ap()
    dqs = nc.dram_tensor("q_scale", [DEPTH, HD], F32, kind="ExternalInput").ap()
    dks = nc.dram_tensor("k_scale", [DEPTH, HD], F32, kind="ExternalInput").ap()
    dWo = nc.dram_tensor("Wo", [DEPTH, D, D], F32, kind="ExternalInput").ap()
    dlng = nc.dram_tensor("ln_g", [DEPTH, D], F32, kind="ExternalInput").ap()
    dlnb = nc.dram_tensor("ln_b", [DEPTH, D], F32, kind="ExternalInput").ap()
    dW1 = nc.dram_tensor("W1", [DEPTH, D, 2 * DI], F32, kind="ExternalInput").ap()
    db1 = nc.dram_tensor("b1", [DEPTH, 2 * DI], F32, kind="ExternalInput").ap()
    dW2 = nc.dram_tensor("W2", [DEPTH, DI, D], F32, kind="ExternalInput").ap()
    db2 = nc.dram_tensor("b2", [DEPTH, D], F32, kind="ExternalInput").ap()
    dx1 = nc.dram_tensor("x1s", [128, NCH * T], F32, kind="Internal").ap()

    xT = st.enter_context(nc.sbuf_tensor("xT", [128, NCH * T], F32))
    XS = st.enter_context(nc.sbuf_tensor("XS", [128, NCH * T], BF16))
    AO = st.enter_context(nc.sbuf_tensor("AO", [128, NCH * T], BF16))
    TAB = st.enter_context(nc.sbuf_tensor("TAB", [128, 4 * T], BF16))
    CH = st.enter_context(nc.sbuf_tensor("CH", [128, 3 * 1024], F32))
    RN2 = st.enter_context(nc.sbuf_tensor("RN2", [2, T], F32))
    AR = st.enter_context(nc.sbuf_tensor("AR", [128, 10112], BF16))
    WSL = st.enter_context(nc.sbuf_tensor("WSL", [128, 6144], BF16))
    WIL = st.enter_context(nc.sbuf_tensor("WIL", [128, 2048], BF16))
    IDN = st.enter_context(nc.sbuf_tensor("IDN", [128, 128], F32))
    IDNB = st.enter_context(nc.sbuf_tensor("IDNB", [128, 128], BF16))
    TRIL = st.enter_context(nc.sbuf_tensor("TRIL", [128, 128], F32))
    TRIU = st.enter_context(nc.sbuf_tensor("TRIU", [128, 128], F32))
    ONES = st.enter_context(nc.sbuf_tensor("ONES", [128, 128], BF16))
    SELQK = st.enter_context(nc.sbuf_tensor("SELQK", [2, 128], F32))
    HSEL = st.enter_context(nc.sbuf_tensor("HSEL", [128, 2], BF16))
    BIAS = st.enter_context(nc.sbuf_tensor("BIAS", [128, 512], F32))
    MISC = st.enter_context(nc.sbuf_tensor("MISC", [128, 32], F32))
    BROW = st.enter_context(nc.sbuf_tensor("BROW", [44, 128], F32))

    PS = st.enter_context(nc.psum_tensor("PS", [128, 4096], F32))

    p = P(nc)

    def bias_col(L, name, idx=0):
        off = {"qs": 0, "ks": 1, "lng": 2, "lnb": 10, "b2": 18, "b1a": 26,
               "b1g": 48}[name]
        c = L * 80 + off + idx
        return BIAS[:, c:c + 1]

    NEGM = MISC[:, 0:1]
    NEGM2 = MISC[:, 1:2]
    LCOL = MISC[:, 2:3]
    LINV = MISC[:, 3:4]
    IOTAC = MISC[:, 4:5]
    INVF = MISC[:, 5:6]
    LNBS = MISC[:, 6:7]
    SGN = MISC[:, 7:8]
    C0 = MISC[:, 8:9]
    C1 = MISC[:, 9:10]
    EPS5 = MISC[:, 10:11]
    EPS30 = MISC[:, 11:12]

    KW, KX = 4, 6
    dma_ring = {"W": [0, [0] * KW], "X": [0, [0] * KX]}

    def _dma(queue_key, eng, fn, waits):
        idx, counts = dma_ring[queue_key]
        K = len(counts)
        s = idx % K
        name = f"s{queue_key}D{s}"
        p.wait(eng, (name, 16 * counts[s]) if counts[s] else None)
        for ev in waits:
            p.wait(eng, ev)
        counts[s] += 1
        dma_ring[queue_key][0] = idx + 1
        return p.op(eng, fn, inc=name, inc_by=16)

    def dma_w(dst, src, waits=()):
        return _dma("W", "gpsimd", lambda e, d=dst, s=src: e.dma_start(d, s), waits)

    def dma_s(dst, src, waits=(), transpose=False, slow=False):
        if transpose:
            return _dma("X", "sync",
                        lambda e, d=dst, s=src: e.dma_start_transpose(d, s), waits)
        if slow:
            return _dma("X", "sync", lambda e, d=dst, s=src:
                        e.dma_start(d, s, allow_slow_non_contiguous=True), waits)
        return _dma("X", "sync", lambda e, d=dst, s=src: e.dma_start(d, s), waits)

    def eop(eng, fn, waits=(), inc=True):
        return p.op(eng, fn, inc=ECN[eng] if inc else None, waits=waits)

    def mm(out, lhsT, rhs, start, stop, waits=(), inc=False):
        return eop("tensor",
                   lambda e, o=out, l=lhsT, r=rhs, s0=start, s1=stop:
                   e.matmul(o, l, r, start=s0, stop=s1),
                   waits=waits, inc=inc)

    def vfence():
        return eop("vector", lambda e: e.tensor_copy(MISC[:, 13:14],
                                                     MISC[:, 12:13]))

    def pfence():
        return eop("gpsimd", lambda e: e.memset(MISC[:, 14:15], 0.0))

    def tsc(eng, out, in0, s1, s2, op0, op1=None, waits=(), inc=True):
        if s2 is None:
            return eop(eng, lambda e, o=out, i=in0, a=s1, oo=op0:
                       e.tensor_scalar(o, i, a, None, oo), waits=waits, inc=inc)
        return eop(eng, lambda e, o=out, i=in0, a=s1, b=s2, oo=op0, o1=op1:
                   e.tensor_scalar(o, i, a, b, oo, o1), waits=waits, inc=inc)

    def slab_src(dram, row0, c0, w, nk, kstep=128, lay_off=0):
        Cst = dram.shape[-1]
        off = lay_off + row0 * Cst + c0
        return bass.AP(dram.tensor, dram.offset + off,
                       [[Cst, 128], [kstep * Cst, nk], [1, w]])

    g, v, a = "gpsimd", "vector", "scalar"

    ev = eop(g, lambda e: e.memset(IDN[:, :], 0.0))
    ev_idn = eop(g, lambda e: e.affine_select(out=IDN[:, :], in_=IDN[:, :],
                                              compare_op=ALU.not_equal, fill=1.0,
                                              base=0, pattern=[[-1, 128]],
                                              channel_multiplier=1), waits=[ev])
    eop(a, lambda e: e.activation(IDNB[:, :], IDN[:, :], AF.Copy),
        waits=[ev_idn])
    ev = eop(g, lambda e: e.memset(TRIL[:, :], 0.0))
    eop(g, lambda e: e.affine_select(out=TRIL[:, :], in_=TRIL[:, :],
                                     compare_op=ALU.is_ge, fill=NEG, base=0,
                                     pattern=[[-1, 128]], channel_multiplier=1),
        waits=[ev])
    ev = eop(g, lambda e: e.memset(TRIU[:, :], 0.0))
    eop(g, lambda e: e.affine_select(out=TRIU[:, :], in_=TRIU[:, :],
                                     compare_op=ALU.is_ge, fill=NEG, base=0,
                                     pattern=[[1, 128]], channel_multiplier=-1),
        waits=[ev])
    eop(g, lambda e: e.memset(ONES[:, :], 1.0), inc=False)
    eop(g, lambda e: e.memset(EPS5, 1e-5), inc=False)
    eop(g, lambda e: e.memset(EPS30, 1e-30), inc=False)
    eop(g, lambda e: e.memset(MISC[:, 12:13], 0.0), inc=False)
    ev = eop(g, lambda e: e.memset(SELQK[:, :], 1.0))
    ev = eop(g, lambda e: e.affine_select(out=SELQK[:, :], in_=SELQK[:, :],
                                          compare_op=ALU.is_ge, fill=0.0, base=63,
                                          pattern=[[-1, 128]],
                                          channel_multiplier=64), waits=[ev])
    eop(g, lambda e: e.affine_select(out=SELQK[:, :], in_=SELQK[:, :],
                                     compare_op=ALU.is_ge, fill=0.0, base=0,
                                     pattern=[[1, 128]], channel_multiplier=-64),
        waits=[ev])
    ev = eop(g, lambda e: e.memset(HSEL[:, :], 1.0))
    ev = eop(g, lambda e: e.affine_select(out=HSEL[:, :], in_=HSEL[:, :],
                                          compare_op=ALU.is_ge, fill=0.0, base=63,
                                          pattern=[[64, 2]],
                                          channel_multiplier=-1), waits=[ev])
    eop(g, lambda e: e.affine_select(out=HSEL[:, :], in_=HSEL[:, :],
                                     compare_op=ALU.is_ge, fill=0.0, base=64,
                                     pattern=[[-128, 2]], channel_multiplier=1),
        waits=[ev])
    eop(g, lambda e: e.iota(CH[:, 0:T], [[1, T]], channel_multiplier=0,
                            allow_small_or_imprecise_dtypes=True), inc=False)
    ev_g0 = eop(g, lambda e: e.iota(IOTAC, [[0, 1]], channel_multiplier=1,
                                    allow_small_or_imprecise_dtypes=True))

    ev = tsc(v, C0, IOTAC, 1.0 / 64.0, -0.499, ALU.mult, ALU.add, waits=[ev_g0])
    ev = tsc(v, C0, C0, MAGIC, MAGIC, ALU.add, ALU.subtract, waits=[ev])
    ev = tsc(v, C1, C0, -64.0, None, ALU.mult, waits=[ev])
    ev = eop(v, lambda e: e.tensor_add(C1, C1, IOTAC), waits=[ev])
    ev = tsc(v, C0, C1, 0.5, -0.25, ALU.mult, ALU.add, waits=[ev])
    ev = tsc(v, C0, C0, MAGIC, MAGIC, ALU.add, ALU.subtract, waits=[ev])
    ev = tsc(v, INVF, C0, -math.log(10000.0) / 32.0, None, ALU.mult, waits=[ev])
    ev_invf = eop(a, lambda e: e.activation(INVF, INVF, AF.Exp), waits=[ev])
    ev = tsc(v, C1, C0, 2.0 / 89.6, 25.6 / 89.6, ALU.mult, ALU.add, waits=[ev])
    ev_lnbs = eop(a, lambda e: e.activation(LNBS, C1, AF.Ln), waits=[ev])
    ev = tsc(v, C0, IOTAC, 0.5, -0.25, ALU.mult, ALU.add, waits=[ev])
    ev = tsc(v, C0, C0, MAGIC, MAGIC, ALU.add, ALU.subtract, waits=[ev])
    ev = tsc(v, C0, C0, -2.0, None, ALU.mult, waits=[ev])
    ev = eop(v, lambda e: e.tensor_add(C0, C0, IOTAC), waits=[ev])
    ev_sgn = tsc(v, SGN, C0, 2.0, -1.0, ALU.mult, ALU.add, waits=[ev])

    ev_trow = ev_g0
    p.wait("tensor", p.ev("sP"))

    CH1 = CH[:, 2048:3072]
    for h in range(2):
        A_ = PS[:, 2048:3072]
        B_ = PS[:, 3072:4096]
        A2_ = PS[:, 0:1024]
        trow_h = CH[:, h * 1024:(h + 1) * 1024]
        ev = tsc(v, A_, trow_h, INVF, None, ALU.mult, waits=[ev_trow, ev_invf])
        ev = tsc(v, B_, A_, 1.0 / (2 * PI), 0.25, ALU.mult, ALU.add, waits=[ev])
        ev = tsc(v, CH1, B_, MAGIC, MAGIC, ALU.add, ALU.subtract, waits=[ev])
        ev = eop(v, lambda e, A=A_, B=B_:
                 e.scalar_tensor_tensor(B, CH1, -2 * PI, A, ALU.mult, ALU.add),
                 waits=[ev])
        ev = tsc(v, B_, B_, PI / 2, None, ALU.add, waits=[ev])
        ev = tsc(v, B_, B_, -PI_LO, PI_LO, ALU.max, ALU.min, waits=[ev])
        ev_c = eop(a, lambda e, B=B_: e.activation(B, B, AF.Sin), waits=[ev])
        ev_cc = eop(v, lambda e, B=B_: e.tensor_copy(CH1, B), waits=[ev_c])
        ev = tsc(v, B_, trow_h, 1.0 / 128.0, -8.0, ALU.mult, ALU.add,
                 waits=[ev_cc, ev_c])
        ev = tsc(v, B_, B_, LNBS, None, ALU.mult, waits=[ev, ev_lnbs])
        ev_s = eop(a, lambda e, B=B_: e.activation(B, B, AF.Exp), waits=[ev])
        ev1 = eop(v, lambda e, B=B_, o=TAB[:, 0 * T + h * 1024: 0 * T + (h + 1) * 1024]:
                  e.tensor_mul(o, CH1, B), waits=[ev_s, ev_cc])
        ev = tsc(v, A2_, trow_h, 1.0 / 128.0, -8.0, ALU.mult, ALU.add, waits=[ev1])
        ev = tsc(v, A2_, A2_, LNBS, None, ALU.mult, waits=[ev])
        ev_iv = eop(a, lambda e, A2=A2_: e.activation(A2, A2, AF.Exp, scale=-1.0),
                    waits=[ev])
        ev2 = eop(v, lambda e, A2=A2_, o=TAB[:, 2 * T + h * 1024: 2 * T + (h + 1) * 1024]:
                  e.tensor_mul(o, CH1, A2), waits=[ev_iv])
        ev = tsc(v, CH1, A_, 1.0 / (2 * PI), MAGIC, ALU.mult, ALU.add, waits=[ev2])
        ev = tsc(v, CH1, CH1, MAGIC, None, ALU.subtract, waits=[ev])
        ev = eop(v, lambda e, A=A_: e.scalar_tensor_tensor(A, CH1, -2 * PI, A,
                                                           ALU.mult, ALU.add),
                 waits=[ev])
        ev = tsc(v, A_, A_, -PI_LO, PI_LO, ALU.max, ALU.min, waits=[ev])
        ev_sn = eop(a, lambda e, A=A_: e.activation(A, A, AF.Sin), waits=[ev])
        ev_sc = tsc(v, CH1, A_, SGN, None, ALU.mult, waits=[ev_sn, ev_sgn])
        ev3 = eop(v, lambda e, B=B_, o=TAB[:, 1 * T + h * 1024: 1 * T + (h + 1) * 1024]:
                  e.tensor_mul(o, CH1, B), waits=[ev_sc])
        ev4 = eop(v, lambda e, A2=A2_, o=TAB[:, 3 * T + h * 1024: 3 * T + (h + 1) * 1024]:
                  e.tensor_mul(o, CH1, A2), waits=[ev3])
        ev_trow = ev4
    ev_tables = p.ev("sV")

    ev_bias_ms = eop(g, lambda e: e.memset(BIAS[:, :], 0.0))
    ev_q = None
    for L in range(DEPTH):
        for nm, dr in (("qs", dqs), ("ks", dks)):
            col = L * 80 + {"qs": 0, "ks": 1}[nm]
            for hq in range(2):
                for tq in range(2):
                    dst = bass.AP(BIAS, col + (64 * hq + tq) * 512, [[1024, 32]])
                    srcp = bass.AP(dr.tensor, dr.offset + L * HD + 32 * tq,
                                   [[1, 32]])
                    ev_q = dma_s(dst, srcp,
                                 waits=[ev_q] if ev_q else [ev_bias_ms])
    ev_qs_gather = ev_q

    def vec_load(src_ap, n_rows, dst_ap, prev_ev, n_cols=128):
        wl = [prev_ev] if prev_ev else []
        ev_d = dma_s(BROW[0:n_rows, 0:n_cols], src_ap, waits=wl)
        ev_m = eop("tensor", lambda e, n=n_rows, m=n_cols:
                   e.transpose(PS[0:m, 1024:1024 + n], BROW[0:n, 0:m],
                               IDN[0:n, 0:n]),
                   waits=[ev_d], inc=True)
        return eop(v, lambda e, d=dst_ap, n=n_rows, m=n_cols:
                   e.tensor_copy(d, PS[0:m, 1024:1024 + n]), waits=[ev_m])

    ev_b = None
    for L in range(DEPTH):
        ev_b = vec_load(bass.AP(dlng.tensor, dlng.offset + L * D, [[128, 8], [1, 128]]),
                        8, BIAS[:, L * 80 + 2: L * 80 + 10], ev_b)
        ev_b = vec_load(bass.AP(dlnb.tensor, dlnb.offset + L * D, [[128, 8], [1, 128]]),
                        8, BIAS[:, L * 80 + 10: L * 80 + 18], ev_b)
        ev_b = vec_load(bass.AP(db2.tensor, db2.offset + L * D, [[128, 8], [1, 128]]),
                        8, BIAS[:, L * 80 + 18: L * 80 + 26], ev_b)
        ev_b = vec_load(bass.AP(db1.tensor, db1.offset + L * 2 * DI, [[128, 22], [1, 128]]),
                        22, BIAS[:, L * 80 + 26: L * 80 + 48], ev_b)
        ev_b = vec_load(bass.AP(db1.tensor, db1.offset + L * 2 * DI + DI,
                                [[128, 21], [1, 128]]),
                        21, BIAS[:, L * 80 + 48: L * 80 + 69], ev_b)
        ev_b = vec_load(bass.AP(db1.tensor, db1.offset + L * 2 * DI + DI + 21 * 128,
                                [[42, 1], [1, 42]]),
                        1, BIAS[0:42, L * 80 + 69: L * 80 + 70], ev_b, n_cols=42)
    ev_bias = ev_b

    ev_in_prev = [None, None]
    ev_mm = None
    for tt in range(16):
        pg = tt % 2
        stg = CH[:, pg * 1024:(pg + 1) * 1024]
        wl = []
        if ev_in_prev[pg] is not None:
            wl.append(ev_in_prev[pg])
        if tt < 2:
            wl += [ev_tables, ev_bias]
        ev_d = dma_s(stg, dx[tt * 128:(tt + 1) * 128, :], waits=wl)
        pb = PS[:, 2048 + pg * 1024: 2048 + (pg + 1) * 1024]
        for c in range(NCH):
            w = [ev_d] if c == 0 else []
            ev_mm = eop("tensor",
                        lambda e, c=c, stg=stg, pb=pb:
                        e.transpose(pb[:, c * 128:(c + 1) * 128],
                                    stg[:, c * 128:(c + 1) * 128], IDN[:, :]),
                        waits=w, inc=(c == NCH - 1))
        dst = bass.AP(xT, tt * 128, [[NCH * T, 128], [T, NCH], [1, 128]])
        ev_in_prev[pg] = eop(v, lambda e, d=dst, pb=pb: e.tensor_copy(d, pb),
                             waits=[ev_mm])
    ev_xin = p.ev("sV")


    def emit_xs(shift, waits=()):
        last = None
        wl = list(waits)
        for c in range(NCH):
            src = xT[:, c * T:(c + 1) * T]
            dst = XS[:, c * T:(c + 1) * T]
            if shift and c >= 4:
                last = eop(g, lambda e, d=dst, s=src:
                           e.tensor_copy(d[:, 1:T], s[:, 0:T - 1]), waits=wl)
                last = eop(g, lambda e, d=dst: e.memset(d[:, 0:1], 0.0))
            else:
                last = eop(g, lambda e, d=dst, s=src: e.tensor_copy(d, s), waits=wl)
            wl = []
        return last

    def emit_ln(L, fp32_io=False, save_evs=None):
        last = None
        p.wait("tensor", p.ev("sV"))
        p.wait("tensor", p.ev("sA"))
        p.wait("vector", p.ev("sV"))
        ev_sqmm = [None, None]
        ev_s1_read = None
        ev_s2_read = None
        for h in range(2):
            S1h = PS[:, 0:1024]
            S2h = PS[:, 1024:2048]
            MUB = PS[:, 2048:3072]
            WKB = PS[:, 3072:4096]
            ev_mm2 = None
            for sl in range(2):
                ssl = slice(sl * 512, (sl + 1) * 512)
                for c in range(NCH):
                    xs_c = XS[:, c * T + h * 1024: c * T + (h + 1) * 1024]
                    mm(S1h[:, ssl], ONES[:, :], xs_c[:, ssl], c == 0, c == NCH - 1,
                       inc=(c == NCH - 1),
                       waits=[ev_s1_read] if (c == 0 and sl == 0 and
                                              ev_s1_read) else ())
            for sl in range(2):
                ssl = slice(sl * 512, (sl + 1) * 512)
                for c in range(NCH):
                    x2 = AR[:, (c % 2) * 512:(c % 2) * 512 + 512]
                    xs_c = XS[:, c * T + h * 1024: c * T + (h + 1) * 1024]
                    wsq = []
                    if ev_sqmm[c % 2] is not None:
                        wsq.append(ev_sqmm[c % 2])
                    ev_sq = eop(a, lambda e, i=xs_c[:, ssl], o=x2:
                                e.activation(o, i, AF.Square), waits=wsq)
                    wm2 = [ev_sq]
                    if c == 0 and sl == 0 and ev_s2_read:
                        wm2.append(ev_s2_read)
                    ev_mm2 = mm(S2h[:, ssl], ONES[:, :], x2,
                                c == 0, c == NCH - 1, waits=wm2, inc=True)
                    ev_sqmm[c % 2] = ev_mm2
            ev_s1_read = tsc(v, MUB, S1h, 1.0 / D, None, ALU.mult, waits=[ev_mm2])
            p.wait("vector", ev_s1_read)
            ev_cp0 = eop(v, lambda e: e.tensor_copy(CH[:, 0:1024], MUB))
            if TAP_SELECT is not None and L is not None:
                eop(v, lambda e, h=h: e.tensor_copy(
                    AR[:, 6144 + h * 1024: 6144 + (h + 1) * 1024], CH[:, 0:1024]))
                if h == 0:
                    eop(v, lambda e: e.tensor_copy(AR[:, 8192:9216], S1h))
                    eop(v, lambda e: e.tensor_copy(AR[:, 9216:10112],
                                                   XS[:, 0:896]))
            ev_s2_read = tsc(v, CH[:, 1024:2048], S2h, 1.0 / D, None, ALU.mult)
            ev_mu2 = eop(v, lambda e: e.tensor_mul(MUB, CH[:, 0:1024],
                                                   CH[:, 0:1024]),
                         waits=[ev_cp0, ev_s2_read])
            p.wait("vector", ev_mu2)
            ev5 = eop(v, lambda e: e.scalar_tensor_tensor(
                CH[:, 1024:2048], MUB, -1.0, CH[:, 1024:2048], ALU.mult, ALU.add))
            ev6 = eop(a, lambda e: e.activation(WKB, CH[:, 1024:2048], AF.Ln,
                                                bias=EPS5), waits=[ev5])
            p.wait("scalar", ev6)
            ev7 = eop(a, lambda e: e.activation(WKB, WKB, AF.Exp, scale=-0.5))
            ev8 = eop(v, lambda e: e.tensor_copy(CH[:, 2048:3072], WKB),
                      waits=[ev7])
            ev9 = eop(v, lambda e: e.scalar_tensor_tensor(
                CH[:, 1024:2048], CH[:, 0:1024], -1.0, CH[:, 2048:3072],
                ALU.mult, ALU.mult), waits=[ev8])
            for c in range(NCH):
                xs_c = XS[:, c * T + h * 1024: c * T + (h + 1) * 1024]
                if fp32_io:
                    xs_c = xT[:, c * T + h * 1024: c * T + (h + 1) * 1024]
                t1 = MUB if c % 2 == 0 else WKB
                ev10 = eop(v, lambda e, t1=t1, i=xs_c:
                           e.tensor_mul(t1, i, CH[:, 2048:3072]), waits=[ev9])
                p.wait("vector", ev10)
                ev11 = eop(v, lambda e, t1=t1: e.tensor_add(t1, t1, CH[:, 1024:2048]))
                p.wait("vector", ev11)
                if L is None:
                    wcp = [save_evs[c]] if save_evs is not None else ()
                    last = eop(v, lambda e, t1=t1, o=xs_c: e.tensor_copy(o, t1),
                               waits=wcp)
                else:
                    last = eop(v, lambda e, t1=t1, o=xs_c, gc=bias_col(L, "lng", c),
                               bc=bias_col(L, "lnb", c):
                               e.tensor_scalar(o, t1, gc, bc, ALU.mult, ALU.add))
        return last

    wsl_used = [None] * 6

    def load_slab(slot_i, src, ncols=1024, dma_cols=None, extra_waits=()):
        wl = list(extra_waits)
        if wsl_used[slot_i] is not None:
            wl.append(wsl_used[slot_i])
        slot = WSL[:, slot_i * 1024: slot_i * 1024 + ncols]
        dst = slot if dma_cols is None else slot[:, 0:dma_cols]
        return slot, dma_w(dst, src, waits=wl)

    def emit_mla():
        ev_xs = emit_xs(shift=False, waits=[ev_xin])
        p.wait("scalar", ev_xs)
        p.wait("tensor", ev_xs)
        p.wait("vector", ev_xs)
        ev_ln = emit_ln(None)
        p.wait("tensor", ev_ln)

        latent = AO
        ev_prev = [None, None]
        ev_lat_mm = None
        for n in range(4):
            slot, ev_w = load_slab(n % 4, slab_src(dWc_down, 0, n * 128, 128, 8))
            for q4 in range(4):
                pg = q4 % 2
                pb = PS[:, pg * 512:(pg + 1) * 512]
                w = [ev_w]
                if ev_prev[pg] is not None:
                    w.append(ev_prev[pg])
                for k in range(NCH):
                    ev_lat_mm = mm(pb, slot[:, k * 128:(k + 1) * 128],
                                   XS[:, k * T + q4 * 512: k * T + (q4 + 1) * 512],
                                   k == 0, k == NCH - 1,
                                   waits=w if k == 0 else (), inc=(k == NCH - 1))
                ev_prev[pg] = eop(a, lambda e,
                                  o=latent[:, n * T + q4 * 512: n * T + (q4 + 1) * 512],
                                  i=pb: e.activation(o, i, AF.Copy), waits=[ev_lat_mm])
            wsl_used[n % 4] = ev_lat_mm
        ev_latent = p.ev("sA")

        kvbuf = AR[:, 0:2048]
        vstage_m = AR[:, 5120:7168]
        slot, ev_w = load_slab(4, slab_src(dWkv_up, 0, 0, 128, 4), ncols=512)
        wlB = [wsl_used[5]] if wsl_used[5] is not None else []
        slotB = WSL[:, 5 * 1024: 5 * 1024 + 512]
        dstB1 = bass.AP(WSL, 5 * 1024 + 0, [[6144, 128], [128, 4], [1, 64]])
        dstB2 = bass.AP(WSL, 5 * 1024 + 64, [[6144, 128], [128, 4], [1, 64]])
        ev_wB1 = dma_w(dstB1, slab_src(dWkv_up, 0, 64, 64, 4), waits=wlB)
        ev_wB = dma_w(dstB2, slab_src(dWkv_up, 0, 0, 64, 4))
        ev_kv_mm = None
        for q4 in range(4):
            pg = q4 % 2
            pb = PS[:, pg * 512:(pg + 1) * 512]
            w = [ev_w, ev_latent]
            if ev_prev[pg] is not None:
                w.append(ev_prev[pg])
            for k in range(4):
                ev_kv_mm = mm(pb, slot[:, k * 128:(k + 1) * 128],
                              latent[:, k * T + q4 * 512: k * T + (q4 + 1) * 512],
                              k == 0, k == 3, waits=w if k == 0 else (), inc=(k == 3))
            eop(a, lambda e, o=kvbuf[0:64, q4 * 512:(q4 + 1) * 512],
                i=pb[0:64, :]: e.activation(o, i, AF.Copy), waits=[ev_kv_mm])
            ev_prev[pg] = eop(a, lambda e,
                              o=vstage_m[64:128, q4 * 512:(q4 + 1) * 512],
                              i=pb[64:128, :]: e.activation(o, i, AF.Copy))
        wsl_used[4] = ev_kv_mm
        for q4 in range(4):
            pg = q4 % 2
            pb = PS[:, pg * 512:(pg + 1) * 512]
            w = [ev_wB, ev_wB1]
            if ev_prev[pg] is not None:
                w.append(ev_prev[pg])
            for k in range(4):
                ev_kv_mm = mm(pb, slotB[:, k * 128:(k + 1) * 128],
                              latent[:, k * T + q4 * 512: k * T + (q4 + 1) * 512],
                              k == 0, k == 3, waits=w if k == 0 else (), inc=(k == 3))
            ev_prev[pg] = eop(a, lambda e,
                              o=kvbuf[64:128, q4 * 512:(q4 + 1) * 512],
                              i=pb[64:128, :]: e.activation(o, i, AF.Copy),
                              waits=[ev_kv_mm])
        wsl_used[5] = ev_kv_mm
        ev_kv = p.ev("sA")
        ev_kdup = ev_kv

        Vtok = AR[:, 4096:5120]
        ev_vt = None
        for tt in range(16):
            ev_vt = dma_s(Vtok[:, tt * 64:(tt + 1) * 64],
                          vstage_m[64:128, tt * 128:(tt + 1) * 128],
                          waits=[ev_kv] if tt == 0 else (), transpose=True)
        ev_vtok = p.evs_prefix("sXD")

        qhat = AR[:, 2048:4096]
        Pbuf = [AR[:, 5120:7168], AR[:, 7168:9216]]
        PT = AR[:, 9216:9984]
        SB = PS[:, 0:2048]
        ev_p_guard0 = ev_vtok

        ev_q_used = None
        ev_exp_prev = None
        ev_p_used = [None, None]
        ev_av_evac = [None, None]
        pt_slot_ev = [None] * 6
        pt_i = 0
        ev_qp_prev = [None, None]
        ev_sim_last = None
        ev_avmm = None
        tps_ev = [None, None]

        for c in range(NCH):
            slot, ev_w = load_slab(c % 2, slab_src(dWq_up, 0, c * 128, 128, 4),
                                   ncols=512)
            ev_evq = None
            for q4 in range(4):
                pg = q4 % 2
                pb = PS[:, 2048 + pg * 512: 2048 + (pg + 1) * 512]
                w = [ev_w]
                if ev_qp_prev[pg] is not None:
                    w.append(ev_qp_prev[pg])
                if tps_ev[pg] is not None:
                    w.append(tps_ev[pg])
                ev_mmq = None
                for k in range(4):
                    ev_mmq = mm(pb, slot[:, k * 128:(k + 1) * 128],
                                latent[:, k * T + q4 * 512: k * T + (q4 + 1) * 512],
                                k == 0, k == 3, waits=w if k == 0 else (), inc=(k == 3))
                w2 = [ev_mmq]
                if ev_q_used is not None and q4 == 0:
                    w2.append(ev_q_used)
                ev_evq = eop(a, lambda e, o=qhat[:, q4 * 512:(q4 + 1) * 512], i=pb:
                             e.activation(o, i, AF.Copy), waits=w2)
                ev_qp_prev[pg] = ev_evq
            wsl_used[c % 2] = ev_mmq
            ev_qhat = ev_evq

            for hh in range(2):
                qrow = qhat[hh * 64:(hh + 1) * 64, :]
                for qt in range(16):
                    srange = 128 * (qt + 1)
                    nchunks = cdiv(srange, 512)
                    w = [ev_qhat, ev_kv, ev_kdup]
                    if ev_exp_prev is not None:
                        w.append(ev_exp_prev)
                    ev_sim = None
                    for ci in range(nchunks):
                        n0 = ci * 512
                        n1 = min(srange, n0 + 512)
                        ev_sim = mm(SB[:, n0:n1], qrow[:, qt * 128:(qt + 1) * 128],
                                    kvbuf[hh * 64:hh * 64 + 64, n0:n1], True, True,
                                    waits=w if ci == 0 else (),
                                    inc=(ci == nchunks - 1))
                    ev_msk = eop(v, lambda e, o=SB[:, (srange - 128):srange]:
                                 e.tensor_add(o, o, TRIL[:, :]), waits=[ev_sim])
                    ev_rmx = eop(v, lambda e, i=SB[:, 0:srange]:
                                 e.tensor_reduce(NEGM, i, AX.X, ALU.max,
                                                 negate=True), waits=[ev_msk])
                    p.wait("vector", ev_rmx)
                    ev_bb = tsc(v, NEGM2, NEGM, MLA_SCALE, None, ALU.mult)
                    pg = qt % 2
                    Pq = Pbuf[pg]
                    w = [ev_bb]
                    if pg == 0:
                        w.append(ev_p_guard0)
                    if ev_p_used[pg] is not None:
                        w.append(ev_p_used[pg])
                    ev_exp = eop(a, lambda e, i=SB[:, 0:srange], o=Pq[:, 0:srange]:
                                 e.activation(o, i, AF.Exp, bias=NEGM2,
                                              scale=MLA_SCALE, accum_out=LCOL),
                                 waits=w)
                    ev_exp_prev = ev_exp
                    ev_rcp = eop(v, lambda e: e.reciprocal(LINV, LCOL),
                                 waits=[ev_exp])
                    p.wait("vector", ev_rcp)
                    ev_pn = tsc(v, Pq[:, 0:srange], Pq[:, 0:srange], LINV, None,
                                ALU.mult)
                    avb = PS[:, 3072 + pg * 512: 3072 + pg * 512 + 512]
                    av = avb[0:64, 0:128]

                    def emit_av(ci, sidx_e, ev_e):
                        nonlocal ev_avmm
                        w2 = [ev_e, ev_vtok]
                        if ci == 0 and ev_av_evac[pg] is not None:
                            w2.append(ev_av_evac[pg])
                        ev_avmm = mm(av, Vtok[:, ci * 64:(ci + 1) * 64],
                                     PT[:, sidx_e * 128:(sidx_e + 1) * 128],
                                     ci == 0, ci == qt, waits=w2, inc=True)
                        pt_slot_ev[sidx_e] = ev_avmm

                    slots = []
                    for ci in range(qt + 1):
                        sidx = pt_i % 6
                        tb = pt_i % 2
                        pt_i += 1
                        tps = PS[:, 2048 + tb * 512:
                                 2048 + tb * 512 + 64].bitcast(BF16)
                        wt = [ev_pn]
                        if tps_ev[tb] is not None:
                            wt.append(tps_ev[tb])
                        ev_t1 = eop("tensor", lambda e, o=tps,
                                    i=Pq[:, ci * 128:(ci + 1) * 128]:
                                    e.transpose(o, i, IDNB[:, :]),
                                    waits=wt, inc=True)
                        we = [ev_t1]
                        if pt_slot_ev[sidx] is not None:
                            we.append(pt_slot_ev[sidx])
                        ev_e = eop(a, lambda e, d=PT[:, sidx * 128:(sidx + 1) * 128],
                                   s=tps: e.activation(d, s, AF.Copy),
                                   waits=we, inc=True)
                        tps_ev[tb] = ev_e
                        slots.append((ci, sidx, ev_e))
                        if ci >= 1:
                            emit_av(*slots[ci - 1])
                    emit_av(*slots[qt])
                    ev_p_used[pg] = p.ev("sT")
                    dst = XS[hh * 64:(hh + 1) * 64,
                             c * T + qt * 128: c * T + (qt + 1) * 128]
                    ev_av_evac[pg] = eop(a, lambda e, d=dst, s=av:
                                         e.activation(d, s, AF.Copy), waits=[ev_avmm])
            ev_q_used = p.ev("sT")
        ev_att = p.ev("sA")

        ev_prev = [None, None]
        ev_pj_mm = None
        for n in range(NCH):
            slot, ev_w = load_slab(n % 4, slab_src(dWc_proj, 0, n * 128, 128, 8))
            for q4 in range(4):
                pg = q4 % 2
                pb = PS[:, pg * 512:(pg + 1) * 512]
                w = [ev_w, ev_att]
                if ev_prev[pg] is not None:
                    w.append(ev_prev[pg])
                for k in range(NCH):
                    ev_pj_mm = mm(pb, slot[:, k * 128:(k + 1) * 128],
                                  XS[:, k * T + q4 * 512: k * T + (q4 + 1) * 512],
                                  k == 0, k == NCH - 1,
                                  waits=w if k == 0 else (), inc=(k == NCH - 1))
                ev_prev[pg] = eop(v, lambda e,
                                  o=xT[:, n * T + q4 * 512: n * T + (q4 + 1) * 512],
                                  i=pb:
                                  e.scalar_tensor_tensor(o, i, 1.0, o, ALU.mult,
                                                         ALU.add),
                                  waits=[ev_pj_mm])
            wsl_used[n % 4] = ev_pj_mm
        return p.ev("sV"), p.ev("sT")

    def emit_attn(L, ev_ready_v, ev_ready_t):
        p.wait("gpsimd", ev_ready_v)
        p.wait("gpsimd", ev_ready_t)
        ev_xs = emit_xs(shift=True)
        p.wait("tensor", ev_xs)
        p.wait("scalar", ev_xs)
        p.wait("vector", ev_xs)
        p.wait("vector", ev_qs_gather)

        qhat = AR[:, 0:2048]
        khat = AR[:, 2048:4096]
        vstage = AR[:, 4096:5120]
        Vtok = AR[:, 5120:7168]
        PbufR = [AR[:, 7168:7680], AR[:, 7680:8192]]
        PT = AR[:, 8192:8960]
        X2B = AR[:, 8960:9984]

        ev_pp_prev = [ev_ready_v, ev_ready_v]
        ev_ss_prev = [None, None]
        ev_rn_prev = p.ev("sV")
        ev_bcb_prev = p.ev("sV")
        ev_exp_prev = [p.ev("sA"), p.ev("sA")]
        ev_p_used = [None, None]
        ev_av_evac = [p.ev("sA"), p.ev("sA")]
        pt_slot_ev = [None] * 6
        pt_i = 0
        tps_ev = [None, None]
        ev_vstage_used = [None, None]
        ev_vtok_used = None
        ev_wil_used = None
        ev_avmm = None
        ao_guard = [ev_ready_t]

        lay = L * D * 3 * D
        for c in range(NCH):
            sq, ev_wq = load_slab((3 * c) % 6,
                                  slab_src(dWqkv, 0, c * 128, 128, 8, lay_off=lay))
            sk, ev_wk = load_slab((3 * c + 1) % 6,
                                  slab_src(dWqkv, 0, D + c * 128, 128, 8, lay_off=lay))
            sv_, ev_wv = load_slab((3 * c + 2) % 6,
                                   slab_src(dWqkv, 0, 2 * D + c * 128, 128, 8,
                                            lay_off=lay))

            def il_src(slab):
                return bass.AP(slab.tensor, slab.offset,
                               [[6144, 128], [128, 8], [64, 2], [1, 32], [32, 2]])

            wq_il = WIL[:, 0:1024]
            wk_il = WIL[:, 1024:2048]
            wil_w = [ev_wq]
            if ev_wil_used is not None:
                wil_w.append(ev_wil_used)
            ev_ilq = eop(g, lambda e, d=wq_il, s=il_src(sq): e.tensor_copy(d, s),
                         waits=wil_w)
            ev_ilk = eop(g, lambda e, d=wk_il, s=il_src(sk): e.tensor_copy(d, s),
                         waits=[ev_wk])
            wsl_used[(3 * c) % 6] = ev_ilq
            wsl_used[(3 * c + 1) % 6] = ev_ilk

            def qk_chain(slab_il, ev_il, scale_col, tabc0, tabs0, out):
                nonlocal ev_rn_prev, ev_bcb_prev
                ev_out = None
                for h in range(2):
                    ev_ch_evac = [None, None]
                    for qq in range(2):
                        q4 = 2 * h + qq
                        pg = q4 % 2
                        pb = PS[:, pg * 512:(pg + 1) * 512]
                        w = [ev_il]
                        if ev_pp_prev[pg] is not None:
                            w.append(ev_pp_prev[pg])
                        if tps_ev[pg] is not None:
                            w.append(tps_ev[pg])
                        ev_mmq = None
                        for k in range(NCH):
                            ev_mmq = mm(pb, slab_il[:, k * 128:(k + 1) * 128],
                                        XS[:, k * T + q4 * 512:
                                           k * T + (q4 + 1) * 512],
                                        k == 0, k == NCH - 1,
                                        waits=w if k == 0 else (),
                                        inc=(k == NCH - 1))
                        x2s = X2B[:, qq * 512:(qq + 1) * 512]
                        wsq = [ev_mmq]
                        if ev_ss_prev[qq] is not None:
                            wsq.append(ev_ss_prev[qq])
                        ev_sqr = eop(a, lambda e, i=pb, o=x2s:
                                     e.activation(o, i, AF.Square), waits=wsq)
                        wss = [ev_sqr]
                        if ev_rn_prev is not None:
                            wss.append(ev_rn_prev)
                        ev_ss = mm(PS[0:2, 1024:1536], HSEL[:, :], x2s, True, True,
                                   waits=wss, inc=True)
                        ev_ss_prev[qq] = ev_ss
                        ev_rn_prev = eop(v, lambda e,
                                         o=RN2[:, h * 1024 + qq * 512:
                                               h * 1024 + (qq + 1) * 512],
                                         i=PS[0:2, 1024:1536]: e.tensor_copy(o, i),
                                         waits=[ev_ss])
                        ev_ch_evac[qq] = eop(v, lambda e,
                                             o=CH[:, qq * 512:(qq + 1) * 512], i=pb:
                                             e.tensor_scalar(o, i, scale_col, None,
                                                             ALU.mult))
                        ev_pp_prev[pg] = ev_ch_evac[qq]
                    ev_rs = eop(a, lambda e, sl=slice(h * 1024, (h + 1) * 1024):
                                e.activation(RN2[:, sl], RN2[:, sl], AF.Ln,
                                             bias=EPS30[0:2, :]),
                                waits=[ev_rn_prev])
                    p.wait("scalar", ev_rs)
                    ev_rs = eop(a, lambda e, sl=slice(h * 1024, (h + 1) * 1024):
                                e.activation(RN2[:, sl], RN2[:, sl], AF.Exp,
                                             scale=-0.5))
                    q1 = CH[:, 0:1024]
                    sgw = CH[:, 1024:2048]
                    u_ = CH[:, 2048:3072]
                    shmask = [m_ ^ 1 for m_ in range(32)]
                    ev_sh = eop(v, lambda e, s=shmask: e.stream_shuffle(sgw, q1, s),
                                waits=[ev_ch_evac[0], ev_ch_evac[1]])
                    eop(v, lambda e, tc=TAB[:, tabc0 + h * 1024:
                                            tabc0 + (h + 1) * 1024]:
                        e.tensor_mul(u_, q1, tc))
                    p.wait("vector", ev_sh)
                    ev_sg = eop(v, lambda e, ts_=TAB[:, tabs0 + h * 1024:
                                                     tabs0 + (h + 1) * 1024]:
                                e.tensor_mul(sgw, sgw, ts_))
                    p.wait("vector", ev_sg)
                    ev_e1 = eop(v, lambda e: e.tensor_add(u_, u_, sgw))
                    p.wait("vector", ev_e1)
                    for qq in range(2):
                        bcb = PS[:, 1536:2048]
                        wbc = [ev_rs, ev_e1]
                        if ev_bcb_prev is not None:
                            wbc.append(ev_bcb_prev)
                        ev_bc = mm(bcb, SELQK[:, :],
                                   RN2[:, h * 1024 + qq * 512:
                                       h * 1024 + (qq + 1) * 512],
                                   True, True, waits=wbc, inc=True)
                        ev_out = eop(v, lambda e,
                                     o=out[:, h * 1024 + qq * 512:
                                           h * 1024 + (qq + 1) * 512],
                                     i=u_[:, qq * 512:(qq + 1) * 512], b_=bcb:
                                     e.tensor_mul(o, i, b_), waits=[ev_bc])
                        ev_bcb_prev = ev_out
                return ev_out

            ev_qh = qk_chain(wq_il, ev_ilq, bias_col(L, "qs"), 0 * T, 1 * T, qhat)
            ev_kh = qk_chain(wk_il, ev_ilk, bias_col(L, "ks"), 2 * T, 3 * T, khat)
            ev_wil_used = p.ev("sT")

            ev_vt = None
            for q4 in range(4):
                pg = q4 % 2
                pb = PS[:, pg * 512:(pg + 1) * 512]
                w = [ev_wv]
                if ev_pp_prev[pg] is not None:
                    w.append(ev_pp_prev[pg])
                if tps_ev[pg] is not None:
                    w.append(tps_ev[pg])
                ev_mmv = None
                for k in range(NCH):
                    ev_mmv = mm(pb, sv_[:, k * 128:(k + 1) * 128],
                                XS[:, k * T + q4 * 512: k * T + (q4 + 1) * 512],
                                k == 0, k == NCH - 1,
                                waits=w if k == 0 else (), inc=(k == NCH - 1))
                vs = vstage[:, (q4 % 2) * 512:(q4 % 2 + 1) * 512]
                wev = [ev_mmv]
                if ev_vstage_used[q4 % 2] is not None:
                    wev.append(ev_vstage_used[q4 % 2])
                ev_ev = eop(a, lambda e, o=vs, i=pb: e.activation(o, i, AF.Copy),
                            waits=wev)
                ev_pp_prev[pg] = ev_ev
                evs_q4 = []
                for i in range(4):
                    tt = 4 * q4 + i
                    wtp = [ev_ev]
                    if tt == 0 and ev_vtok_used is not None:
                        wtp.append(ev_vtok_used)
                    ev_vt = dma_s(Vtok[:, tt * 128:(tt + 1) * 128],
                                  vs[:, i * 128:(i + 1) * 128],
                                  waits=wtp, transpose=True)
                    evs_q4.append(ev_vt)
                ev_vstage_used[q4 % 2] = evs_q4
            ev_vtok = p.evs_prefix("sXD")
            wsl_used[(3 * c + 2) % 6] = p.ev("sT")

            for hh in range(2):
                qrow = qhat[hh * 64:(hh + 1) * 64, :]
                krow = khat[hh * 64:(hh + 1) * 64, :]
                for qt in range(16):
                    s0 = max(0, qt * 128 - 256)
                    send = (qt + 1) * 128
                    sw = send - s0
                    pg = qt % 2
                    Spb = PS[:, 2048 + pg * 512: 2048 + pg * 512 + sw]
                    w = [ev_qh, ev_kh]
                    if ev_exp_prev[pg] is not None:
                        w.append(ev_exp_prev[pg])
                    ev_sim = mm(Spb, qrow[:, qt * 128:(qt + 1) * 128],
                                krow[:, s0:send], True, True, waits=w, inc=True)
                    ev_m = ev_sim
                    if qt >= 2:
                        ev_m = eop(v, lambda e, o=Spb[:, 0:128]:
                                   e.tensor_add(o, o, TRIU[:, :]), waits=[ev_sim])
                    ev_m = eop(v, lambda e, o=Spb[:, sw - 128:sw]:
                               e.tensor_add(o, o, TRIL[:, :]), waits=[ev_m])
                    ev_rmx = eop(v, lambda e, i=Spb:
                                 e.tensor_reduce(NEGM, i, AX.X, ALU.max,
                                                 negate=True), waits=[ev_m])
                    p.wait("vector", ev_rmx)
                    ev_bb = tsc(v, NEGM2, NEGM, QK_SCALE, None, ALU.mult)
                    Pq = PbufR[pg]
                    w = [ev_bb]
                    if ev_p_used[pg] is not None:
                        w.append(ev_p_used[pg])
                    ev_exp = eop(a, lambda e, i=Spb, o=Pq[:, 0:sw]:
                                 e.activation(o, i, AF.Exp, bias=NEGM2,
                                              scale=QK_SCALE, accum_out=LCOL),
                                 waits=w)
                    ev_exp_prev[pg] = ev_exp
                    ev_rcp = eop(v, lambda e: e.reciprocal(LINV, LCOL),
                                 waits=[ev_exp])
                    p.wait("vector", ev_rcp)
                    ev_pn = tsc(v, Pq[:, 0:sw], Pq[:, 0:sw], LINV, None, ALU.mult)
                    avb = PS[:, 3072 + pg * 512: 3072 + pg * 512 + 512]
                    av = avb[0:64, 0:128]
                    ncu = sw // 128

                    def emit_av(ci, sidx_e, ev_e):
                        nonlocal ev_avmm
                        stile = (s0 // 128) + ci
                        w2 = [ev_e, ev_vtok]
                        if ci == 0 and ev_av_evac[pg] is not None:
                            w2.append(ev_av_evac[pg])
                        ev_avmm = mm(av,
                                     Vtok[:, stile * 128 + hh * 64:
                                          stile * 128 + hh * 64 + 64],
                                     PT[:, sidx_e * 128:(sidx_e + 1) * 128],
                                     ci == 0, ci == ncu - 1, waits=w2,
                                     inc=True)
                        pt_slot_ev[sidx_e] = ev_avmm

                    slots = []
                    for ci in range(ncu):
                        sidx = pt_i % 6
                        tb = pt_i % 2
                        pt_i += 1
                        tps = PS[:, tb * 512: tb * 512 + 64].bitcast(BF16)
                        wt = [ev_pn]
                        if tps_ev[tb] is not None:
                            wt.append(tps_ev[tb])
                        if ev_pp_prev[tb] is not None:
                            wt.append(ev_pp_prev[tb])
                        ev_t1 = eop("tensor", lambda e, o=tps,
                                    i=Pq[:, ci * 128:(ci + 1) * 128]:
                                    e.transpose(o, i, IDNB[:, :]),
                                    waits=wt, inc=True)
                        we = [ev_t1]
                        if pt_slot_ev[sidx] is not None:
                            we.append(pt_slot_ev[sidx])
                        ev_e = eop(a, lambda e, d=PT[:, sidx * 128:(sidx + 1) * 128],
                                   s=tps: e.activation(d, s, AF.Copy),
                                   waits=we, inc=True)
                        tps_ev[tb] = ev_e
                        slots.append((ci, sidx, ev_e))
                        if ci >= 1:
                            emit_av(*slots[ci - 1])
                    emit_av(*slots[ncu - 1])
                    ev_p_used[pg] = p.ev("sT")
                    dst = AO[hh * 64:(hh + 1) * 64,
                             c * T + qt * 128: c * T + (qt + 1) * 128]
                    w3 = [ev_avmm] + ao_guard
                    ao_guard = []
                    ev_av_evac[pg] = eop(a, lambda e, d=dst, s=av:
                                         e.activation(d, s, AF.Copy), waits=w3)
            ev_vtok_used = p.ev("sT")
        ev_att = p.ev("sA")

        ev_prev = [None, None]
        ev_pj = None
        for n in range(NCH):
            slot, ev_w = load_slab(n % 6, slab_src(dWo, 0, n * 128, 128, 8,
                                                   lay_off=L * D * D))
            for q4 in range(4):
                pg = q4 % 2
                pb = PS[:, pg * 512:(pg + 1) * 512]
                w = [ev_w, ev_att]
                if ev_prev[pg] is not None:
                    w.append(ev_prev[pg])
                for k in range(NCH):
                    ev_pj = mm(pb, slot[:, k * 128:(k + 1) * 128],
                               AO[:, k * T + q4 * 512: k * T + (q4 + 1) * 512],
                               k == 0, k == NCH - 1,
                               waits=w if k == 0 else (), inc=(k == NCH - 1))
                ev_prev[pg] = eop(v, lambda e,
                                  o=xT[:, n * T + q4 * 512: n * T + (q4 + 1) * 512],
                                  i=pb: e.scalar_tensor_tensor(o, i, 1.0, o,
                                                               ALU.mult, ALU.add),
                                  waits=[ev_pj])
            wsl_used[n % 6] = ev_pj
        return p.ev("sV"), p.ev("sT")

    def emit_ffn(L, ev_ready_v, ev_ready_t):
        p.wait("gpsimd", ev_ready_v)
        p.wait("gpsimd", ev_ready_t)
        ev_xs = emit_xs(shift=True)
        p.wait("scalar", ev_xs)
        p.wait("vector", ev_xs)
        p.wait("tensor", ev_xs)
        emit_ln(L)
        p.wait("tensor", p.ev("sA"))
        p.wait("tensor", p.ev("sV"))

        def g_tile(j):
            if j < 16:
                return AO[:, j * 1024:(j + 1) * 1024]
            return AR[:, (j - 16) * 1024:(j - 15) * 1024]

        GGs = [CH[:, 0:1024], CH[:, 1024:2048]]
        AB = CH[:, 2048:3072]
        ev_res_last = None
        for h in range(2):
            hsl0 = h * 1024
            ev_glu = [None] * 22
            ev_act_pair = [None] * 22
            ev_gelu = [None] * 22
            lay1 = L * D * 2 * DI
            for j in range(22):
                wj = 128 if j < 21 else 42
                sa_, ev_wa = load_slab((2 * j) % 6,
                                       slab_src(dW1, 0, j * 128, wj, 8, lay_off=lay1),
                                       ncols=8 * wj)
                sg_, ev_wg = load_slab((2 * j + 1) % 6,
                                       slab_src(dW1, 0, DI + j * 128, wj, 8,
                                                lay_off=lay1),
                                       ncols=8 * wj)
                apb = PS[0:wj, (j % 2) * 1024:(j % 2) * 1024 + 1024]
                gpb = PS[0:wj, 2048 + (j % 2) * 1024: 2048 + (j % 2) * 1024 + 1024]
                w = [ev_wa]
                if j >= 2 and ev_act_pair[j - 2] is not None:
                    w.append(ev_act_pair[j - 2])
                if j == 0:
                    w.append(ev_res_last if ev_res_last is not None else ev_ready_v)
                    w.append(p.ev("sA"))
                ev_mma = None
                for k in range(NCH):
                    for sl in range(2):
                        ev_mma = mm(apb[:, sl * 512:(sl + 1) * 512],
                                    sa_[:, k * wj:(k + 1) * wj],
                                    XS[:, k * T + hsl0 + sl * 512:
                                       k * T + hsl0 + (sl + 1) * 512],
                                    k == 0, k == NCH - 1,
                                    waits=w if (k == 0 and sl == 0) else (),
                                    inc=(k == NCH - 1 and sl == 1))
                wgm = [ev_wg]
                if j >= 2 and ev_gelu[j - 2] is not None:
                    wgm.append(ev_gelu[j - 2])
                if j == 0:
                    wgm.append(p.ev("sA"))
                ev_mmg = None
                for k in range(NCH):
                    for sl in range(2):
                        ev_mmg = mm(gpb[:, sl * 512:(sl + 1) * 512],
                                    sg_[:, k * wj:(k + 1) * wj],
                                    XS[:, k * T + hsl0 + sl * 512:
                                       k * T + hsl0 + (sl + 1) * 512],
                                    k == 0, k == NCH - 1,
                                    waits=wgm if (k == 0 and sl == 0) else (),
                                    inc=(k == NCH - 1 and sl == 1))
                wsl_used[(2 * j) % 6] = ev_mma
                wsl_used[(2 * j + 1) % 6] = ev_mmg
                GG = GGs[j % 2]
                wg1 = [ev_mmg]
                if j >= 2 and ev_glu[j - 2] is not None:
                    wg1.append(ev_glu[j - 2])
                ev_g1 = eop(a, lambda e, i=gpb, o=GG, bc=bias_col(L, "b1g", j), wj=wj:
                            e.activation(o[0:wj, :], i, AF.Gelu, bias=bc[0:wj, :]),
                            waits=wg1)
                ev_gelu[j] = ev_g1
                wg2 = [ev_mma]
                if j >= 1 and ev_glu[j - 1] is not None:
                    wg2.append(ev_glu[j - 1])
                ev_g2 = eop(v, lambda e, i=apb, bc=bias_col(L, "b1a", j), wj=wj:
                            e.tensor_scalar(AB[0:wj, :], i, bc[0:wj, :], None,
                                            ALU.add), waits=wg2)
                ev_act_pair[j] = ev_g2
                ev_glu[j] = eop(v, lambda e, o=g_tile(j), GG=GG, wj=wj:
                                e.tensor_mul(o[0:wj, :], AB[0:wj, :], GG[0:wj, :]),
                                waits=[ev_g1, ev_g2])
            ev_g_all = p.ev("sV")

            ev_prev = [None, None]
            lay2 = L * DI * D
            for n in range(NCH):
                base_slot = 0 if n % 2 == 0 else 3
                slabs = []
                for part in range(3):
                    nk = 8 if part < 2 else 5
                    row0 = part * 8 * 128
                    s_, ev_wx = load_slab(base_slot + part,
                                          slab_src(dW2, row0, n * 128, 128, nk,
                                                   lay_off=lay2),
                                          ncols=((nk + 1) * 128 if part == 2
                                                 else nk * 128),
                                          dma_cols=(nk * 128 if part == 2 else None))
                    slabs.append((s_, ev_wx))
                tail_src = bass.AP(dW2.tensor,
                                   dW2.offset + lay2 + 2688 * D + n * 128,
                                   [[D, 42], [1, 128]])
                s2_, _ = slabs[2]
                ev_wt2 = dma_w(s2_[0:42, 5 * 128:6 * 128], tail_src,
                               waits=[slabs[2][1]])
                ev_mm2 = None
                for q4 in range(2):
                    pg = q4 % 2
                    pb = PS[:, pg * 512:(pg + 1) * 512]
                    w = [ev_g_all, slabs[0][1], slabs[1][1], slabs[2][1], ev_wt2,
                         p.ev("sA")]
                    if ev_prev[pg] is not None:
                        w.append(ev_prev[pg])
                    for j in range(22):
                        wj = 128 if j < 21 else 42
                        part, kk = divmod(j, 8)
                        s_ = slabs[part][0]
                        ev_mm2 = mm(pb, s_[0:wj, kk * 128:(kk + 1) * 128],
                                    g_tile(j)[0:wj, q4 * 512:(q4 + 1) * 512],
                                    j == 0, j == 21,
                                    waits=w if j == 0 else (), inc=(j == 21))
                    ev_prev[pg] = eop(v, lambda e,
                                      o=xT[:, n * T + hsl0 + q4 * 512:
                                           n * T + hsl0 + (q4 + 1) * 512],
                                      i=pb, bc=bias_col(L, "b2", n):
                                      e.scalar_tensor_tensor(o, i, bc, o,
                                                             ALU.add, ALU.add),
                                      waits=[ev_mm2])
                for sidx in range(3):
                    wsl_used[base_slot + sidx] = ev_mm2
            ev_res_last = p.ev("sV")
        return p.ev("sV"), p.ev("sT")

    ev_v, ev_t = (ev_xin, p.ev("sT"))
    if stage >= 1:
        ev_v, ev_t = emit_mla()
    if stage >= 2:
        ev_save = []
        for c in range(NCH):
            ev_save.append(dma_s(dx1[:, c * T:(c + 1) * T],
                                 xT[:, c * T:(c + 1) * T], waits=[ev_v, ev_t]))
        ev_xs0 = emit_xs(shift=False, waits=[ev_v, ev_t])
        p.wait("scalar", ev_xs0)
        p.wait("tensor", ev_xs0)
        p.wait("vector", ev_xs0)
        ev_v = emit_ln(None, fp32_io=True, save_evs=ev_save)
        ev_t = p.ev("sT")
        for L in range(n_inner):
            ev_v, ev_t = emit_attn(L, ev_v, ev_t)
            if stage >= 3:
                ev_v, ev_t = emit_ffn(L, ev_v, ev_t)
            if stage < 4:
                break

    if TAP_SELECT is not None:
        region = {"XS": XS[:, :], "AO": AO[:, :], "AR": AR[:, 0:10112],
                  "TAB": TAB[:, :], "CH": CH[:, :], "xT": xT[:, :],
                  "BIAS": BIAS[:, :], "RN2": RN2[:, :],
                  "MISC": MISC[:, :], "ONES": ONES[:, :],
                  "WIL": WIL[:, :]}[TAP_SELECT]
        ncols = min(region.shape[-1], 16384)
        region = region[:, 0:ncols]
        yflat = bass.AP(dy.tensor, dy.offset, [[16384, 128], [1, ncols]])
        p.wait("gpsimd", ev_v)
        p.wait("gpsimd", ev_t)
        p.wait("gpsimd", p.ev("sA"))
        for e2 in p.evs_prefix("sXD"):
            p.wait("gpsimd", e2)
        dma_w(yflat, region)
        for e2 in p.evs_prefix("sWD"):
            p.wait("gpsimd", e2)
        with nc.Block() as block:
            p.replay(block)
        nc._stack.close()
        return nc

    if stage >= 2:
        ev_add_prev = [None, None]
        ev_ld_prev = [None, None]
        for ch in range(16):
            c, h = divmod(ch, 2)
            pg = ch % 2
            stg = CH[:, pg * 1024:(pg + 1) * 1024]
            reg = slice(c * T + h * 1024, c * T + (h + 1) * 1024)
            wl = []
            if ch < 2:
                wl = [ev_v, ev_t, p.ev("sA")]
            if ev_add_prev[pg] is not None:
                wl.append(ev_add_prev[pg])
            ev_ld = dma_s(stg, dx1[:, reg], waits=wl)
            ev_ld_prev[pg] = ev_ld
            wa = [ev_ld]
            if ch < 2:
                wa.append(ev_v)
            ev_a = eop(v, lambda e, o=xT[:, reg], s=stg:
                       e.tensor_add(o, o, s), waits=wa)
            ev_add_prev[pg] = ev_a
        ev_v = ev_add_prev[pg]
        ev_t = p.ev("sT")
        p.wait("vector", ev_v)

    ev_out_prev = [None, None]
    for tt in range(16):
        pg = tt % 2
        stg = CH[:, pg * 1024:(pg + 1) * 1024]
        pb = PS[:, 2048 + pg * 1024: 2048 + (pg + 1) * 1024]
        ev_mmo = None
        for c in range(NCH):
            w = []
            if tt == 0 and c == 0:
                w = [ev_v, ev_t]
            if c == 0 and ev_out_prev[pg] is not None:
                w.append(ev_out_prev[pg])
            ev_mmo = eop("tensor",
                         lambda e, c=c, pb=pb, tt=tt:
                         e.transpose(pb[:, c * 128:(c + 1) * 128],
                                     xT[:, c * T + tt * 128: c * T + (tt + 1) * 128],
                                     IDN[:, :]),
                         waits=w, inc=(c == NCH - 1))
        ev_ev = eop(v, lambda e, s=stg, pb=pb: e.tensor_copy(s, pb), waits=[ev_mmo])
        ev_out_prev[pg] = dma_s(dy[tt * 128:(tt + 1) * 128, :], stg, waits=[ev_ev])
    for e2 in p.evs_prefix("sXD"):
        p.wait("sync", e2)
    for e2 in p.evs_prefix("sWD"):
        p.wait("sync", e2)

    with nc.Block() as block:
        p.replay(block)
    nc._stack.close()
    import sys
    globals()["_LAST_P"] = p
    return nc


_NC_CACHE = {}


def _get_nc(stage=STAGE, n_inner=N_INNER):
    key = (stage, n_inner)
    if key not in _NC_CACHE:
        _NC_CACHE[key] = build_program(stage, n_inner)
    return _NC_CACHE[key]


def kernel(**inputs):
    x = np.ascontiguousarray(inputs["L0_attn_x"], dtype=np.float32)
    nc = _get_nc()
    wmap = {
        "Wc_down": np.ascontiguousarray(inputs["L0_attn_Wc_down"], dtype=np.float32),
        "Wq_up": np.ascontiguousarray(inputs["L0_attn_Wq_up"], dtype=np.float32),
        "Wkv_up": np.ascontiguousarray(inputs["L0_attn_Wkv_up"], dtype=np.float32),
        "Wc_proj": np.ascontiguousarray(inputs["L0_attn_Wc_proj"], dtype=np.float32),
        "Wqkv": np.ascontiguousarray(inputs["L0_ffn_Wqkv"], dtype=np.float32),
        "q_scale": np.ascontiguousarray(inputs["L0_ffn_q_scale"], dtype=np.float32),
        "k_scale": np.ascontiguousarray(inputs["L0_ffn_k_scale"], dtype=np.float32),
        "Wo": np.ascontiguousarray(inputs["L0_ffn_Wo"], dtype=np.float32),
        "ln_g": np.ascontiguousarray(inputs["L0_ffn_ln_g"], dtype=np.float32),
        "ln_b": np.ascontiguousarray(inputs["L0_ffn_ln_b"], dtype=np.float32),
        "W1": np.ascontiguousarray(inputs["L0_ffn_W1"], dtype=np.float32),
        "b1": np.ascontiguousarray(inputs["L0_ffn_b1"], dtype=np.float32),
        "W2": np.ascontiguousarray(inputs["L0_ffn_W2"], dtype=np.float32),
        "b2": np.ascontiguousarray(inputs["L0_ffn_b2"], dtype=np.float32),
    }
    in_maps = [dict(wmap, x=np.ascontiguousarray(x[b])) for b in range(8)]
    res = run_bass_kernel_spmd(nc, in_maps, core_ids=list(range(8)))
    y = np.stack([res.results[b]["y"] for b in range(8)], axis=0)
    return y


# version 05
# speedup vs baseline: 1.3128x; 1.1926x over previous
import math
import numpy as np

import concourse.bass as bass
import concourse.mybir as mybir
from concourse.bass_utils import run_bass_kernel_spmd

F32 = mybir.dt.float32
BF16 = mybir.dt.bfloat16
AF = mybir.ActivationFunctionType
ALU = mybir.AluOpType
AX = mybir.AxisListType

T = 2048
D = 1024
NCH = 8
HEADS = 16
HD = 64
DI = 2730
DEPTH = 6
LAT = 512
MAGIC = 1.5 * 2.0 ** 23
PI = math.pi
NEG = -1.0e30
PI_LO = 3.1415925
QK_SCALE = 8.0
MLA_SCALE = 0.125

STAGE = 4
TAP_SELECT = None
N_INNER = DEPTH

ENGS = ("tensor", "vector", "scalar", "gpsimd", "sync")


def cdiv(a, b):
    return (a + b - 1) // b


class P:

    def __init__(self, nc):
        self.nc = nc
        self.items = {e: [] for e in ENGS}
        self.semh = {}
        self.cnt = {}
        self.hw = {e: {} for e in ENGS}

    def sem(self, name):
        if name not in self.semh:
            self.semh[name] = self.nc._stack.enter_context(self.nc.semaphore(name))
            self.cnt[name] = 0
        return self.semh[name]

    def wait(self, eng, ev):
        if ev is None:
            return
        if isinstance(ev, list):
            for e2 in ev:
                self.wait(eng, e2)
            return
        name, val = ev
        if val <= 0:
            return
        assert val <= self.cnt.get(name, 0), \
            f"wait on future {name}={val}>{self.cnt.get(name, 0)}"
        if val <= self.hw[eng].get(name, 0):
            return
        self.hw[eng][name] = val
        s = self.sem(name)
        self.items[eng].append(lambda e, s=s, v=val: e.wait_ge(s, v))

    def op(self, eng, fn, inc=None, inc_by=1, waits=()):
        for ev in waits:
            self.wait(eng, ev)
        ev = None
        if inc is not None:
            s = self.sem(inc)
            self.cnt[inc] += inc_by
            ev = (inc, self.cnt[inc])

            def wrapped(e, fn=fn, s=s, by=inc_by):
                fn(e).then_inc(s, by)
            self.items[eng].append(wrapped)
        else:
            self.items[eng].append(fn)
        return ev

    def ev(self, name):
        return (name, self.cnt.get(name, 0))

    def evs_prefix(self, prefix):
        return [(n, c) for n, c in self.cnt.items() if n.startswith(prefix) and c]

    def replay(self, block):
        for eng in ENGS:
            seq = self.items[eng]

            def runner(e, seq=seq):
                for f in seq:
                    f(e)
            getattr(block, eng)(runner)


ECN = {"tensor": "sT", "vector": "sV", "scalar": "sA", "gpsimd": "sP"}


def build_program(stage=STAGE, n_inner=N_INNER, taps=None):
    import contextlib
    nc = bass.Bass("TRN2", target_bir_lowering=False)
    nc._stack = contextlib.ExitStack()
    st = nc._stack

    dx = nc.dram_tensor("x", [T, D], F32, kind="ExternalInput").ap()
    dy = nc.dram_tensor("y", [T, D], F32, kind="ExternalOutput").ap()
    dWc_down = nc.dram_tensor("Wc_down", [D, LAT], F32, kind="ExternalInput").ap()
    dWq_up = nc.dram_tensor("Wq_up", [LAT, D], F32, kind="ExternalInput").ap()
    dWkv_up = nc.dram_tensor("Wkv_up", [LAT, 128], F32, kind="ExternalInput").ap()
    dWc_proj = nc.dram_tensor("Wc_proj", [D, D], F32, kind="ExternalInput").ap()
    dWqkv = nc.dram_tensor("Wqkv", [DEPTH, D, 3 * D], F32, kind="ExternalInput").ap()
    dqs = nc.dram_tensor("q_scale", [DEPTH, HD], F32, kind="ExternalInput").ap()
    dks = nc.dram_tensor("k_scale", [DEPTH, HD], F32, kind="ExternalInput").ap()
    dWo = nc.dram_tensor("Wo", [DEPTH, D, D], F32, kind="ExternalInput").ap()
    dlng = nc.dram_tensor("ln_g", [DEPTH, D], F32, kind="ExternalInput").ap()
    dlnb = nc.dram_tensor("ln_b", [DEPTH, D], F32, kind="ExternalInput").ap()
    dW1 = nc.dram_tensor("W1", [DEPTH, D, 2 * DI], F32, kind="ExternalInput").ap()
    db1 = nc.dram_tensor("b1", [DEPTH, 2 * DI], F32, kind="ExternalInput").ap()
    dW2 = nc.dram_tensor("W2", [DEPTH, DI, D], F32, kind="ExternalInput").ap()
    db2 = nc.dram_tensor("b2", [DEPTH, D], F32, kind="ExternalInput").ap()
    dx1 = nc.dram_tensor("x1s", [128, NCH * T], F32, kind="Internal").ap()

    xT = st.enter_context(nc.sbuf_tensor("xT", [128, NCH * T], F32))
    XS = st.enter_context(nc.sbuf_tensor("XS", [128, NCH * T], BF16))
    AO = st.enter_context(nc.sbuf_tensor("AO", [128, NCH * T], BF16))
    TAB = st.enter_context(nc.sbuf_tensor("TAB", [128, 4 * T], BF16))
    CH = st.enter_context(nc.sbuf_tensor("CH", [128, 3 * 1024], F32))
    RN2 = st.enter_context(nc.sbuf_tensor("RN2", [2, T], F32))
    AR = st.enter_context(nc.sbuf_tensor("AR", [128, 10112], BF16))
    WSL = st.enter_context(nc.sbuf_tensor("WSL", [128, 6144], BF16))
    WIL = st.enter_context(nc.sbuf_tensor("WIL", [128, 2048], BF16))
    IDN = st.enter_context(nc.sbuf_tensor("IDN", [128, 128], F32))
    IDNB = st.enter_context(nc.sbuf_tensor("IDNB", [128, 128], BF16))
    DG = st.enter_context(nc.sbuf_tensor("DG", [128, 256], BF16))
    TRIL = st.enter_context(nc.sbuf_tensor("TRIL", [128, 128], F32))
    TRIU = st.enter_context(nc.sbuf_tensor("TRIU", [128, 128], F32))
    ONES = st.enter_context(nc.sbuf_tensor("ONES", [128, 128], BF16))
    SELQK = st.enter_context(nc.sbuf_tensor("SELQK", [2, 128], F32))
    HSEL = st.enter_context(nc.sbuf_tensor("HSEL", [128, 2], BF16))
    BIAS = st.enter_context(nc.sbuf_tensor("BIAS", [128, 512], F32))
    MISC = st.enter_context(nc.sbuf_tensor("MISC", [128, 32], F32))
    BROW = st.enter_context(nc.sbuf_tensor("BROW", [44, 128], F32))

    PS = st.enter_context(nc.psum_tensor("PS", [128, 4096], F32))

    p = P(nc)

    def bias_col(L, name, idx=0):
        off = {"qs": 0, "ks": 1, "lng": 2, "lnb": 10, "b2": 18, "b1a": 26,
               "b1g": 48}[name]
        c = L * 80 + off + idx
        return BIAS[:, c:c + 1]

    NEGM = MISC[:, 0:1]
    NEGM2 = MISC[:, 1:2]
    LCOL = MISC[:, 2:3]
    LINV = MISC[:, 3:4]
    IOTAC = MISC[:, 4:5]
    INVF = MISC[:, 5:6]
    LNBS = MISC[:, 6:7]
    SGN = MISC[:, 7:8]
    C0 = MISC[:, 8:9]
    C1 = MISC[:, 9:10]
    EPS5 = MISC[:, 10:11]
    EPS30 = MISC[:, 11:12]

    KW, KX = 4, 6
    dma_ring = {"W": [0, [0] * KW], "X": [0, [0] * KX]}

    def _dma(queue_key, eng, fn, waits):
        idx, counts = dma_ring[queue_key]
        K = len(counts)
        s = idx % K
        name = f"s{queue_key}D{s}"
        p.wait(eng, (name, 16 * counts[s]) if counts[s] else None)
        for ev in waits:
            p.wait(eng, ev)
        counts[s] += 1
        dma_ring[queue_key][0] = idx + 1
        return p.op(eng, fn, inc=name, inc_by=16)

    def dma_w(dst, src, waits=()):
        return _dma("W", "gpsimd", lambda e, d=dst, s=src: e.dma_start(d, s), waits)

    def dma_s(dst, src, waits=(), transpose=False, slow=False):
        if transpose:
            return _dma("X", "sync",
                        lambda e, d=dst, s=src: e.dma_start_transpose(d, s), waits)
        if slow:
            return _dma("X", "sync", lambda e, d=dst, s=src:
                        e.dma_start(d, s, allow_slow_non_contiguous=True), waits)
        return _dma("X", "sync", lambda e, d=dst, s=src: e.dma_start(d, s), waits)

    def eop(eng, fn, waits=(), inc=True):
        return p.op(eng, fn, inc=ECN[eng] if inc else None, waits=waits)

    def mm(out, lhsT, rhs, start, stop, waits=(), inc=False):
        return eop("tensor",
                   lambda e, o=out, l=lhsT, r=rhs, s0=start, s1=stop:
                   e.matmul(o, l, r, start=s0, stop=s1),
                   waits=waits, inc=inc)

    def vfence():
        return eop("vector", lambda e: e.tensor_copy(MISC[:, 13:14],
                                                     MISC[:, 12:13]))

    def pfence():
        return eop("gpsimd", lambda e: e.memset(MISC[:, 14:15], 0.0))

    def tsc(eng, out, in0, s1, s2, op0, op1=None, waits=(), inc=True):
        if s2 is None:
            return eop(eng, lambda e, o=out, i=in0, a=s1, oo=op0:
                       e.tensor_scalar(o, i, a, None, oo), waits=waits, inc=inc)
        return eop(eng, lambda e, o=out, i=in0, a=s1, b=s2, oo=op0, o1=op1:
                   e.tensor_scalar(o, i, a, b, oo, o1), waits=waits, inc=inc)

    def slab_src(dram, row0, c0, w, nk, kstep=128, lay_off=0):
        Cst = dram.shape[-1]
        off = lay_off + row0 * Cst + c0
        return bass.AP(dram.tensor, dram.offset + off,
                       [[Cst, 128], [kstep * Cst, nk], [1, w]])

    g, v, a = "gpsimd", "vector", "scalar"

    ev = eop(g, lambda e: e.memset(IDN[:, :], 0.0))
    ev_idn = eop(g, lambda e: e.affine_select(out=IDN[:, :], in_=IDN[:, :],
                                              compare_op=ALU.not_equal, fill=1.0,
                                              base=0, pattern=[[-1, 128]],
                                              channel_multiplier=1), waits=[ev])
    eop(a, lambda e: e.activation(IDNB[:, :], IDN[:, :], AF.Copy),
        waits=[ev_idn])
    ev = eop(g, lambda e: e.memset(TRIL[:, :], 0.0))
    eop(g, lambda e: e.affine_select(out=TRIL[:, :], in_=TRIL[:, :],
                                     compare_op=ALU.is_ge, fill=NEG, base=0,
                                     pattern=[[-1, 128]], channel_multiplier=1),
        waits=[ev])
    ev = eop(g, lambda e: e.memset(TRIU[:, :], 0.0))
    eop(g, lambda e: e.affine_select(out=TRIU[:, :], in_=TRIU[:, :],
                                     compare_op=ALU.is_ge, fill=NEG, base=0,
                                     pattern=[[1, 128]], channel_multiplier=-1),
        waits=[ev])
    eop(g, lambda e: e.memset(ONES[:, :], 1.0), inc=False)
    eop(g, lambda e: e.memset(EPS5, 1e-5), inc=False)
    eop(g, lambda e: e.memset(EPS30, 1e-30), inc=False)
    eop(g, lambda e: e.memset(MISC[:, 12:13], 0.0), inc=False)
    ev = eop(g, lambda e: e.memset(SELQK[:, :], 1.0))
    ev = eop(g, lambda e: e.affine_select(out=SELQK[:, :], in_=SELQK[:, :],
                                          compare_op=ALU.is_ge, fill=0.0, base=63,
                                          pattern=[[-1, 128]],
                                          channel_multiplier=64), waits=[ev])
    eop(g, lambda e: e.affine_select(out=SELQK[:, :], in_=SELQK[:, :],
                                     compare_op=ALU.is_ge, fill=0.0, base=0,
                                     pattern=[[1, 128]], channel_multiplier=-64),
        waits=[ev])
    ev = eop(g, lambda e: e.memset(HSEL[:, :], 1.0))
    ev = eop(g, lambda e: e.affine_select(out=HSEL[:, :], in_=HSEL[:, :],
                                          compare_op=ALU.is_ge, fill=0.0, base=63,
                                          pattern=[[64, 2]],
                                          channel_multiplier=-1), waits=[ev])
    eop(g, lambda e: e.affine_select(out=HSEL[:, :], in_=HSEL[:, :],
                                     compare_op=ALU.is_ge, fill=0.0, base=64,
                                     pattern=[[-128, 2]], channel_multiplier=1),
        waits=[ev])
    eop(g, lambda e: e.iota(CH[:, 0:T], [[1, T]], channel_multiplier=0,
                            allow_small_or_imprecise_dtypes=True), inc=False)
    ev_g0 = eop(g, lambda e: e.iota(IOTAC, [[0, 1]], channel_multiplier=1,
                                    allow_small_or_imprecise_dtypes=True))

    ev = tsc(v, C0, IOTAC, 1.0 / 64.0, -0.499, ALU.mult, ALU.add, waits=[ev_g0])
    ev = tsc(v, C0, C0, MAGIC, MAGIC, ALU.add, ALU.subtract, waits=[ev])
    ev = tsc(v, C1, C0, -64.0, None, ALU.mult, waits=[ev])
    ev = eop(v, lambda e: e.tensor_add(C1, C1, IOTAC), waits=[ev])
    ev = tsc(v, C0, C1, 0.5, -0.25, ALU.mult, ALU.add, waits=[ev])
    ev = tsc(v, C0, C0, MAGIC, MAGIC, ALU.add, ALU.subtract, waits=[ev])
    ev = tsc(v, INVF, C0, -math.log(10000.0) / 32.0, None, ALU.mult, waits=[ev])
    ev_invf = eop(a, lambda e: e.activation(INVF, INVF, AF.Exp), waits=[ev])
    ev = tsc(v, C1, C0, 2.0 / 89.6, 25.6 / 89.6, ALU.mult, ALU.add, waits=[ev])
    ev_lnbs = eop(a, lambda e: e.activation(LNBS, C1, AF.Ln), waits=[ev])
    ev = tsc(v, C0, IOTAC, 0.5, -0.25, ALU.mult, ALU.add, waits=[ev])
    ev = tsc(v, C0, C0, MAGIC, MAGIC, ALU.add, ALU.subtract, waits=[ev])
    ev = tsc(v, C0, C0, -2.0, None, ALU.mult, waits=[ev])
    ev = eop(v, lambda e: e.tensor_add(C0, C0, IOTAC), waits=[ev])
    ev_sgn = tsc(v, SGN, C0, 2.0, -1.0, ALU.mult, ALU.add, waits=[ev])

    ev_trow = ev_g0
    p.wait("tensor", p.ev("sP"))

    CH1 = CH[:, 2048:3072]
    for h in range(2):
        A_ = PS[:, 2048:3072]
        B_ = PS[:, 3072:4096]
        A2_ = PS[:, 0:1024]
        trow_h = CH[:, h * 1024:(h + 1) * 1024]
        ev = tsc(v, A_, trow_h, INVF, None, ALU.mult, waits=[ev_trow, ev_invf])
        ev = tsc(v, B_, A_, 1.0 / (2 * PI), 0.25, ALU.mult, ALU.add, waits=[ev])
        ev = tsc(v, CH1, B_, MAGIC, MAGIC, ALU.add, ALU.subtract, waits=[ev])
        ev = eop(v, lambda e, A=A_, B=B_:
                 e.scalar_tensor_tensor(B, CH1, -2 * PI, A, ALU.mult, ALU.add),
                 waits=[ev])
        ev = tsc(v, B_, B_, PI / 2, None, ALU.add, waits=[ev])
        ev = tsc(v, B_, B_, -PI_LO, PI_LO, ALU.max, ALU.min, waits=[ev])
        ev_c = eop(a, lambda e, B=B_: e.activation(B, B, AF.Sin), waits=[ev])
        ev_cc = eop(v, lambda e, B=B_: e.tensor_copy(CH1, B), waits=[ev_c])
        ev = tsc(v, B_, trow_h, 1.0 / 128.0, -8.0, ALU.mult, ALU.add,
                 waits=[ev_cc, ev_c])
        ev = tsc(v, B_, B_, LNBS, None, ALU.mult, waits=[ev, ev_lnbs])
        ev_s = eop(a, lambda e, B=B_: e.activation(B, B, AF.Exp), waits=[ev])
        ev1 = eop(v, lambda e, B=B_, o=TAB[:, 0 * T + h * 1024: 0 * T + (h + 1) * 1024]:
                  e.tensor_mul(o, CH1, B), waits=[ev_s, ev_cc])
        ev = tsc(v, A2_, trow_h, 1.0 / 128.0, -8.0, ALU.mult, ALU.add, waits=[ev1])
        ev = tsc(v, A2_, A2_, LNBS, None, ALU.mult, waits=[ev])
        ev_iv = eop(a, lambda e, A2=A2_: e.activation(A2, A2, AF.Exp, scale=-1.0),
                    waits=[ev])
        ev2 = eop(v, lambda e, A2=A2_, o=TAB[:, 2 * T + h * 1024: 2 * T + (h + 1) * 1024]:
                  e.tensor_mul(o, CH1, A2), waits=[ev_iv])
        ev = tsc(v, CH1, A_, 1.0 / (2 * PI), MAGIC, ALU.mult, ALU.add, waits=[ev2])
        ev = tsc(v, CH1, CH1, MAGIC, None, ALU.subtract, waits=[ev])
        ev = eop(v, lambda e, A=A_: e.scalar_tensor_tensor(A, CH1, -2 * PI, A,
                                                           ALU.mult, ALU.add),
                 waits=[ev])
        ev = tsc(v, A_, A_, -PI_LO, PI_LO, ALU.max, ALU.min, waits=[ev])
        ev_sn = eop(a, lambda e, A=A_: e.activation(A, A, AF.Sin), waits=[ev])
        ev_sc = tsc(v, CH1, A_, SGN, None, ALU.mult, waits=[ev_sn, ev_sgn])
        ev3 = eop(v, lambda e, B=B_, o=TAB[:, 1 * T + h * 1024: 1 * T + (h + 1) * 1024]:
                  e.tensor_mul(o, CH1, B), waits=[ev_sc])
        ev4 = eop(v, lambda e, A2=A2_, o=TAB[:, 3 * T + h * 1024: 3 * T + (h + 1) * 1024]:
                  e.tensor_mul(o, CH1, A2), waits=[ev3])
        ev_trow = ev4
    ev_tables = p.ev("sV")

    ev_bias_ms = eop(g, lambda e: e.memset(BIAS[:, :], 0.0))
    ev_q = None
    for L in range(DEPTH):
        for nm, dr in (("qs", dqs), ("ks", dks)):
            col = L * 80 + {"qs": 0, "ks": 1}[nm]
            for hq in range(2):
                for tq in range(2):
                    dst = bass.AP(BIAS, col + (64 * hq + tq) * 512, [[1024, 32]])
                    srcp = bass.AP(dr.tensor, dr.offset + L * HD + 32 * tq,
                                   [[1, 32]])
                    ev_q = dma_s(dst, srcp,
                                 waits=[ev_q] if ev_q else [ev_bias_ms])
    ev_qs_gather = ev_q

    def vec_load(src_ap, n_rows, dst_ap, prev_ev, n_cols=128):
        wl = [prev_ev] if prev_ev else []
        ev_d = dma_s(BROW[0:n_rows, 0:n_cols], src_ap, waits=wl)
        ev_m = eop("tensor", lambda e, n=n_rows, m=n_cols:
                   e.transpose(PS[0:m, 1024:1024 + n], BROW[0:n, 0:m],
                               IDN[0:n, 0:n]),
                   waits=[ev_d], inc=True)
        return eop(v, lambda e, d=dst_ap, n=n_rows, m=n_cols:
                   e.tensor_copy(d, PS[0:m, 1024:1024 + n]), waits=[ev_m])

    ev_b = None
    for L in range(DEPTH):
        ev_b = vec_load(bass.AP(dlng.tensor, dlng.offset + L * D, [[128, 8], [1, 128]]),
                        8, BIAS[:, L * 80 + 2: L * 80 + 10], ev_b)
        ev_b = vec_load(bass.AP(dlnb.tensor, dlnb.offset + L * D, [[128, 8], [1, 128]]),
                        8, BIAS[:, L * 80 + 10: L * 80 + 18], ev_b)
        ev_b = vec_load(bass.AP(db2.tensor, db2.offset + L * D, [[128, 8], [1, 128]]),
                        8, BIAS[:, L * 80 + 18: L * 80 + 26], ev_b)
        ev_b = vec_load(bass.AP(db1.tensor, db1.offset + L * 2 * DI, [[128, 22], [1, 128]]),
                        22, BIAS[:, L * 80 + 26: L * 80 + 48], ev_b)
        ev_b = vec_load(bass.AP(db1.tensor, db1.offset + L * 2 * DI + DI,
                                [[128, 21], [1, 128]]),
                        21, BIAS[:, L * 80 + 48: L * 80 + 69], ev_b)
        ev_b = vec_load(bass.AP(db1.tensor, db1.offset + L * 2 * DI + DI + 21 * 128,
                                [[42, 1], [1, 42]]),
                        1, BIAS[0:42, L * 80 + 69: L * 80 + 70], ev_b, n_cols=42)
    ev_bias = ev_b

    ev_in_prev = [None, None]
    ev_mm = None
    for tt in range(16):
        pg = tt % 2
        stg = CH[:, pg * 1024:(pg + 1) * 1024]
        wl = []
        if ev_in_prev[pg] is not None:
            wl.append(ev_in_prev[pg])
        if tt < 2:
            wl += [ev_tables, ev_bias]
        ev_d = dma_s(stg, dx[tt * 128:(tt + 1) * 128, :], waits=wl)
        pb = PS[:, 2048 + pg * 1024: 2048 + (pg + 1) * 1024]
        for c in range(NCH):
            w = [ev_d] if c == 0 else []
            ev_mm = eop("tensor",
                        lambda e, c=c, stg=stg, pb=pb:
                        e.transpose(pb[:, c * 128:(c + 1) * 128],
                                    stg[:, c * 128:(c + 1) * 128], IDN[:, :]),
                        waits=w, inc=(c == NCH - 1))
        dst = bass.AP(xT, tt * 128, [[NCH * T, 128], [T, NCH], [1, 128]])
        ev_in_prev[pg] = eop(v, lambda e, d=dst, pb=pb: e.tensor_copy(d, pb),
                             waits=[ev_mm])
    ev_xin = p.ev("sV")


    def emit_xs(shift, waits=()):
        last = None
        wl = list(waits)
        for c in range(NCH):
            src = xT[:, c * T:(c + 1) * T]
            dst = XS[:, c * T:(c + 1) * T]
            if shift and c >= 4:
                last = eop(g, lambda e, d=dst, s=src:
                           e.tensor_copy(d[:, 1:T], s[:, 0:T - 1]), waits=wl)
                last = eop(g, lambda e, d=dst: e.memset(d[:, 0:1], 0.0))
            else:
                last = eop(g, lambda e, d=dst, s=src: e.tensor_copy(d, s), waits=wl)
            wl = []
        return last

    def emit_ln(L, fp32_io=False, save_evs=None):
        last = None
        p.wait("tensor", p.ev("sV"))
        p.wait("tensor", p.ev("sA"))
        p.wait("vector", p.ev("sV"))
        ev_sqmm = [None, None]
        ev_s1_read = None
        ev_s2_read = None
        for h in range(2):
            S1h = PS[:, 0:1024]
            S2h = PS[:, 1024:2048]
            MUB = PS[:, 2048:3072]
            WKB = PS[:, 3072:4096]
            ev_mm2 = None
            for sl in range(2):
                ssl = slice(sl * 512, (sl + 1) * 512)
                for c in range(NCH):
                    xs_c = XS[:, c * T + h * 1024: c * T + (h + 1) * 1024]
                    mm(S1h[:, ssl], ONES[:, :], xs_c[:, ssl], c == 0, c == NCH - 1,
                       inc=(c == NCH - 1),
                       waits=[ev_s1_read] if (c == 0 and sl == 0 and
                                              ev_s1_read) else ())
            for sl in range(2):
                ssl = slice(sl * 512, (sl + 1) * 512)
                for c in range(NCH):
                    x2 = AR[:, (c % 2) * 512:(c % 2) * 512 + 512]
                    xs_c = XS[:, c * T + h * 1024: c * T + (h + 1) * 1024]
                    wsq = []
                    if ev_sqmm[c % 2] is not None:
                        wsq.append(ev_sqmm[c % 2])
                    ev_sq = eop(a, lambda e, i=xs_c[:, ssl], o=x2:
                                e.activation(o, i, AF.Square), waits=wsq)
                    wm2 = [ev_sq]
                    if c == 0 and sl == 0 and ev_s2_read:
                        wm2.append(ev_s2_read)
                    ev_mm2 = mm(S2h[:, ssl], ONES[:, :], x2,
                                c == 0, c == NCH - 1, waits=wm2, inc=True)
                    ev_sqmm[c % 2] = ev_mm2
            ev_s1_read = tsc(v, MUB, S1h, 1.0 / D, None, ALU.mult, waits=[ev_mm2])
            p.wait("vector", ev_s1_read)
            ev_cp0 = eop(v, lambda e: e.tensor_copy(CH[:, 0:1024], MUB))
            if TAP_SELECT is not None and L is not None:
                eop(v, lambda e, h=h: e.tensor_copy(
                    AR[:, 6144 + h * 1024: 6144 + (h + 1) * 1024], CH[:, 0:1024]))
                if h == 0:
                    eop(v, lambda e: e.tensor_copy(AR[:, 8192:9216], S1h))
                    eop(v, lambda e: e.tensor_copy(AR[:, 9216:10112],
                                                   XS[:, 0:896]))
            ev_s2_read = tsc(v, CH[:, 1024:2048], S2h, 1.0 / D, None, ALU.mult)
            ev_mu2 = eop(v, lambda e: e.tensor_mul(MUB, CH[:, 0:1024],
                                                   CH[:, 0:1024]),
                         waits=[ev_cp0, ev_s2_read])
            p.wait("vector", ev_mu2)
            ev5 = eop(v, lambda e: e.scalar_tensor_tensor(
                CH[:, 1024:2048], MUB, -1.0, CH[:, 1024:2048], ALU.mult, ALU.add))
            ev6 = eop(a, lambda e: e.activation(WKB, CH[:, 1024:2048], AF.Ln,
                                                bias=EPS5), waits=[ev5])
            p.wait("scalar", ev6)
            ev7 = eop(a, lambda e: e.activation(WKB, WKB, AF.Exp, scale=-0.5))
            ev8 = eop(v, lambda e: e.tensor_copy(CH[:, 2048:3072], WKB),
                      waits=[ev7])
            ev9 = eop(v, lambda e: e.scalar_tensor_tensor(
                CH[:, 1024:2048], CH[:, 0:1024], -1.0, CH[:, 2048:3072],
                ALU.mult, ALU.mult), waits=[ev8])
            for c in range(NCH):
                xs_c = XS[:, c * T + h * 1024: c * T + (h + 1) * 1024]
                if fp32_io:
                    xs_c = xT[:, c * T + h * 1024: c * T + (h + 1) * 1024]
                t1 = MUB if c % 2 == 0 else WKB
                ev10 = eop(v, lambda e, t1=t1, i=xs_c:
                           e.tensor_mul(t1, i, CH[:, 2048:3072]), waits=[ev9])
                p.wait("vector", ev10)
                ev11 = eop(v, lambda e, t1=t1: e.tensor_add(t1, t1, CH[:, 1024:2048]))
                p.wait("vector", ev11)
                if L is None:
                    wcp = [save_evs[c]] if save_evs is not None else ()
                    last = eop(v, lambda e, t1=t1, o=xs_c: e.tensor_copy(o, t1),
                               waits=wcp)
                else:
                    last = eop(v, lambda e, t1=t1, o=xs_c, gc=bias_col(L, "lng", c),
                               bc=bias_col(L, "lnb", c):
                               e.tensor_scalar(o, t1, gc, bc, ALU.mult, ALU.add))
        return last

    wsl_used = [None] * 6

    def load_slab(slot_i, src, ncols=1024, dma_cols=None, extra_waits=()):
        wl = list(extra_waits)
        if wsl_used[slot_i] is not None:
            wl.append(wsl_used[slot_i])
        slot = WSL[:, slot_i * 1024: slot_i * 1024 + ncols]
        dst = slot if dma_cols is None else slot[:, 0:dma_cols]
        return slot, dma_w(dst, src, waits=wl)

    def emit_mla():
        ev_xs = emit_xs(shift=False, waits=[ev_xin])
        p.wait("scalar", ev_xs)
        p.wait("tensor", ev_xs)
        p.wait("vector", ev_xs)
        ev_ln = emit_ln(None)
        p.wait("tensor", ev_ln)

        latent = AO
        ev_prev = [None, None]
        ev_lat_mm = None
        for n in range(4):
            slot, ev_w = load_slab(n % 4, slab_src(dWc_down, 0, n * 128, 128, 8))
            for q4 in range(4):
                pg = q4 % 2
                pb = PS[:, pg * 512:(pg + 1) * 512]
                w = [ev_w]
                if ev_prev[pg] is not None:
                    w.append(ev_prev[pg])
                for k in range(NCH):
                    ev_lat_mm = mm(pb, slot[:, k * 128:(k + 1) * 128],
                                   XS[:, k * T + q4 * 512: k * T + (q4 + 1) * 512],
                                   k == 0, k == NCH - 1,
                                   waits=w if k == 0 else (), inc=(k == NCH - 1))
                ev_prev[pg] = eop(a, lambda e,
                                  o=latent[:, n * T + q4 * 512: n * T + (q4 + 1) * 512],
                                  i=pb: e.activation(o, i, AF.Copy), waits=[ev_lat_mm])
            wsl_used[n % 4] = ev_lat_mm
        ev_latent = p.ev("sA")

        kvbuf = AR[:, 0:2048]
        vstage_m = AR[:, 5120:7168]
        slot, ev_w = load_slab(4, slab_src(dWkv_up, 0, 0, 128, 4), ncols=512)
        wlB = [wsl_used[5]] if wsl_used[5] is not None else []
        slotB = WSL[:, 5 * 1024: 5 * 1024 + 512]
        dstB1 = bass.AP(WSL, 5 * 1024 + 0, [[6144, 128], [128, 4], [1, 64]])
        dstB2 = bass.AP(WSL, 5 * 1024 + 64, [[6144, 128], [128, 4], [1, 64]])
        ev_wB1 = dma_w(dstB1, slab_src(dWkv_up, 0, 64, 64, 4), waits=wlB)
        ev_wB = dma_w(dstB2, slab_src(dWkv_up, 0, 0, 64, 4))
        ev_kv_mm = None
        for q4 in range(4):
            pg = q4 % 2
            pb = PS[:, pg * 512:(pg + 1) * 512]
            w = [ev_w, ev_latent]
            if ev_prev[pg] is not None:
                w.append(ev_prev[pg])
            for k in range(4):
                ev_kv_mm = mm(pb, slot[:, k * 128:(k + 1) * 128],
                              latent[:, k * T + q4 * 512: k * T + (q4 + 1) * 512],
                              k == 0, k == 3, waits=w if k == 0 else (), inc=(k == 3))
            eop(a, lambda e, o=kvbuf[0:64, q4 * 512:(q4 + 1) * 512],
                i=pb[0:64, :]: e.activation(o, i, AF.Copy), waits=[ev_kv_mm])
            ev_prev[pg] = eop(a, lambda e,
                              o=vstage_m[64:128, q4 * 512:(q4 + 1) * 512],
                              i=pb[64:128, :]: e.activation(o, i, AF.Copy))
        wsl_used[4] = ev_kv_mm
        for q4 in range(4):
            pg = q4 % 2
            pb = PS[:, pg * 512:(pg + 1) * 512]
            w = [ev_wB, ev_wB1]
            if ev_prev[pg] is not None:
                w.append(ev_prev[pg])
            for k in range(4):
                ev_kv_mm = mm(pb, slotB[:, k * 128:(k + 1) * 128],
                              latent[:, k * T + q4 * 512: k * T + (q4 + 1) * 512],
                              k == 0, k == 3, waits=w if k == 0 else (), inc=(k == 3))
            ev_prev[pg] = eop(a, lambda e,
                              o=kvbuf[64:128, q4 * 512:(q4 + 1) * 512],
                              i=pb[64:128, :]: e.activation(o, i, AF.Copy),
                              waits=[ev_kv_mm])
        wsl_used[5] = ev_kv_mm
        ev_kv = p.ev("sA")
        ev_kdup = ev_kv

        Vtok = AR[:, 4096:5120]
        ev_vt = None
        for tt in range(16):
            ev_vt = dma_s(Vtok[:, tt * 64:(tt + 1) * 64],
                          vstage_m[64:128, tt * 128:(tt + 1) * 128],
                          waits=[ev_kv] if tt == 0 else (), transpose=True)
        ev_vtok = p.evs_prefix("sXD")

        qhat = AR[:, 2048:4096]
        Pbuf = [AR[:, 5120:7168], AR[:, 7168:9216]]
        PT = AR[:, 9216:9984]
        SB = PS[:, 0:2048]
        ev_p_guard0 = ev_vtok

        ev_q_used = None
        ev_exp_prev = None
        ev_p_used = [None, None]
        ev_av_evac = [None, None]
        pt_slot_ev = [None] * 6
        pt_i = 0
        ev_qp_prev = [None, None]
        ev_sim_last = None
        ev_avmm = None
        tps_ev = [None, None]
        dg_used = [p.ev("sT"), p.ev("sT")]

        for c in range(NCH):
            slot, ev_w = load_slab(c % 2, slab_src(dWq_up, 0, c * 128, 128, 4),
                                   ncols=512)
            ev_evq = None
            for q4 in range(4):
                pg = q4 % 2
                pb = PS[:, 2048 + pg * 512: 2048 + (pg + 1) * 512]
                w = [ev_w]
                if ev_qp_prev[pg] is not None:
                    w.append(ev_qp_prev[pg])
                if tps_ev[pg] is not None:
                    w.append(tps_ev[pg])
                ev_mmq = None
                for k in range(4):
                    ev_mmq = mm(pb, slot[:, k * 128:(k + 1) * 128],
                                latent[:, k * T + q4 * 512: k * T + (q4 + 1) * 512],
                                k == 0, k == 3, waits=w if k == 0 else (), inc=(k == 3))
                w2 = [ev_mmq]
                if ev_q_used is not None and q4 == 0:
                    w2.append(ev_q_used)
                ev_evq = eop(a, lambda e, o=qhat[:, q4 * 512:(q4 + 1) * 512], i=pb:
                             e.activation(o, i, AF.Copy), waits=w2)
                ev_qp_prev[pg] = ev_evq
            wsl_used[c % 2] = ev_mmq
            ev_qhat = ev_evq

            for hh in range(2):
                qrow = qhat[hh * 64:(hh + 1) * 64, :]
                for qt in range(16):
                    srange = 128 * (qt + 1)
                    nchunks = cdiv(srange, 512)
                    w = [ev_qhat, ev_kv, ev_kdup]
                    if ev_exp_prev is not None:
                        w.append(ev_exp_prev)
                    ev_sim = None
                    for ci in range(nchunks):
                        n0 = ci * 512
                        n1 = min(srange, n0 + 512)
                        ev_sim = mm(SB[:, n0:n1], qrow[:, qt * 128:(qt + 1) * 128],
                                    kvbuf[hh * 64:hh * 64 + 64, n0:n1], True, True,
                                    waits=w if ci == 0 else (),
                                    inc=(ci == nchunks - 1))
                    ev_msk = eop(v, lambda e, o=SB[:, (srange - 128):srange]:
                                 e.tensor_add(o, o, TRIL[:, :]), waits=[ev_sim])
                    ev_rmx = eop(v, lambda e, i=SB[:, 0:srange]:
                                 e.tensor_reduce(NEGM, i, AX.X, ALU.max,
                                                 negate=True), waits=[ev_msk])
                    p.wait("vector", ev_rmx)
                    ev_bb = tsc(v, NEGM2, NEGM, MLA_SCALE, None, ALU.mult)
                    pg = qt % 2
                    Pq = Pbuf[pg]
                    w = [ev_bb]
                    if pg == 0:
                        w.append(ev_p_guard0)
                    if ev_p_used[pg] is not None:
                        w.append(ev_p_used[pg])
                    ev_exp = eop(a, lambda e, i=SB[:, 0:srange], o=Pq[:, 0:srange]:
                                 e.activation(o, i, AF.Exp, bias=NEGM2,
                                              scale=MLA_SCALE, accum_out=LCOL),
                                 waits=w)
                    ev_exp_prev = ev_exp
                    ev_rcp = eop(v, lambda e: e.reciprocal(LINV, LCOL),
                                 waits=[ev_exp])
                    p.wait("vector", ev_rcp)
                    wdg = []
                    if dg_used[pg] is not None:
                        wdg.append(dg_used[pg])
                    ev_dg = tsc(v, DG[:, pg * 128:(pg + 1) * 128], IDNB[:, :],
                                LINV, None, ALU.mult, waits=wdg)
                    avb = PS[:, 3072 + pg * 512: 3072 + pg * 512 + 512]
                    av = avb[0:64, 0:128]

                    def emit_av(ci, sidx_e, ev_e):
                        nonlocal ev_avmm
                        w2 = [ev_e, ev_vtok]
                        if ci == 0 and ev_av_evac[pg] is not None:
                            w2.append(ev_av_evac[pg])
                        ev_avmm = mm(av, Vtok[:, ci * 64:(ci + 1) * 64],
                                     PT[:, sidx_e * 128:(sidx_e + 1) * 128],
                                     ci == 0, ci == qt, waits=w2, inc=True)
                        pt_slot_ev[sidx_e] = ev_avmm

                    slots = []
                    for ci in range(qt + 1):
                        sidx = pt_i % 6
                        tb = pt_i % 2
                        pt_i += 1
                        tps = PS[:, 2048 + tb * 512: 2048 + tb * 512 + 128]
                        wt = [ev_dg]
                        if tps_ev[tb] is not None:
                            wt.append(tps_ev[tb])
                        ev_t1 = mm(tps, Pq[:, ci * 128:(ci + 1) * 128],
                                   DG[:, pg * 128:(pg + 1) * 128],
                                   True, True, waits=wt, inc=True)
                        we = [ev_t1]
                        if pt_slot_ev[sidx] is not None:
                            we.append(pt_slot_ev[sidx])
                        ev_e = eop(a, lambda e, d=PT[:, sidx * 128:(sidx + 1) * 128],
                                   s=tps: e.activation(d, s, AF.Copy),
                                   waits=we, inc=True)
                        tps_ev[tb] = ev_e
                        slots.append((ci, sidx, ev_e))
                        if ci >= 1:
                            emit_av(*slots[ci - 1])
                    emit_av(*slots[qt])
                    ev_p_used[pg] = p.ev("sT")
                    dg_used[pg] = p.ev("sT")
                    dst = XS[hh * 64:(hh + 1) * 64,
                             c * T + qt * 128: c * T + (qt + 1) * 128]
                    ev_av_evac[pg] = eop(a, lambda e, d=dst, s=av:
                                         e.activation(d, s, AF.Copy), waits=[ev_avmm])
            ev_q_used = p.ev("sT")
        ev_att = p.ev("sA")

        ev_prev = [None, None]
        ev_pj_mm = None
        for n in range(NCH):
            slot, ev_w = load_slab(n % 4, slab_src(dWc_proj, 0, n * 128, 128, 8))
            for q4 in range(4):
                pg = q4 % 2
                pb = PS[:, pg * 512:(pg + 1) * 512]
                w = [ev_w, ev_att]
                if ev_prev[pg] is not None:
                    w.append(ev_prev[pg])
                for k in range(NCH):
                    ev_pj_mm = mm(pb, slot[:, k * 128:(k + 1) * 128],
                                  XS[:, k * T + q4 * 512: k * T + (q4 + 1) * 512],
                                  k == 0, k == NCH - 1,
                                  waits=w if k == 0 else (), inc=(k == NCH - 1))
                ev_prev[pg] = eop(v, lambda e,
                                  o=xT[:, n * T + q4 * 512: n * T + (q4 + 1) * 512],
                                  i=pb:
                                  e.scalar_tensor_tensor(o, i, 1.0, o, ALU.mult,
                                                         ALU.add),
                                  waits=[ev_pj_mm])
            wsl_used[n % 4] = ev_pj_mm
        return p.ev("sV"), p.ev("sT")

    def emit_attn(L, ev_ready_v, ev_ready_t):
        p.wait("gpsimd", ev_ready_v)
        p.wait("gpsimd", ev_ready_t)
        ev_xs = emit_xs(shift=True)
        p.wait("tensor", ev_xs)
        p.wait("scalar", ev_xs)
        p.wait("vector", ev_xs)
        p.wait("vector", ev_qs_gather)

        qhat = AR[:, 0:2048]
        khat = AR[:, 2048:4096]
        vstage = AR[:, 4096:5120]
        Vtok = AR[:, 5120:7168]
        PbufR = [AR[:, 7168:7680], AR[:, 7680:8192]]
        PT = AR[:, 8192:8960]
        X2B = AR[:, 8960:9984]

        ev_pp_prev = [ev_ready_v, ev_ready_v]
        ev_ss_prev = [None, None]
        ev_rn_prev = p.ev("sV")
        ev_bcb_prev = p.ev("sV")
        ev_exp_prev = [p.ev("sA"), p.ev("sA")]
        ev_p_used = [None, None]
        ev_av_evac = [p.ev("sA"), p.ev("sA")]
        pt_slot_ev = [None] * 6
        pt_i = 0
        tps_ev = [None, None]
        dg_used = [p.ev("sT"), p.ev("sT")]
        ev_vstage_used = [None, None]
        ev_vtok_used = None
        ev_wil_used = None
        ev_avmm = None
        ao_guard = [ev_ready_t]

        lay = L * D * 3 * D
        for c in range(NCH):
            sq, ev_wq = load_slab((3 * c) % 6,
                                  slab_src(dWqkv, 0, c * 128, 128, 8, lay_off=lay))
            sk, ev_wk = load_slab((3 * c + 1) % 6,
                                  slab_src(dWqkv, 0, D + c * 128, 128, 8, lay_off=lay))
            sv_, ev_wv = load_slab((3 * c + 2) % 6,
                                   slab_src(dWqkv, 0, 2 * D + c * 128, 128, 8,
                                            lay_off=lay))

            def il_src(slab):
                return bass.AP(slab.tensor, slab.offset,
                               [[6144, 128], [128, 8], [64, 2], [1, 32], [32, 2]])

            wq_il = WIL[:, 0:1024]
            wk_il = WIL[:, 1024:2048]
            wil_w = [ev_wq]
            if ev_wil_used is not None:
                wil_w.append(ev_wil_used)
            ev_ilq = eop(g, lambda e, d=wq_il, s=il_src(sq): e.tensor_copy(d, s),
                         waits=wil_w)
            ev_ilk = eop(g, lambda e, d=wk_il, s=il_src(sk): e.tensor_copy(d, s),
                         waits=[ev_wk])
            wsl_used[(3 * c) % 6] = ev_ilq
            wsl_used[(3 * c + 1) % 6] = ev_ilk

            def qk_chain(slab_il, ev_il, scale_col, tabc0, tabs0, out):
                nonlocal ev_rn_prev, ev_bcb_prev
                ev_out = None
                for h in range(2):
                    ev_ch_evac = [None, None]
                    for qq in range(2):
                        q4 = 2 * h + qq
                        pg = q4 % 2
                        pb = PS[:, pg * 512:(pg + 1) * 512]
                        w = [ev_il]
                        if ev_pp_prev[pg] is not None:
                            w.append(ev_pp_prev[pg])
                        if tps_ev[pg] is not None:
                            w.append(tps_ev[pg])
                        ev_mmq = None
                        for k in range(NCH):
                            ev_mmq = mm(pb, slab_il[:, k * 128:(k + 1) * 128],
                                        XS[:, k * T + q4 * 512:
                                           k * T + (q4 + 1) * 512],
                                        k == 0, k == NCH - 1,
                                        waits=w if k == 0 else (),
                                        inc=(k == NCH - 1))
                        x2s = X2B[:, qq * 512:(qq + 1) * 512]
                        wsq = [ev_mmq]
                        if ev_ss_prev[qq] is not None:
                            wsq.append(ev_ss_prev[qq])
                        ev_sqr = eop(a, lambda e, i=pb, o=x2s:
                                     e.activation(o, i, AF.Square), waits=wsq)
                        wss = [ev_sqr]
                        if ev_rn_prev is not None:
                            wss.append(ev_rn_prev)
                        ev_ss = mm(PS[0:2, 1024:1536], HSEL[:, :], x2s, True, True,
                                   waits=wss, inc=True)
                        ev_ss_prev[qq] = ev_ss
                        ev_rn_prev = eop(v, lambda e,
                                         o=RN2[:, h * 1024 + qq * 512:
                                               h * 1024 + (qq + 1) * 512],
                                         i=PS[0:2, 1024:1536]: e.tensor_copy(o, i),
                                         waits=[ev_ss])
                        ev_ch_evac[qq] = eop(v, lambda e,
                                             o=CH[:, qq * 512:(qq + 1) * 512], i=pb:
                                             e.tensor_scalar(o, i, scale_col, None,
                                                             ALU.mult))
                        ev_pp_prev[pg] = ev_ch_evac[qq]
                    ev_rs = eop(a, lambda e, sl=slice(h * 1024, (h + 1) * 1024):
                                e.activation(RN2[:, sl], RN2[:, sl], AF.Ln,
                                             bias=EPS30[0:2, :]),
                                waits=[ev_rn_prev])
                    p.wait("scalar", ev_rs)
                    ev_rs = eop(a, lambda e, sl=slice(h * 1024, (h + 1) * 1024):
                                e.activation(RN2[:, sl], RN2[:, sl], AF.Exp,
                                             scale=-0.5))
                    q1 = CH[:, 0:1024]
                    sgw = CH[:, 1024:2048]
                    u_ = CH[:, 2048:3072]
                    shmask = [m_ ^ 1 for m_ in range(32)]
                    ev_sh = eop(v, lambda e, s=shmask: e.stream_shuffle(sgw, q1, s),
                                waits=[ev_ch_evac[0], ev_ch_evac[1]])
                    eop(v, lambda e, tc=TAB[:, tabc0 + h * 1024:
                                            tabc0 + (h + 1) * 1024]:
                        e.tensor_mul(u_, q1, tc))
                    p.wait("vector", ev_sh)
                    ev_sg = eop(v, lambda e, ts_=TAB[:, tabs0 + h * 1024:
                                                     tabs0 + (h + 1) * 1024]:
                                e.tensor_mul(sgw, sgw, ts_))
                    p.wait("vector", ev_sg)
                    ev_e1 = eop(v, lambda e: e.tensor_add(u_, u_, sgw))
                    p.wait("vector", ev_e1)
                    for qq in range(2):
                        bcb = PS[:, 1536:2048]
                        wbc = [ev_rs, ev_e1]
                        if ev_bcb_prev is not None:
                            wbc.append(ev_bcb_prev)
                        ev_bc = mm(bcb, SELQK[:, :],
                                   RN2[:, h * 1024 + qq * 512:
                                       h * 1024 + (qq + 1) * 512],
                                   True, True, waits=wbc, inc=True)
                        ev_out = eop(v, lambda e,
                                     o=out[:, h * 1024 + qq * 512:
                                           h * 1024 + (qq + 1) * 512],
                                     i=u_[:, qq * 512:(qq + 1) * 512], b_=bcb:
                                     e.tensor_mul(o, i, b_), waits=[ev_bc])
                        ev_bcb_prev = ev_out
                return ev_out

            ev_qh = qk_chain(wq_il, ev_ilq, bias_col(L, "qs"), 0 * T, 1 * T, qhat)
            ev_kh = qk_chain(wk_il, ev_ilk, bias_col(L, "ks"), 2 * T, 3 * T, khat)
            ev_wil_used = p.ev("sT")

            ev_vt = None
            for q4 in range(4):
                pg = q4 % 2
                pb = PS[:, pg * 512:(pg + 1) * 512]
                w = [ev_wv]
                if ev_pp_prev[pg] is not None:
                    w.append(ev_pp_prev[pg])
                if tps_ev[pg] is not None:
                    w.append(tps_ev[pg])
                ev_mmv = None
                for k in range(NCH):
                    ev_mmv = mm(pb, sv_[:, k * 128:(k + 1) * 128],
                                XS[:, k * T + q4 * 512: k * T + (q4 + 1) * 512],
                                k == 0, k == NCH - 1,
                                waits=w if k == 0 else (), inc=(k == NCH - 1))
                vs = vstage[:, (q4 % 2) * 512:(q4 % 2 + 1) * 512]
                wev = [ev_mmv]
                if ev_vstage_used[q4 % 2] is not None:
                    wev.append(ev_vstage_used[q4 % 2])
                ev_ev = eop(a, lambda e, o=vs, i=pb: e.activation(o, i, AF.Copy),
                            waits=wev)
                ev_pp_prev[pg] = ev_ev
                evs_q4 = []
                for i in range(4):
                    tt = 4 * q4 + i
                    wtp = [ev_ev]
                    if tt == 0 and ev_vtok_used is not None:
                        wtp.append(ev_vtok_used)
                    ev_vt = dma_s(Vtok[:, tt * 128:(tt + 1) * 128],
                                  vs[:, i * 128:(i + 1) * 128],
                                  waits=wtp, transpose=True)
                    evs_q4.append(ev_vt)
                ev_vstage_used[q4 % 2] = evs_q4
            ev_vtok = p.evs_prefix("sXD")
            wsl_used[(3 * c + 2) % 6] = p.ev("sT")

            for hh in range(2):
                qrow = qhat[hh * 64:(hh + 1) * 64, :]
                krow = khat[hh * 64:(hh + 1) * 64, :]
                for qt in range(16):
                    s0 = max(0, qt * 128 - 256)
                    send = (qt + 1) * 128
                    sw = send - s0
                    pg = qt % 2
                    Spb = PS[:, 2048 + pg * 512: 2048 + pg * 512 + sw]
                    w = [ev_qh, ev_kh]
                    if ev_exp_prev[pg] is not None:
                        w.append(ev_exp_prev[pg])
                    ev_sim = mm(Spb, qrow[:, qt * 128:(qt + 1) * 128],
                                krow[:, s0:send], True, True, waits=w, inc=True)
                    ev_m = ev_sim
                    if qt >= 2:
                        ev_m = eop(v, lambda e, o=Spb[:, 0:128]:
                                   e.tensor_add(o, o, TRIU[:, :]), waits=[ev_sim])
                    ev_m = eop(v, lambda e, o=Spb[:, sw - 128:sw]:
                               e.tensor_add(o, o, TRIL[:, :]), waits=[ev_m])
                    ev_rmx = eop(v, lambda e, i=Spb:
                                 e.tensor_reduce(NEGM, i, AX.X, ALU.max,
                                                 negate=True), waits=[ev_m])
                    p.wait("vector", ev_rmx)
                    ev_bb = tsc(v, NEGM2, NEGM, QK_SCALE, None, ALU.mult)
                    Pq = PbufR[pg]
                    w = [ev_bb]
                    if ev_p_used[pg] is not None:
                        w.append(ev_p_used[pg])
                    ev_exp = eop(a, lambda e, i=Spb, o=Pq[:, 0:sw]:
                                 e.activation(o, i, AF.Exp, bias=NEGM2,
                                              scale=QK_SCALE, accum_out=LCOL),
                                 waits=w)
                    ev_exp_prev[pg] = ev_exp
                    ev_rcp = eop(v, lambda e: e.reciprocal(LINV, LCOL),
                                 waits=[ev_exp])
                    p.wait("vector", ev_rcp)
                    wdg = []
                    if dg_used[pg] is not None:
                        wdg.append(dg_used[pg])
                    ev_dg = tsc(v, DG[:, pg * 128:(pg + 1) * 128], IDNB[:, :],
                                LINV, None, ALU.mult, waits=wdg)
                    avb = PS[:, 3072 + pg * 512: 3072 + pg * 512 + 512]
                    av = avb[0:64, 0:128]
                    ncu = sw // 128

                    def emit_av(ci, sidx_e, ev_e):
                        nonlocal ev_avmm
                        stile = (s0 // 128) + ci
                        w2 = [ev_e, ev_vtok]
                        if ci == 0 and ev_av_evac[pg] is not None:
                            w2.append(ev_av_evac[pg])
                        ev_avmm = mm(av,
                                     Vtok[:, stile * 128 + hh * 64:
                                          stile * 128 + hh * 64 + 64],
                                     PT[:, sidx_e * 128:(sidx_e + 1) * 128],
                                     ci == 0, ci == ncu - 1, waits=w2,
                                     inc=True)
                        pt_slot_ev[sidx_e] = ev_avmm

                    slots = []
                    for ci in range(ncu):
                        sidx = pt_i % 6
                        tb = pt_i % 2
                        pt_i += 1
                        tps = PS[:, tb * 512: tb * 512 + 128]
                        wt = [ev_dg]
                        if tps_ev[tb] is not None:
                            wt.append(tps_ev[tb])
                        if ev_pp_prev[tb] is not None:
                            wt.append(ev_pp_prev[tb])
                        ev_t1 = mm(tps, Pq[:, ci * 128:(ci + 1) * 128],
                                   DG[:, pg * 128:(pg + 1) * 128],
                                   True, True, waits=wt, inc=True)
                        we = [ev_t1]
                        if pt_slot_ev[sidx] is not None:
                            we.append(pt_slot_ev[sidx])
                        ev_e = eop(a, lambda e, d=PT[:, sidx * 128:(sidx + 1) * 128],
                                   s=tps: e.activation(d, s, AF.Copy),
                                   waits=we, inc=True)
                        tps_ev[tb] = ev_e
                        slots.append((ci, sidx, ev_e))
                        if ci >= 1:
                            emit_av(*slots[ci - 1])
                    emit_av(*slots[ncu - 1])
                    ev_p_used[pg] = p.ev("sT")
                    dg_used[pg] = p.ev("sT")
                    dst = AO[hh * 64:(hh + 1) * 64,
                             c * T + qt * 128: c * T + (qt + 1) * 128]
                    w3 = [ev_avmm] + ao_guard
                    ao_guard = []
                    ev_av_evac[pg] = eop(a, lambda e, d=dst, s=av:
                                         e.activation(d, s, AF.Copy), waits=w3)
            ev_vtok_used = p.ev("sT")
        ev_att = p.ev("sA")

        ev_prev = [None, None]
        ev_pj = None
        for n in range(NCH):
            slot, ev_w = load_slab(n % 6, slab_src(dWo, 0, n * 128, 128, 8,
                                                   lay_off=L * D * D))
            for q4 in range(4):
                pg = q4 % 2
                pb = PS[:, pg * 512:(pg + 1) * 512]
                w = [ev_w, ev_att]
                if ev_prev[pg] is not None:
                    w.append(ev_prev[pg])
                for k in range(NCH):
                    ev_pj = mm(pb, slot[:, k * 128:(k + 1) * 128],
                               AO[:, k * T + q4 * 512: k * T + (q4 + 1) * 512],
                               k == 0, k == NCH - 1,
                               waits=w if k == 0 else (), inc=(k == NCH - 1))
                ev_prev[pg] = eop(v, lambda e,
                                  o=xT[:, n * T + q4 * 512: n * T + (q4 + 1) * 512],
                                  i=pb: e.scalar_tensor_tensor(o, i, 1.0, o,
                                                               ALU.mult, ALU.add),
                                  waits=[ev_pj])
            wsl_used[n % 6] = ev_pj
        return p.ev("sV"), p.ev("sT")

    def emit_ffn(L, ev_ready_v, ev_ready_t):
        p.wait("gpsimd", ev_ready_v)
        p.wait("gpsimd", ev_ready_t)
        ev_xs = emit_xs(shift=True)
        p.wait("scalar", ev_xs)
        p.wait("vector", ev_xs)
        p.wait("tensor", ev_xs)
        emit_ln(L)
        p.wait("tensor", p.ev("sA"))
        p.wait("tensor", p.ev("sV"))

        def g_tile(j):
            if j < 16:
                return AO[:, j * 1024:(j + 1) * 1024]
            return AR[:, (j - 16) * 1024:(j - 15) * 1024]

        GGs = [CH[:, 0:1024], CH[:, 1024:2048]]
        AB = CH[:, 2048:3072]
        ev_res_last = None
        for h in range(2):
            hsl0 = h * 1024
            ev_glu = [None] * 22
            ev_act_pair = [None] * 22
            ev_gelu = [None] * 22
            lay1 = L * D * 2 * DI
            for j in range(22):
                wj = 128 if j < 21 else 42
                sa_, ev_wa = load_slab((2 * j) % 6,
                                       slab_src(dW1, 0, j * 128, wj, 8, lay_off=lay1),
                                       ncols=8 * wj)
                sg_, ev_wg = load_slab((2 * j + 1) % 6,
                                       slab_src(dW1, 0, DI + j * 128, wj, 8,
                                                lay_off=lay1),
                                       ncols=8 * wj)
                apb = PS[0:wj, (j % 2) * 1024:(j % 2) * 1024 + 1024]
                gpb = PS[0:wj, 2048 + (j % 2) * 1024: 2048 + (j % 2) * 1024 + 1024]
                w = [ev_wa]
                if j >= 2 and ev_act_pair[j - 2] is not None:
                    w.append(ev_act_pair[j - 2])
                if j == 0:
                    w.append(ev_res_last if ev_res_last is not None else ev_ready_v)
                    w.append(p.ev("sA"))
                ev_mma = None
                for k in range(NCH):
                    for sl in range(2):
                        ev_mma = mm(apb[:, sl * 512:(sl + 1) * 512],
                                    sa_[:, k * wj:(k + 1) * wj],
                                    XS[:, k * T + hsl0 + sl * 512:
                                       k * T + hsl0 + (sl + 1) * 512],
                                    k == 0, k == NCH - 1,
                                    waits=w if (k == 0 and sl == 0) else (),
                                    inc=(k == NCH - 1 and sl == 1))
                wgm = [ev_wg]
                if j >= 2 and ev_gelu[j - 2] is not None:
                    wgm.append(ev_gelu[j - 2])
                if j == 0:
                    wgm.append(p.ev("sA"))
                ev_mmg = None
                for k in range(NCH):
                    for sl in range(2):
                        ev_mmg = mm(gpb[:, sl * 512:(sl + 1) * 512],
                                    sg_[:, k * wj:(k + 1) * wj],
                                    XS[:, k * T + hsl0 + sl * 512:
                                       k * T + hsl0 + (sl + 1) * 512],
                                    k == 0, k == NCH - 1,
                                    waits=wgm if (k == 0 and sl == 0) else (),
                                    inc=(k == NCH - 1 and sl == 1))
                wsl_used[(2 * j) % 6] = ev_mma
                wsl_used[(2 * j + 1) % 6] = ev_mmg
                GG = GGs[j % 2]
                wg1 = [ev_mmg]
                if j >= 2 and ev_glu[j - 2] is not None:
                    wg1.append(ev_glu[j - 2])
                ev_g1 = eop(a, lambda e, i=gpb, o=GG, bc=bias_col(L, "b1g", j), wj=wj:
                            e.activation(o[0:wj, :], i, AF.Gelu, bias=bc[0:wj, :]),
                            waits=wg1)
                ev_gelu[j] = ev_g1
                wg2 = [ev_mma]
                if j >= 1 and ev_glu[j - 1] is not None:
                    wg2.append(ev_glu[j - 1])
                ev_g2 = eop(v, lambda e, i=apb, bc=bias_col(L, "b1a", j), wj=wj:
                            e.tensor_scalar(AB[0:wj, :], i, bc[0:wj, :], None,
                                            ALU.add), waits=wg2)
                ev_act_pair[j] = ev_g2
                ev_glu[j] = eop(v, lambda e, o=g_tile(j), GG=GG, wj=wj:
                                e.tensor_mul(o[0:wj, :], AB[0:wj, :], GG[0:wj, :]),
                                waits=[ev_g1, ev_g2])
            ev_g_all = p.ev("sV")

            ev_prev = [None, None]
            lay2 = L * DI * D
            for n in range(NCH):
                base_slot = 0 if n % 2 == 0 else 3
                slabs = []
                for part in range(3):
                    nk = 8 if part < 2 else 5
                    row0 = part * 8 * 128
                    s_, ev_wx = load_slab(base_slot + part,
                                          slab_src(dW2, row0, n * 128, 128, nk,
                                                   lay_off=lay2),
                                          ncols=((nk + 1) * 128 if part == 2
                                                 else nk * 128),
                                          dma_cols=(nk * 128 if part == 2 else None))
                    slabs.append((s_, ev_wx))
                tail_src = bass.AP(dW2.tensor,
                                   dW2.offset + lay2 + 2688 * D + n * 128,
                                   [[D, 42], [1, 128]])
                s2_, _ = slabs[2]
                ev_wt2 = dma_w(s2_[0:42, 5 * 128:6 * 128], tail_src,
                               waits=[slabs[2][1]])
                ev_mm2 = None
                for q4 in range(2):
                    pg = q4 % 2
                    pb = PS[:, pg * 512:(pg + 1) * 512]
                    w = [ev_g_all, slabs[0][1], slabs[1][1], slabs[2][1], ev_wt2,
                         p.ev("sA")]
                    if ev_prev[pg] is not None:
                        w.append(ev_prev[pg])
                    for j in range(22):
                        wj = 128 if j < 21 else 42
                        part, kk = divmod(j, 8)
                        s_ = slabs[part][0]
                        ev_mm2 = mm(pb, s_[0:wj, kk * 128:(kk + 1) * 128],
                                    g_tile(j)[0:wj, q4 * 512:(q4 + 1) * 512],
                                    j == 0, j == 21,
                                    waits=w if j == 0 else (), inc=(j == 21))
                    ev_prev[pg] = eop(v, lambda e,
                                      o=xT[:, n * T + hsl0 + q4 * 512:
                                           n * T + hsl0 + (q4 + 1) * 512],
                                      i=pb, bc=bias_col(L, "b2", n):
                                      e.scalar_tensor_tensor(o, i, bc, o,
                                                             ALU.add, ALU.add),
                                      waits=[ev_mm2])
                for sidx in range(3):
                    wsl_used[base_slot + sidx] = ev_mm2
            ev_res_last = p.ev("sV")
        return p.ev("sV"), p.ev("sT")

    ev_v, ev_t = (ev_xin, p.ev("sT"))
    if stage >= 1:
        ev_v, ev_t = emit_mla()
    if stage >= 2:
        ev_save = []
        for c in range(NCH):
            ev_save.append(dma_s(dx1[:, c * T:(c + 1) * T],
                                 xT[:, c * T:(c + 1) * T], waits=[ev_v, ev_t]))
        ev_xs0 = emit_xs(shift=False, waits=[ev_v, ev_t])
        p.wait("scalar", ev_xs0)
        p.wait("tensor", ev_xs0)
        p.wait("vector", ev_xs0)
        ev_v = emit_ln(None, fp32_io=True, save_evs=ev_save)
        ev_t = p.ev("sT")
        for L in range(n_inner):
            ev_v, ev_t = emit_attn(L, ev_v, ev_t)
            if stage >= 3:
                ev_v, ev_t = emit_ffn(L, ev_v, ev_t)
            if stage < 4:
                break

    if TAP_SELECT is not None:
        region = {"XS": XS[:, :], "AO": AO[:, :], "AR": AR[:, 0:10112],
                  "TAB": TAB[:, :], "CH": CH[:, :], "xT": xT[:, :],
                  "BIAS": BIAS[:, :], "RN2": RN2[:, :],
                  "MISC": MISC[:, :], "ONES": ONES[:, :],
                  "WIL": WIL[:, :]}[TAP_SELECT]
        ncols = min(region.shape[-1], 16384)
        region = region[:, 0:ncols]
        yflat = bass.AP(dy.tensor, dy.offset, [[16384, 128], [1, ncols]])
        p.wait("gpsimd", ev_v)
        p.wait("gpsimd", ev_t)
        p.wait("gpsimd", p.ev("sA"))
        for e2 in p.evs_prefix("sXD"):
            p.wait("gpsimd", e2)
        dma_w(yflat, region)
        for e2 in p.evs_prefix("sWD"):
            p.wait("gpsimd", e2)
        with nc.Block() as block:
            p.replay(block)
        nc._stack.close()
        return nc

    if stage >= 2:
        ev_add_prev = [None, None]
        ev_ld_prev = [None, None]
        for ch in range(16):
            c, h = divmod(ch, 2)
            pg = ch % 2
            stg = CH[:, pg * 1024:(pg + 1) * 1024]
            reg = slice(c * T + h * 1024, c * T + (h + 1) * 1024)
            wl = []
            if ch < 2:
                wl = [ev_v, ev_t, p.ev("sA")]
            if ev_add_prev[pg] is not None:
                wl.append(ev_add_prev[pg])
            ev_ld = dma_s(stg, dx1[:, reg], waits=wl)
            ev_ld_prev[pg] = ev_ld
            wa = [ev_ld]
            if ch < 2:
                wa.append(ev_v)
            ev_a = eop(v, lambda e, o=xT[:, reg], s=stg:
                       e.tensor_add(o, o, s), waits=wa)
            ev_add_prev[pg] = ev_a
        ev_v = ev_add_prev[pg]
        ev_t = p.ev("sT")
        p.wait("vector", ev_v)

    ev_out_prev = [None, None]
    for tt in range(16):
        pg = tt % 2
        stg = CH[:, pg * 1024:(pg + 1) * 1024]
        pb = PS[:, 2048 + pg * 1024: 2048 + (pg + 1) * 1024]
        ev_mmo = None
        for c in range(NCH):
            w = []
            if tt == 0 and c == 0:
                w = [ev_v, ev_t]
            if c == 0 and ev_out_prev[pg] is not None:
                w.append(ev_out_prev[pg])
            ev_mmo = eop("tensor",
                         lambda e, c=c, pb=pb, tt=tt:
                         e.transpose(pb[:, c * 128:(c + 1) * 128],
                                     xT[:, c * T + tt * 128: c * T + (tt + 1) * 128],
                                     IDN[:, :]),
                         waits=w, inc=(c == NCH - 1))
        ev_ev = eop(v, lambda e, s=stg, pb=pb: e.tensor_copy(s, pb), waits=[ev_mmo])
        ev_out_prev[pg] = dma_s(dy[tt * 128:(tt + 1) * 128, :], stg, waits=[ev_ev])
    for e2 in p.evs_prefix("sXD"):
        p.wait("sync", e2)
    for e2 in p.evs_prefix("sWD"):
        p.wait("sync", e2)

    with nc.Block() as block:
        p.replay(block)
    nc._stack.close()
    import sys
    globals()["_LAST_P"] = p
    return nc


_NC_CACHE = {}


def _get_nc(stage=STAGE, n_inner=N_INNER):
    key = (stage, n_inner)
    if key not in _NC_CACHE:
        _NC_CACHE[key] = build_program(stage, n_inner)
    return _NC_CACHE[key]


def kernel(**inputs):
    x = np.ascontiguousarray(inputs["L0_attn_x"], dtype=np.float32)
    nc = _get_nc()
    wmap = {
        "Wc_down": np.ascontiguousarray(inputs["L0_attn_Wc_down"], dtype=np.float32),
        "Wq_up": np.ascontiguousarray(inputs["L0_attn_Wq_up"], dtype=np.float32),
        "Wkv_up": np.ascontiguousarray(inputs["L0_attn_Wkv_up"], dtype=np.float32),
        "Wc_proj": np.ascontiguousarray(inputs["L0_attn_Wc_proj"], dtype=np.float32),
        "Wqkv": np.ascontiguousarray(inputs["L0_ffn_Wqkv"], dtype=np.float32),
        "q_scale": np.ascontiguousarray(inputs["L0_ffn_q_scale"], dtype=np.float32),
        "k_scale": np.ascontiguousarray(inputs["L0_ffn_k_scale"], dtype=np.float32),
        "Wo": np.ascontiguousarray(inputs["L0_ffn_Wo"], dtype=np.float32),
        "ln_g": np.ascontiguousarray(inputs["L0_ffn_ln_g"], dtype=np.float32),
        "ln_b": np.ascontiguousarray(inputs["L0_ffn_ln_b"], dtype=np.float32),
        "W1": np.ascontiguousarray(inputs["L0_ffn_W1"], dtype=np.float32),
        "b1": np.ascontiguousarray(inputs["L0_ffn_b1"], dtype=np.float32),
        "W2": np.ascontiguousarray(inputs["L0_ffn_W2"], dtype=np.float32),
        "b2": np.ascontiguousarray(inputs["L0_ffn_b2"], dtype=np.float32),
    }
    in_maps = [dict(wmap, x=np.ascontiguousarray(x[b])) for b in range(8)]
    res = run_bass_kernel_spmd(nc, in_maps, core_ids=list(range(8)))
    y = np.stack([res.results[b]["y"] for b in range(8)], axis=0)
    return y
